# version 02
# baseline (speedup 1.0000x reference)
import contextlib

import numpy as np
import ml_dtypes

import concourse.bass as bass
import concourse.mybir as mybir
from concourse.bass_utils import run_bass_kernel_spmd

F32 = mybir.dt.float32
BF16 = mybir.dt.bfloat16
FP8 = mybir.dt.float8e4
AX = mybir.AxisListType.X
ALU = mybir.AluOpType
ACT = mybir.ActivationFunctionType

T = 2048
D = 1024
NT = T // 128
NC = D // 128
NPAIR = 8
MAGIC8 = 1572864.0
NEG = -1.0e30

_EOFF, _OOFF = {}, {}
_o = 0
for _l in range(1, 5):
    n = 2048 >> _l
    _EOFF[_l] = _o
    _o += n // 2
    _OOFF[_l] = _o
    _o += n // 2
_OOFF[5], _OOFF[6], _EOFF[5], _EOFF[6] = 1920, 1952, 1984, 2016

_RL = {1: 32, 2: 16, 3: 16, 4: 8, 5: 4, 6: 2}


def _mask_s(causal, mask_pad):
    m = np.full((128, 128), NEG, np.float32)
    for i in range(128):
        b = i // 16
        if mask_pad and b == 7:
            continue
        for j in range(16 * b, 16 * b + 16):
            if causal and j % 16 > i % 16:
                continue
            m[i, j] = 0.0
    return m


def _mask_t01(causal, mask_pad):
    m = np.zeros((128, 128), np.float32)
    for j in range(128):
        b = j // 16
        if mask_pad and b == 7:
            continue
        for i in range(16 * b, 16 * b + 16):
            if causal and j % 16 > i % 16:
                continue
            m[j, i] = 1.0
    return m.astype(ml_dtypes.bfloat16)


_NVAR = {1: 4, 2: 8, 3: 8, 4: 8}


def _perm_names():
    names = []
    for l in range(1, 5):
        for v in range(_NVAR[l]):
            names.append(f"P_L{l}_{v}")
    for v in range(12):
        names.append(f"P_C56_{v}")
    return names


def _host_consts():
    c = {}
    c["mask_s0"] = _mask_s(True, False)
    c["mask_sb"] = _mask_s(False, False)
    c["mask_sc"] = _mask_s(False, True)
    c["mask_t0"] = _mask_t01(True, False)
    c["mask_tb"] = _mask_t01(False, False)
    c["mask_tc"] = _mask_t01(False, True)
    c["ident"] = np.eye(128, dtype=np.float32).astype(ml_dtypes.bfloat16)
    for l in range(1, 5):
        r = _RL[l]
        for v in range(_NVAR[l]):
            m = np.zeros((128, 128), np.float32)
            for j in range(r):
                row = v * r + j
                w = row % 16
                m[row, ((17 * (1 << l) - 1) + (1 << l) * w
                        + (1 << (l + 5)) * (row // 16)) % 128] = 1.0
            c[f"P_L{l}_{v}"] = m.astype(ml_dtypes.bfloat16)
    for v in range(12):
        tt = v + 4
        m = np.zeros((128, 128), np.float32)
        if (tt // 4) % 2 == 1:
            pp, qq = tt // 8, tt % 4
            for j in range(4):
                j5 = 16 * pp + 4 * qq + j
                m[64 + j5, (543 + 32 * j5) % 128] = 1.0
        if tt >= 8:
            for j in range(2):
                j6 = 2 * (tt - 8) + j
                m[96 + j6, (1087 + 64 * j6) % 128] = 1.0
        c[f"P_C56_{v}"] = m.astype(ml_dtypes.bfloat16)
    c["ones2"] = np.ones((2, 128), np.float32).astype(ml_dtypes.bfloat16)
    c["zeros2"] = np.zeros((2, 128), np.float32).astype(ml_dtypes.bfloat16)
    c["onescol"] = np.ones((128, 1), np.float32).astype(ml_dtypes.bfloat16)
    sc = np.zeros((128, 1), np.float32)
    sc[0:64] = 4.0 ** -4
    sc[64:96] = 4.0 ** -5
    sc[96:128] = 4.0 ** -6
    c["scalecol"] = sc
    return c


class Ctr:
    def __init__(self, sem):
        self.sem = sem
        self.n = 0

    def inc(self, inst, k=1):
        inst.then_inc(self.sem, k)
        self.n += k
        return self.n


def build_nc(phase_limit=99):
    nc = bass.Bass("TRN2", target_bir_lowering=False)
    x_h = nc.dram_tensor("x", [T, D], F32, kind="ExternalInput").ap()
    wqkvT_h = nc.dram_tensor("wqkvT", [D, 3 * D], F32, kind="ExternalInput").ap()
    woutT_h = nc.dram_tensor("woutT", [D, D], F32, kind="ExternalInput").ap()
    wffn_h = nc.dram_tensor("wffn", [D, D], F32, kind="ExternalInput").ap()
    brow_h = nc.dram_tensor("brow", [2, D], F32, kind="ExternalInput").ap()
    hc = _host_consts()
    cst = {}
    for name, arr in hc.items():
        dt = F32 if arr.dtype == np.float32 else BF16
        cst[name] = nc.dram_tensor("c_" + name, list(arr.shape), dt,
                                   kind="ExternalInput").ap()
    import os as _os0
    _dbg = int(_os0.environ.get("KDBG", "0"))
    _kind = "ExternalOutput" if _dbg else "Internal"
    out_h = nc.dram_tensor("out", [T, D], F32, kind="ExternalOutput").ap()
    xd_sp = nc.dram_tensor("xd_spill", [T, D], BF16, kind=_kind).ap()
    wd_sp = nc.dram_tensor("wd_spill", [D, D], BF16, kind=_kind).ap()
    attn_sp = nc.dram_tensor("attn_spill", [T, D], BF16, kind=_kind).ap()
    xn_dbg = nc.dram_tensor("xn_dbg", [T, D], BF16, kind=_kind).ap() if _dbg else None
    qp_dbg = nc.dram_tensor("qp_dbg", [128, 2048], BF16, kind=_kind).ap() if _dbg else None
    kp_dbg = nc.dram_tensor("kp_dbg", [128, 2048], BF16, kind=_kind).ap() if _dbg else None
    vp_dbg = nc.dram_tensor("vp_dbg", [128, 2048], BF16, kind=_kind).ap() if _dbg else None
    stg_dbg = nc.dram_tensor("stg_dbg", [128, 2 * 8 * 65], BF16, kind=_kind).ap() if _dbg else None
    y0_dbg = nc.dram_tensor("y0_dbg", [128, 2 * 16 * 65], BF16, kind=_kind).ap() if _dbg else None
    em_dbg = nc.dram_tensor("em_dbg", [128, 48], F32, kind=_kind).ap() if _dbg else None
    em2_dbg = nc.dram_tensor("em2_dbg", [128, 160], F32, kind=_kind).ap() if _dbg else None

    ctx = contextlib.ExitStack()
    with ctx:
        E = ctx.enter_context

        brow_b = E(nc.sbuf_tensor("brow_b", [2, D], BF16))
        msk_s = E(nc.sbuf_tensor("msk_s", [128, 3, 128], F32))
        msk_t = E(nc.sbuf_tensor("msk_t", [128, 3, 128], BF16))
        ident = E(nc.sbuf_tensor("ident", [128, 128], BF16))
        nperm = len(_perm_names())
        perm = E(nc.sbuf_tensor("perm", [128, nperm, 128], BF16))
        ones2 = E(nc.sbuf_tensor("ones2", [2, 128], BF16))
        zeros2 = E(nc.sbuf_tensor("zeros2", [2, 128], BF16))
        onescol = E(nc.sbuf_tensor("onescol", [128, 1], BF16))
        scalecol = E(nc.sbuf_tensor("scalecol", [128, 1], F32))
        _ACT_CONSTS = [1e-5, -1.0, -0.5, 0.125, 0.25, 0.0625,
                       4.0 ** -3, 4.0 ** -4, 4.0 ** -5, 4.0 ** -6]
        actc = E(nc.sbuf_tensor("actc", [128, len(_ACT_CONSTS)], F32))
        for i, val in enumerate(_ACT_CONSTS):
            nc.const_aps.register(F32, val, actc[:, i:i + 1])

        ps_g = E(nc.psum_tensor("ps_g", [128, 2, 512], F32))
        ps_a = E(nc.psum_tensor("ps_a", [128, 2, 512], F32))
        ps_b = E(nc.psum_tensor("ps_b", [128, 2, 512], F32))
        ps_y = E(nc.psum_tensor("ps_y", [128, 2, 512], F32))

        ctx2 = contextlib.ExitStack()
        ctx.enter_context(ctx2)
        E2 = ctx2.enter_context
        xnT = E2(nc.sbuf_tensor("xnT", [128, NC, T], BF16))
        wqkvT = E2(nc.sbuf_tensor("wqkvT_s", [128, NC, 3 * D], BF16))
        v_sb = E2(nc.sbuf_tensor("v_sb", [128, NT, D], BF16))
        vT = E2(nc.sbuf_tensor("vT", [128, NPAIR, T], BF16))

        ctx1 = contextlib.ExitStack()
        ctx.enter_context(ctx1)
        E1 = ctx1.enter_context
        x_t = E1(nc.sbuf_tensor("x_t", [128, 2, D], F32))
        wffn_t = E1(nc.sbuf_tensor("wffn_t", [128, 1, D], F32))
        wd_t = E1(nc.sbuf_tensor("wd_t", [128, 2, D], BF16))
        xn_f = E1(nc.sbuf_tensor("xn_f", [128, 2, D], F32))
        xnb_t = E1(nc.sbuf_tensor("xnb_t", [128, 2, D], BF16))
        
        q_g = E1(nc.sbuf_tensor("q_g", [128, 32], F32))
        q_inv = E1(nc.sbuf_tensor("q_inv", [128, 32], F32))
        q_s = E1(nc.sbuf_tensor("q_s", [128, 32], F32))
        q_u = E1(nc.sbuf_tensor("q_u", [128, D], F32))
        q_mg = E1(nc.sbuf_tensor("q_mg", [128, D], F32))
        q_h8 = E1(nc.sbuf_tensor("q_h8", [128, D], FP8))
        q_l8 = E1(nc.sbuf_tensor("q_l8", [128, D], FP8))
        q_qt = E1(nc.sbuf_tensor("q_qt", [128, D], F32))
        xd_t = E1(nc.sbuf_tensor("xd_t", [128, 2, D], BF16))
        dump = q_u
        ln_nm = E1(nc.sbuf_tensor("ln_nm", [128, 2], F32))
        ln_s2 = E1(nc.sbuf_tensor("ln_s2", [128, 2], F32))
        ln_nv = E1(nc.sbuf_tensor("ln_nv", [128, 2], F32))
        ln_rs = E1(nc.sbuf_tensor("ln_rs", [128, 2], F32))
        ln_nmr = E1(nc.sbuf_tensor("ln_nmr", [128, 2], F32))

        sx = Ctr(E(nc.semaphore("sx")))
        sw = Ctr(E(nc.semaphore("sw")))
        st = Ctr(E(nc.semaphore("st")))
        sv = Ctr(E(nc.semaphore("sv")))
        ss = Ctr(E(nc.semaphore("ss")))
        sp = Ctr(E(nc.semaphore("sp")))
        so = Ctr(E(nc.semaphore("so")))
        sa = Ctr(E(nc.semaphore("sa")))
        blk = E(nc.Block())

        @blk.vector
        def _(v):
            for i, val in enumerate(_ACT_CONSTS):
                v.memset(actc[:, i:i + 1], val)

        @blk.sync
        def _(s):
            for t in range(2):
                sx.inc(s.dma_start(out=x_t[:, t, :],
                                   in_=x_h[128 * t:128 * (t + 1), :]), 16)

        sw_marks = {}

        @blk.gpsimd
        def _(g):
            for c_ in range(NC):
                for h_ in range(2):
                    sw.inc(g.dma_start(
                        out=wqkvT[:, c_, 1536 * h_:1536 * (h_ + 1)],
                        in_=wqkvT_h[128 * c_:128 * (c_ + 1),
                                    1536 * h_:1536 * (h_ + 1)]), 16)
            sw_marks["wqkvT"] = sw.n
            sw.inc(g.dma_start(out=brow_b[:], in_=brow_h[:]), 16)
            for i, nm in enumerate(["mask_s0", "mask_sb", "mask_sc"]):
                sw.inc(g.dma_start(out=msk_s[:, i, :], in_=cst[nm]), 16)
            for i, nm in enumerate(["mask_t0", "mask_tb", "mask_tc"]):
                sw.inc(g.dma_start(out=msk_t[:, i, :], in_=cst[nm]), 16)
            sw.inc(g.dma_start(out=ident[:], in_=cst["ident"]), 16)
            for pi, nm in enumerate(_perm_names()):
                sw.inc(g.dma_start(out=perm[:, pi, :], in_=cst[nm]), 16)
            sw.inc(g.dma_start(out=ones2[:], in_=cst["ones2"]), 16)
            sw.inc(g.dma_start(out=zeros2[:], in_=cst["zeros2"]), 16)
            sw.inc(g.dma_start(out=onescol[:], in_=cst["onescol"]), 16)
            sw.inc(g.dma_start(out=scalecol[:], in_=cst["scalecol"]), 16)
            sw_marks["all"] = sw.n

        def _quant(v, src, dst):
            v.tensor_reduce(out=q_g[:], in_=src.rearrange("p (b k) -> p b k", k=32),
                            axis=AX, op=ALU.max, apply_absolute_value=True)
            v.drain()
            v.reciprocal(out=q_inv[:], in_=q_g[:])
            v.drain()
            v.tensor_scalar(out=q_inv[:], in0=q_inv[:], scalar1=7.5, scalar2=7.5e30,
                            op0=ALU.mult, op1=ALU.min)
            v.tensor_scalar(out=q_s[:], in0=q_g[:], scalar1=1.0 / 7.5, scalar2=1e-30,
                            op0=ALU.mult, op1=ALU.max)
            v.tensor_tensor(out=q_u[:].rearrange("p (b k) -> p b k", k=32),
                            in0=src.rearrange("p (b k) -> p b k", k=32),
                            in1=q_inv[:].unsqueeze(2).broadcast_to((128, 32, 32)),
                            op=ALU.mult)
            v.tensor_scalar(out=q_mg[:], in0=q_u[:], scalar1=1.0, scalar2=-1.0,
                            op0=ALU.min, op1=ALU.max)
            v.tensor_scalar(out=q_mg[:], in0=q_mg[:], scalar1=MAGIC8, scalar2=MAGIC8,
                            op0=ALU.add, op1=ALU.subtract)
            v.tensor_scalar(out=q_h8[:], in0=q_u[:], scalar1=1.0, scalar2=None,
                            op0=ALU.max)
            v.tensor_scalar(out=q_l8[:], in0=q_u[:], scalar1=-1.0, scalar2=None,
                            op0=ALU.min)
            v.scalar_tensor_tensor(out=q_qt[:], in0=q_mg[:], scalar=0.0, in1=q_h8[:],
                                   op0=ALU.add, op1=ALU.add)
            v.scalar_tensor_tensor(out=q_qt[:], in0=q_qt[:], scalar=0.0, in1=q_l8[:],
                                   op0=ALU.add, op1=ALU.add)
            return v.tensor_tensor(out=dst.rearrange("p (b k) -> p b k", k=32),
                            in0=q_qt[:].rearrange("p (b k) -> p b k", k=32),
                            in1=q_s[:].unsqueeze(2).broadcast_to((128, 32, 32)),
                            op=ALU.mult)

        st_wq = []
        xn_done = []
        sv_xnb_t = []
        sv_xd_tl = []
        ss_xnf_t = []
        for t in range(NT):
            r2 = t % 2

            @blk.vector
            def _(v, t=t, r2=r2):
                v.wait_ge(sx.sem, 16 * (t + 1))
                if t >= 2:
                    v.wait_ge(ss.sem, ss_xnf_t[t - 2])
                xt = x_t[:, r2, :]
                v.tensor_scalar(out=dump[:], in0=xt, scalar1=-1.0 / D, scalar2=None,
                                op0=ALU.mult, op1=ALU.add,
                                accum_out=ln_nm[:, r2:r2 + 1])
                v.scalar_tensor_tensor(out=dump[:], in0=xt, scalar=1.0 / D, in1=xt,
                                       op0=ALU.mult, op1=ALU.mult,
                                       accum_out=ln_s2[:, r2:r2 + 1])
                v.drain()
                sv.inc(v.scalar_tensor_tensor(
                    out=ln_nv[:, r2:r2 + 1], in0=ln_nm[:, r2:r2 + 1],
                    scalar=ln_nm[:, r2:r2 + 1], in1=ln_s2[:, r2:r2 + 1],
                    op0=ALU.mult, op1=ALU.subtract))

            sv_stats = sv.n

            @blk.scalar
            def _(s, t=t, r2=r2, sv_stats=sv_stats):
                s.wait_ge(sv.sem, sv_stats)
                if t >= 2:
                    s.wait_ge(sv.sem, sv_xd_tl[t - 2])
                s.activation(out=ln_s2[:, r2:r2 + 1], in_=ln_nv[:, r2:r2 + 1],
                             func=ACT.Ln, bias=1e-5, scale=-1.0)
                s.activation(out=ln_rs[:, r2:r2 + 1], in_=ln_s2[:, r2:r2 + 1],
                             func=ACT.Exp, bias=0.0, scale=-0.5)
                s.mul(out=ln_nmr[:, r2:r2 + 1], in_=ln_rs[:, r2:r2 + 1],
                      mul=ln_nm[:, r2:r2 + 1])
                ss.inc(s.mul(out=ln_s2[:, r2:r2 + 1], in_=ln_rs[:, r2:r2 + 1],
                             mul=1.0))

            ss_xnf_t.append(ss.n)

            @blk.vector
            def _(v, t=t, r2=r2, mark=ss.n):
                v.wait_ge(ss.sem, mark)
                if t >= 2:
                    v.wait_ge(st.sem, xn_done[t - 2] + (32 if xn_dbg is not None else 16))
                sv.inc(v.tensor_scalar(out=xnb_t[:, r2, :], in0=x_t[:, r2, :],
                                scalar1=ln_rs[:, r2:r2 + 1],
                                scalar2=ln_nmr[:, r2:r2 + 1],
                                op0=ALU.mult, op1=ALU.add))
                v.tensor_scalar(out=xn_f[:, r2, :], in0=x_t[:, r2, :],
                                scalar1=ln_rs[:, r2:r2 + 1],
                                scalar2=ln_nmr[:, r2:r2 + 1],
                                op0=ALU.mult, op1=ALU.add)
                sv.inc(_quant(v, xn_f[:, r2, :], xd_t[:, r2, :]))

            sv_xnb_t.append(sv.n - 1)
            sv_xd_tl.append(sv.n)

            @blk.sync
            def _(s, t=t, r2=r2, a=sv.n - 1, b=sv.n, ssm=ss.n):
                s.wait_ge(sv.sem, a)
                st.inc(s.dma_start_transpose(out=xnT[:, :, 128 * t:128 * (t + 1)],
                                             in_=xnb_t[:, r2, :]), 16)
                s.wait_ge(sv.sem, b)
                st.inc(s.dma_start(out=xd_sp[128 * t:128 * (t + 1), :],
                                   in_=xd_t[:, r2, :]), 16)
                if xn_dbg is not None:
                    st.inc(s.dma_start(out=xn_dbg[128 * t:128 * (t + 1), :],
                                       in_=xnb_t[:, r2, :]), 16)
                if t + 2 < NT:
                    s.wait_ge(ss.sem, ssm)
                    sx.inc(s.dma_start(out=x_t[:, t % 2, :],
                                       in_=x_h[128 * (t + 2):128 * (t + 3), :]), 16)
            xn_done.append(st.n - (32 if xn_dbg is not None else 16))

        sv_wq = []
        for wt in range(NC):
            @blk.gpsimd
            def _(g, wt=wt):
                if wt >= 1:
                    g.wait_ge(sv.sem, sv_wq[wt - 1])
                sw.inc(g.dma_start(out=wffn_t[:, 0, :],
                                   in_=wffn_h[128 * wt:128 * (wt + 1), :]), 16)

            @blk.vector
            def _(v, wt=wt, mark=sw.n):
                v.wait_ge(sw.sem, mark)
                if wt >= 2:
                    v.wait_ge(st.sem, st_wq[wt - 2])
                sv.inc(_quant(v, wffn_t[:, 0, :], wd_t[:, wt % 2, :]))
            sv_wq.append(sv.n)

            @blk.sync
            def _(s, wt=wt, mark=sv.n):
                s.wait_ge(sv.sem, mark)
                st.inc(s.dma_start(out=wd_sp[128 * wt:128 * (wt + 1), :],
                                   in_=wd_t[:, wt % 2, :]), 16)
            st_wq.append(st.n)
        st_p1_done = st.n
        if phase_limit <= 1:
            @blk.gpsimd
            def _(g, a=st.n, b=sv.n, c=ss.n):
                g.wait_ge(st.sem, a)
                g.wait_ge(sv.sem, b)
                g.wait_ge(ss.sem, c)
                sa.inc(g.dma_start(out=out_h[0:128, :], in_=x_t[:, 0, :]), 16)
                g.wait_ge(sa.sem, 16)
            return nc

        import os as _os
        N2A = int(_os.environ.get("K2A", str(NT)))
        NO_VT = int(_os.environ.get("KNOVT", "0"))
        gbank_free = {0: 0, 1: 0}
        st_v_last = 0
        for t in range(N2A):
            for n2 in range(2):
                bank = (2 * t + n2) % 2

                @blk.tensor
                def _(p, t=t, n2=n2, bank=bank):
                    if t == 0 and n2 == 0:
                        p.wait_ge(sw.sem, sw_marks["wqkvT"])
                    p.wait_ge(st.sem, xn_done[t])
                    p.wait_ge(ss.sem, gbank_free[bank])
                    for c in range(NC):
                        mm = p.matmul(ps_g[:, bank, :], xnT[:, c, 128 * t:128 * (t + 1)],
                                 wqkvT[:, c, 2 * D + 512 * n2:2 * D + 512 * (n2 + 1)],
                                 start=(c == 0), stop=(c == NC - 1))
                    sp.inc(mm)

                @blk.scalar
                def _(s, t=t, n2=n2, bank=bank, mark=sp.n):
                    s.wait_ge(sp.sem, mark)
                    ss.inc(s.copy(out=v_sb[:, t, 512 * n2:512 * (n2 + 1)],
                                  in_=ps_g[:, bank, :]))
                gbank_free[bank] = ss.n

            if not NO_VT:
                @blk.sync
                def _(s, t=t, mark=ss.n):
                    s.wait_ge(ss.sem, mark)
                    st.inc(s.dma_start_transpose(out=vT[:, :, 128 * t:128 * (t + 1)],
                                                 in_=v_sb[:, t, :]), 16)
                st_v_last = st.n

        if not int(_os.environ.get("KNOCLOSE", "0")):
            ctx1.close()
        ctx3 = contextlib.ExitStack()
        ctx.enter_context(ctx3)
        E3 = ctx3.enter_context
        qT_p = E3(nc.sbuf_tensor("qT_p", [128, T], BF16))
        kT_p = E3(nc.sbuf_tensor("kT_p", [128, T], BF16))
        qP = E3(nc.sbuf_tensor("qP", [128, 2048], BF16))
        kP = E3(nc.sbuf_tensor("kP", [128, 2048], BF16))
        vP = E3(nc.sbuf_tensor("vP", [128, 2048], BF16))
        vPn = E3(nc.sbuf_tensor("vPn", [128, 16, 128], BF16))
        am = E3(nc.sbuf_tensor("am", [128, 2, 512], BF16))
        smS = E3(nc.sbuf_tensor("smS", [128, 512], F32))
        mbuf = E3(nc.sbuf_tensor("mbuf", [128, 2, 24], F32))
        embuf = E3(nc.sbuf_tensor("embuf", [128, 2, 24], F32))
        y0 = E3(nc.sbuf_tensor("y0", [128, 2, 16, 65], BF16))
        stg = E3(nc.sbuf_tensor("stg", [128, 2, 8, 65], BF16))
        avb = E3(nc.sbuf_tensor("avb", [128, 16], F32))
        rinv = E3(nc.sbuf_tensor("rinv", [128, 16], F32))
        attn_hd = E3(nc.sbuf_tensor("attn_hd", [128, 16, 64], BF16))
        em2 = E3(nc.sbuf_tensor("em2", [128, 160], F32)) if _dbg else None

        if phase_limit <= 2:
            @blk.gpsimd
            def _(g, a=st.n, b=sv.n, c=ss.n, d=sp.n):
                g.wait_ge(st.sem, a)
                g.wait_ge(sv.sem, b)
                g.wait_ge(ss.sem, c)
                g.wait_ge(sp.sem, d)
                sa.inc(g.dma_start(out=out_h[0:128, :], in_=x_t[:, 0, :]), 16)
                g.wait_ge(sa.sem, 16)
            return nc

        GRAM = (
            [("L0", 128 * t, 128 * t, 1.0) for t in range(16)]
            + [("L1", _OOFF[1] + 128 * a, _EOFF[1] + 128 * a, 0.25) for a in range(4)]
            + [("L2", _OOFF[2] + 128 * a, _EOFF[2] + 128 * a, 0.0625) for a in range(2)]
            + [("L3", _OOFF[3], _EOFF[3], 4.0 ** -3)]
            + [("CB", None, None, None)]
        )
        CB_SUBS = [(4, 64, 0), (5, 48, 64)]

        def _emit_grams(p, b, hh, orient):
            ps = ps_a if orient == "S" else ps_b
            bank = b % 2
            pr0 = 64 * hh
            first = [True]
            lastmm = [None]

            def mm(out, lt, rh, last):
                lastmm[0] = p.matmul(out, lt, rh, start=first[0], stop=last,
                                     skip_group_check=True)
                first[0] = False

            for bi in range(4):
                ai = 4 * b + bi
                kind, qo, ko, _sc = GRAM[ai]
                lastb = bi == 3
                if kind != "CB":
                    sq = qT_p if ai < 16 else qP
                    sk = kT_p if ai < 16 else kP
                    a_, b_ = (qo, ko) if orient == "S" else (ko, qo)
                    s1, s2 = (sq, sk) if orient == "S" else (sk, sq)
                    mm(ps[:, bank, 128 * bi:128 * (bi + 1)],
                       s1[pr0:pr0 + 64, a_:a_ + 128], s2[pr0:pr0 + 64, b_:b_ + 128],
                       lastb)
                else:
                    mm(ps[:, bank, 128 * bi:128 * (bi + 1)], zeros2[:], zeros2[:],
                       False)
                    for si, (l, w, ro) in enumerate(CB_SUBS):
                        qo2, ko2 = _OOFF[l], _EOFF[l]
                        a_, b_ = (qo2, ko2) if orient == "S" else (ko2, qo2)
                        s1, s2 = (qP, kP) if orient == "S" else (kP, qP)
                        lastmm[0] = p.matmul(
                            ps[ro:ro + w, bank, 128 * bi + ro:128 * bi + ro + w],
                            s1[pr0:pr0 + 64, a_:a_ + w],
                            s2[pr0:pr0 + 64, b_:b_ + w],
                            start=False, stop=(lastb and si == 1),
                            skip_group_check=True)
            return lastmm[0]

        abank_free = {0: 0, 1: 0}
        bbank_free = {0: 0, 1: 0}
        ybank_free = {0: 0, 1: 0}
        am_free = {0: 0, 1: 0}
        sv_attn_all = 0

        @blk.tensor
        def _(p):
            p.wait_ge(sw.sem, sw_marks["all"])
            for bk in range(2):
                p.matmul(ps_y[:, bk, 0:512], zeros2[:], brow_b[:, 0:512],
                         start=True, stop=True, skip_group_check=True)

        N2B = int(_os.environ.get("K2B", str(NPAIR)))
        STG2B = _os.environ.get("K2BS", "full")
        for pair in range(N2B):
            for feat0, dstbuf, qscale in [(128 * pair, qT_p, 0.125),
                                          (D + 128 * pair, kT_p, None)]:
                for n4 in range(4):
                    bank = n4 % 2

                    @blk.tensor
                    def _(p, pair=pair, feat0=feat0, n4=n4, bank=bank):
                        if pair == 0 and n4 == 0 and feat0 < D:
                            p.wait_ge(st.sem, st_p1_done)
                        p.wait_ge(ss.sem, gbank_free[bank])
                        for c in range(NC):
                            mm = p.matmul(ps_g[:, bank, :],
                                     wqkvT[:, c, feat0:feat0 + 128],
                                     xnT[:, c, 512 * n4:512 * (n4 + 1)],
                                     start=(c == 0), stop=(c == NC - 1))
                        sp.inc(mm)

                    @blk.scalar
                    def _(s, dstbuf=dstbuf, qscale=qscale, n4=n4, bank=bank,
                          mark=sp.n):
                        s.wait_ge(sp.sem, mark)
                        if qscale is None:
                            ss.inc(s.copy(out=dstbuf[:, 512 * n4:512 * (n4 + 1)],
                                   in_=ps_g[:, bank, :]))
                        else:
                            ss.inc(s.activation(out=dstbuf[:, 512 * n4:512 * (n4 + 1)],
                                         in_=ps_g[:, bank, :], func=ACT.Copy,
                                         bias=0.0, scale=qscale))
                    gbank_free[bank] = ss.n
            ss_qk = ss.n

            if STG2B == "qk":
                continue

            @blk.vector
            def _(v, pair=pair, ss_qk=ss_qk, stv=st_v_last):
                v.wait_ge(ss.sem, ss_qk)
                if pair == 0:
                    v.wait_ge(st.sem, stv)
                for srcbuf, dstbuf in [(qT_p, qP), (kT_p, kP), (None, vP)]:
                    src0 = vT[:, pair, :] if srcbuf is None else srcbuf[:]
                    rv = src0.rearrange("p (bp r) -> p bp r", r=64)
                    v.tensor_tensor(
                        out=dstbuf[:, _EOFF[1]:_EOFF[1] + 512].rearrange(
                            "p (a w) -> p a w", w=16),
                        in0=rv[:, :, 0:31:2], in1=rv[:, :, 1:32:2], op=ALU.add)
                    v.tensor_tensor(
                        out=dstbuf[:, _OOFF[1]:_OOFF[1] + 512].rearrange(
                            "p (a w) -> p a w", w=16),
                        in0=rv[:, :, 32:63:2], in1=rv[:, :, 33:64:2], op=ALU.add)
                    for l in range(2, 7):
                        P = (2048 >> l) // 32
                        for (dof, sof) in [(_EOFF[l], _EOFF[l - 1]),
                                           (_EOFF[l] + 8, _OOFF[l - 1]),
                                           (_OOFF[l], _EOFF[l - 1] + 16),
                                           (_OOFF[l] + 8, _OOFF[l - 1] + 16)]:
                            o = dstbuf[:, dof:dof + 16 * P].rearrange(
                                "p (g u) -> p g u", u=16)[:, :, 0:8]
                            i0 = dstbuf[:, sof:sof + 32 * P].rearrange(
                                "p (g u) -> p g u", u=32)[:, :, 0:15:2]
                            i1 = dstbuf[:, sof + 1:sof + 1 + 32 * P].rearrange(
                                "p (g u) -> p g u", u=32)[:, :, 0:15:2]
                            last = v.tensor_tensor(out=o, in0=i0, in1=i1, op=ALU.add)
                sv.inc(last)
            sv_pool = sv.n

            @blk.sync
            def _(s, mark=sv.n):
                s.wait_ge(sv.sem, mark)
                st.inc(s.dma_start_transpose(out=vPn[:], in_=vP[:]), 16)
            st_vpn = st.n
            if STG2B == "pool":
                continue

            for hh in range(2):
                for b in range(6):
                    bank = b % 2

                    @blk.tensor
                    def _(p, b=b, hh=hh, bank=bank, sv_pool=sv_pool, ss_qk=ss_qk):
                        if b == 0 and hh == 0:
                            p.wait_ge(ss.sem, ss_qk)
                        if b == 4 and hh == 0:
                            p.wait_ge(sv.sem, sv_pool)
                        p.wait_ge(sv.sem, abank_free[bank])
                        sp.inc(_emit_grams(p, b, hh, "S"))

                    @blk.vector
                    def _(v, b=b, hh=hh, bank=bank, mark=sp.n):
                        v.wait_ge(sp.sem, mark)
                        pv = ps_a[:, bank, :]
                        if b < 4:
                            v.scalar_tensor_tensor(
                                out=smS[:].rearrange("p (n f) -> p n f", f=128),
                                in0=pv.rearrange("p (n f) -> p n f", f=128),
                                scalar=1.0, op0=ALU.mult, op1=ALU.add,
                                in1=msk_s[:, 0:1, :].broadcast_to((128, 4, 128)))
                        elif b == 4:
                            v.scalar_tensor_tensor(
                                out=smS[:].rearrange("p (n f) -> p n f", f=128),
                                in0=pv.rearrange("p (n f) -> p n f", f=128),
                                scalar=0.25, op0=ALU.mult, op1=ALU.add,
                                in1=msk_s[:, 1:2, :].broadcast_to((128, 4, 128)))
                        else:
                            v.scalar_tensor_tensor(
                                out=smS[:, 0:256].rearrange("p (n f) -> p n f", f=128),
                                in0=pv[:, 0:256].rearrange("p (n f) -> p n f", f=128),
                                scalar=0.0625, op0=ALU.mult, op1=ALU.add,
                                in1=msk_s[:, 1:2, :].broadcast_to((128, 2, 128)))
                            v.scalar_tensor_tensor(
                                out=smS[:, 256:384], in0=pv[:, 256:384],
                                scalar=4.0 ** -3, op0=ALU.mult, op1=ALU.add,
                                in1=msk_s[:, 1, :])
                            v.scalar_tensor_tensor(
                                out=smS[:, 384:512], in0=pv[:, 384:512],
                                scalar=scalecol[:], op0=ALU.mult, op1=ALU.add,
                                in1=msk_s[:, 1, :])
                        sv.inc(v.tensor_reduce(
                            out=mbuf[:, hh, 4 * b:4 * (b + 1)],
                            in_=smS[:].rearrange("p (n f) -> p n f", f=128),
                            axis=AX, op=ALU.max))
                    abank_free[bank] = sv.n
                sv_max = sv.n
                if STG2B == "ph1":
                    continue

                @blk.scalar
                def _(s, hh=hh, sv_max=sv_max):
                    s.wait_ge(sv.sem, sv_max)
                    ss.inc(s.activation(out=embuf[:, hh, :], in_=mbuf[:, hh, :],
                                        func=ACT.Exp, bias=0.0, scale=-1.0))
                ss_em = ss.n
                if STG2B == "em":
                    continue

                SUB3 = _os.environ.get("K3S", "full")
                for b in range(6):
                    bank = b % 2

                    @blk.tensor
                    def _(p, b=b, hh=hh, bank=bank):
                        p.wait_ge(ss.sem, bbank_free[bank])
                        sp.inc(_emit_grams(p, b, hh, "T"))

                    if SUB3 == "st":
                        bbank_free[bank] = 0
                        continue

                    @blk.scalar
                    def _(s, b=b, bank=bank, mark=sp.n, amf=am_free[bank]):
                        s.wait_ge(sp.sem, mark)
                        s.wait_ge(sp.sem, amf)
                        pb = ps_b[:, bank, :]
                        ab = am[:, bank, :]
                        if b < 4:
                            ei = s.activation(out=ab, in_=pb, func=ACT.Exp, bias=0.0,
                                              scale=1.0)
                        elif b == 4:
                            ei = s.activation(out=ab, in_=pb, func=ACT.Exp, bias=0.0,
                                              scale=0.25)
                        else:
                            s.activation(out=ab[:, 0:256], in_=pb[:, 0:256],
                                         func=ACT.Exp, bias=0.0, scale=0.0625)
                            s.activation(out=ab[:, 256:384], in_=pb[:, 256:384],
                                         func=ACT.Exp, bias=0.0, scale=4.0 ** -3)
                            s.activation(out=ab[:, 384:448], in_=pb[:, 384:448],
                                         func=ACT.Exp, bias=0.0, scale=4.0 ** -4)
                            s.activation(out=ab[:, 448:480], in_=pb[:, 448:480],
                                         func=ACT.Exp, bias=0.0, scale=4.0 ** -5)
                            ei = s.activation(out=ab[:, 480:512], in_=pb[:, 480:512],
                                              func=ACT.Exp, bias=0.0, scale=4.0 ** -6)
                        ss.inc(ei)
                    bbank_free[bank] = ss.n

                    if SUB3 == "exp":
                        continue

                    @blk.vector
                    def _(v, b=b, bank=bank, mark=ss.n):
                        v.wait_ge(ss.sem, mark)
                        ab3 = am[:, bank, :].rearrange("p (n f) -> p n f", f=128)
                        if b < 4:
                            mi = v.tensor_tensor(out=ab3, in0=ab3,
                                            in1=msk_t[:, 0:1, :].broadcast_to(
                                                (128, 4, 128)), op=ALU.mult)
                        elif b == 4:
                            mi = v.tensor_tensor(out=ab3, in0=ab3,
                                            in1=msk_t[:, 1:2, :].broadcast_to(
                                                (128, 4, 128)), op=ALU.mult)
                        else:
                            v.tensor_tensor(out=ab3[:, 0:3, :], in0=ab3[:, 0:3, :],
                                            in1=msk_t[:, 1:2, :].broadcast_to(
                                                (128, 3, 128)), op=ALU.mult)
                            mi = v.tensor_tensor(out=ab3[:, 3, :], in0=ab3[:, 3, :],
                                                 in1=msk_t[:, 2, :], op=ALU.mult)
                        sv.inc(mi)
                    sv_am = sv.n

                    if SUB3 == "msk":
                        continue

                    @blk.tensor
                    def _(p, pair=pair, hh=hh, b=b, bank=bank, sv_am=sv_am,
                          st_vpn=st_vpn):
                        p.wait_ge(sv.sem, sv_am)
                        p.wait_ge(sv.sem, ybank_free[bank])
                        if b == 4 and hh == 0:
                            p.wait_ge(st.sem, st_vpn)
                        h = 2 * pair + hh
                        K3PV = _os.environ.get("K3PV", "full")
                        first = True
                        lastpv = None
                        for bi in range(4):
                            ai = 4 * b + bi
                            kind = GRAM[ai][0]
                            amat = am[:, bank, 128 * bi:128 * (bi + 1)]
                            yc = 66 * bi
                            if kind == "L0":
                                rhs = v_sb[:, ai, 64 * h:64 * h + 64]
                                lastpv = p.matmul(ps_y[:, bank, yc:yc + 64], amat, rhs,
                                         start=True, stop=True,
                                         skip_group_check=True)
                                if K3PV != "noones":
                                    lastpv = p.matmul(ps_y[:, bank, yc + 64:yc + 65], amat,
                                             onescol[:], start=True,
                                             stop=True, skip_group_check=True)
                            elif kind != "CB":
                                mt = {16: 0, 17: 1, 18: 2, 19: 3,
                                      20: 8, 21: 9, 22: 12}[ai]
                                rhs = vPn[:, mt, 64 * hh:64 * hh + 64]
                                if K3PV == "nolvl":
                                    continue
                                lastpv = p.matmul(ps_y[:, bank, yc:yc + 64], amat, rhs,
                                         start=True, stop=True,
                                         skip_group_check=True)
                                if K3PV != "noones":
                                    lastpv = p.matmul(ps_y[:, bank, yc + 64:yc + 65], amat,
                                             onescol[:], start=True,
                                             stop=True, skip_group_check=True)
                            else:
                                pieces = [(0, 64, 14), (64, 48, 15)]
                                for (r0, w, mt) in pieces:
                                    asl = am[r0:r0 + w, bank,
                                             128 * bi + r0:128 * bi + r0 + w]
                                    p.matmul(ps_y[r0:r0 + w, bank, yc:yc + 64],
                                             asl,
                                             vPn[r0:r0 + w, mt,
                                                 64 * hh:64 * hh + 64],
                                             start=True, stop=True,
                                             skip_group_check=True)
                                    lastpv = p.matmul(
                                        ps_y[r0:r0 + w, bank, yc + 64:yc + 65],
                                        asl, onescol[r0:r0 + w, :], start=True,
                                        stop=True, skip_group_check=True)
                            first = False
                        sp.inc(lastpv)
                    am_free[bank] = sp.n

                    if SUB3 == "pv":
                        ybank_free[bank] = 0
                        continue

                    @blk.vector
                    def _(v, pair=pair, hh=hh, b=b, bank=bank, mark=sp.n,
                          ss_em=ss_em):
                        v.wait_ge(sp.sem, mark)
                        v.wait_ge(ss.sem, ss_em)
                        src = ps_y[:, bank, 0:264].rearrange("p (n f) -> p n f", f=66)[:, :, 0:65]
                        if b < 4:
                            dst = y0[:, hh, 4 * b:4 * (b + 1), :]
                        elif b == 4:
                            dst = stg[:, hh, 0:4, :]
                        else:
                            dst = stg[:, hh, 4:8, :]
                        snap = _dbg and pair == 7 and hh == 0 and b >= 4
                        if snap:
                            v.tensor_copy(out=em2[:, (b - 4) * 4:(b - 4) * 4 + 4],
                                          in_=embuf[:, hh, 4 * b:4 * (b + 1)])
                        if snap and b == 4:
                            bca = embuf[:, hh, 4 * b:4 * (b + 1)].unsqueeze(2) \
                                .broadcast_to((128, 4, 65))
                            v.tensor_copy(
                                out=em2[:, 8:12].rearrange("p (a c) -> p a c", c=1),
                                in_=bca[:, :, 0:1])
                            v.scalar_tensor_tensor(
                                out=em2[:, 16:81].rearrange("p (a c) -> p a c", a=1),
                                in0=src[:, 0:1, :], scalar=1.0, op0=ALU.mult,
                                op1=ALU.mult,
                                in1=embuf[:, hh, 16:17].unsqueeze(2)
                                    .broadcast_to((128, 1, 65)))
                        sv.inc(v.scalar_tensor_tensor(
                            out=dst, in0=src, scalar=1.0, op0=ALU.mult,
                            op1=ALU.mult,
                            in1=embuf[:, hh, 4 * b:4 * (b + 1)].unsqueeze(2)
                                .broadcast_to((128, 4, 65))))
                        if snap and b == 4:
                            v.tensor_copy(out=em2[:, 81:146],
                                          in_=stg[:, 0, 0, :])
                    ybank_free[bank] = sv.n
                sv_stage = sv.n
                if STG2B == "ph3":
                    continue

                PIDX = {1: 0, 2: 4, 3: 12, 4: 20, "C56": 28}
                for wi, wave in enumerate([list(range(0, 7)), list(range(7, 14)),
                                           [14, 15]]):
                    bank = wi % 2

                    @blk.tensor
                    def _(p, hh=hh, wave=wave, bank=bank, sv_stage=sv_stage):
                        p.wait_ge(sv.sem, sv_stage)
                        p.wait_ge(sv.sem, ybank_free[bank])
                        pd = ps_y[:, bank, 0:462].rearrange("p (n f) -> p n f", f=66)
                        K3G = _os.environ.get("K3G", "")
                        mms = []
                        for ti, tt in enumerate(wave):
                            mms.append((pd[:, ti, 0:65], ident[:], y0[:, hh, tt, :]))
                        if K3G == "ionly":
                            for mi, (o_, lt, rh) in enumerate(mms):
                                gi = p.matmul(o_, lt, rh, start=(mi == 0),
                                              stop=(mi == len(mms) - 1),
                                              skip_group_check=True)
                            sp.inc(gi)
                            return
                        for l in range(1, 5):
                            for ti, tt in enumerate(wave):
                                if l == 1:
                                    var, acol = tt % 4, tt // 4
                                elif l == 2:
                                    var, acol = tt % 8, 4 + tt // 8
                                elif l == 3:
                                    if tt % 2 == 0:
                                        continue
                                    var, acol = (tt - 1) // 2, 6
                                else:
                                    if tt % 4 < 2:
                                        continue
                                    var = 2 * (tt // 4) + (tt % 4 - 2)
                                    acol = 7
                                mms.append((pd[:, ti, 0:65],
                                            perm[:, PIDX[l] + var, :],
                                            stg[:, hh, acol, :]))
                        for ti, tt in enumerate(wave):
                            if tt < 4:
                                continue
                            mms.append((pd[:, ti, 0:65],
                                        perm[:, PIDX["C56"] + (tt - 4), :],
                                        stg[:, hh, 7, :]))
                        if K3G == "tilemajor":
                            bytile = {}
                            for (o_, lt, rh) in mms:
                                bytile.setdefault(id(o_.tensor) if False else str(o_.offset), []).append((o_, lt, rh))
                            per = [[] for _ in wave]
                            k = len(wave)
                            per_idx = {}
                            mms2 = []
                            for ti in range(len(wave)):
                                mms2.append(mms[ti])
                            rest = mms[len(wave):]
                            for m in rest:
                                pass
                            groups = {}
                            order = []
                            for m in mms:
                                key = str(m[0].offset)
                                if key not in groups:
                                    groups[key] = []
                                    order.append(key)
                                groups[key].append(m)
                            for key in order:
                                g_ = groups[key]
                                for mi, (o_, lt, rh) in enumerate(g_):
                                    gi = p.matmul(o_, lt, rh, start=(mi == 0),
                                                  stop=(mi == len(g_) - 1),
                                                  skip_group_check=True)
                        else:
                            for mi, (o_, lt, rh) in enumerate(mms):
                                gi = p.matmul(o_, lt, rh, start=(mi == 0),
                                              stop=(mi == len(mms) - 1),
                                              skip_group_check=True)
                        sp.inc(gi)

                    if _os.environ.get("K3G", "") == "mmonly":
                        ybank_free[bank] = 0
                        continue

                    @blk.vector
                    def _(v, pair=pair, hh=hh, wave=wave, bank=bank, mark=sp.n):
                        v.wait_ge(sp.sem, mark)
                        nw = len(wave)
                        t0 = wave[0]
                        pd = ps_y[:, bank, 0:462].rearrange("p (n f) -> p n f", f=66)
                        v.tensor_scalar(out=avb[:, t0:t0 + nw],
                                        in0=pd[:, 0:nw, 64], scalar1=1e-8,
                                        scalar2=None, op0=ALU.add)
                        v.drain()
                        v.reciprocal(out=rinv[:, t0:t0 + nw], in_=avb[:, t0:t0 + nw])
                        v.drain()
                        if wave[0] == 0 and (pair + hh) > 0:
                            v.wait_ge(sa.sem, 16 * (2 * pair + hh))
                        sv.inc(v.scalar_tensor_tensor(
                            out=attn_hd[:, t0:t0 + nw, :],
                            in0=pd[:, 0:nw, 0:64], scalar=1.0,
                            op0=ALU.mult, op1=ALU.mult,
                            in1=rinv[:, t0:t0 + nw].unsqueeze(2)
                                .broadcast_to((128, nw, 64))))
                    ybank_free[bank] = sv.n

                @blk.gpsimd
                def _(g, pair=pair, hh=hh, mark=sv.n):
                    g.wait_ge(sv.sem, mark)
                    h = 2 * pair + hh
                    sa.inc(g.dma_start(
                        out=attn_sp.rearrange("(t p) d -> p t d", p=128)[:, :, 64 * h:64 * h + 64],
                        in_=attn_hd[:]), 16)
        if xn_dbg is not None:
            @blk.gpsimd
            def _(g, mark=sv.n):
                g.wait_ge(sv.sem, mark)
                sa.inc(g.dma_start(out=qp_dbg, in_=qP[:]), 16)
                sa.inc(g.dma_start(out=kp_dbg, in_=kP[:]), 16)
                sa.inc(g.dma_start(out=vp_dbg, in_=vP[:]), 16)
                sa.inc(g.dma_start(out=stg_dbg, in_=stg[:].rearrange("p a b c -> p (a b c)")), 16)
                sa.inc(g.dma_start(out=y0_dbg, in_=y0[:].rearrange("p a b c -> p (a b c)")), 16)
                sa.inc(g.dma_start(out=em_dbg, in_=embuf[:].rearrange("p a b -> p (a b)")), 16)
                sa.inc(g.dma_start(out=em2_dbg, in_=em2[:]), 16)

        sv_attn_all = sv.n
        if phase_limit <= 3:
            @blk.gpsimd
            def _(g, a=st.n, b=sv.n, c=ss.n, d=sp.n, e=sa.n):
                g.wait_ge(st.sem, a)
                g.wait_ge(sv.sem, b)
                g.wait_ge(ss.sem, c)
                g.wait_ge(sp.sem, d)
                g.wait_ge(sa.sem, e)
                sa.inc(g.dma_start(out=out_h[0:128, 0:24], in_=mbuf[:].rearrange("p a b -> p (a b)")[:, 0:24]), 16)
                g.wait_ge(sa.sem, e + 16)
            return nc

        ctx3.close()
        ctx2.close()
        ctx4 = contextlib.ExitStack()
        ctx.enter_context(ctx4)
        E4 = ctx4.enter_context
        xdT = E4(nc.sbuf_tensor("xdT", [128, NC, T], BF16))
        attnT = E4(nc.sbuf_tensor("attnT", [128, NC, T], BF16))
        wdT = E4(nc.sbuf_tensor("wdT", [128, NC, D], BF16))
        woutT = E4(nc.sbuf_tensor("woutT_s", [128, NC, D], BF16))
        x_r = E4(nc.sbuf_tensor("x_r", [128, 2, D], F32))
        out_sb = E4(nc.sbuf_tensor("out_sb", [128, 2, D], F32))

        obank_free = {0: 0, 1: 0}
        xr_free = {0: 0, 1: 0}
        ob_free = {0: 0, 1: 0}
        @blk.gpsimd
        def _(g, mark=sv_attn_all):
            g.wait_ge(sv.sem, mark)
            for c_ in range(NC):
                sw.inc(g.dma_start(out=woutT[:, c_, :],
                                   in_=woutT_h[128 * c_:128 * (c_ + 1), :]), 16)
        sw_marks["woutT"] = sw.n

        @blk.sync
        def _(s, san=sa.n):
            s.wait_ge(sa.sem, san)
            for wt in range(NC):
                st.inc(s.dma_start_transpose(out=wdT[:, :, 128 * wt:128 * (wt + 1)],
                                             in_=wd_sp[128 * wt:128 * (wt + 1), :]), 16)
        st_wd_done = st.n

        at_mark = []
        for t in range(NT):
            @blk.sync
            def _(s, t=t):
                st.inc(s.dma_start_transpose(out=attnT[:, :, 128 * t:128 * (t + 1)],
                                             in_=attn_sp[128 * t:128 * (t + 1), :]), 16)
                st.inc(s.dma_start_transpose(out=xdT[:, :, 128 * t:128 * (t + 1)],
                                             in_=xd_sp[128 * t:128 * (t + 1), :]), 16)
                if t >= 2:
                    s.wait_ge(sv.sem, xr_free[t % 2])
                sx.inc(s.dma_start(out=x_r[:, t % 2, :],
                                   in_=x_h[128 * t:128 * (t + 1), :]), 16)
            at_mark.append((st.n, sx.n))

            for n2 in range(2):
                bank = (2 * t + n2) % 2

                @blk.tensor
                def _(p, t=t, n2=n2, bank=bank):
                    p.wait_ge(st.sem, at_mark[t][0])
                    if t == 0 and n2 == 0:
                        p.wait_ge(st.sem, st_wd_done)
                        p.wait_ge(sw.sem, sw_marks["woutT"])
                        p.wait_ge(ss.sem, gbank_free[0])
                        p.wait_ge(ss.sem, gbank_free[1])
                    p.wait_ge(sv.sem, obank_free[bank])
                    p.matmul(ps_g[:, bank, :], ones2[:],
                             brow_b[:, 512 * n2:512 * (n2 + 1)],
                             start=True, stop=False)
                    for c in range(NC):
                        p.matmul(ps_g[:, bank, :], attnT[:, c, 128 * t:128 * (t + 1)],
                                 woutT[:, c, 512 * n2:512 * (n2 + 1)],
                                 start=False, stop=False)
                    for c in range(NC):
                        mm = p.matmul(ps_g[:, bank, :], xdT[:, c, 128 * t:128 * (t + 1)],
                                 wdT[:, c, 512 * n2:512 * (n2 + 1)],
                                 start=False, stop=(c == NC - 1))
                    sp.inc(mm)

                @blk.vector
                def _(v, t=t, n2=n2, bank=bank, mark=sp.n):
                    v.wait_ge(sp.sem, mark)
                    v.wait_ge(sx.sem, at_mark[t][1])
                    if t >= 2 and n2 == 0:
                        v.wait_ge(so.sem, ob_free[t % 2])
                    sv.inc(v.tensor_tensor(
                        out=out_sb[:, t % 2, 512 * n2:512 * (n2 + 1)],
                        in0=x_r[:, t % 2, 512 * n2:512 * (n2 + 1)],
                        in1=ps_g[:, bank, :], op=ALU.add))
                obank_free[bank] = sv.n
            xr_free[t % 2] = sv.n

            @blk.sync
            def _(s, t=t, mark=sv.n):
                s.wait_ge(sv.sem, mark)
                so.inc(s.dma_start(out=out_h[128 * t:128 * (t + 1), :],
                                   in_=out_sb[:, t % 2, :]), 16)
            ob_free[t % 2] = so.n

        @blk.gpsimd
        def _(g):
            g.wait_ge(so.sem, so.n)

    return nc


_NC_CACHE = {}


def _get_nc():
    import os
    pl = int(os.environ.get("KPHASE", "99"))
    key = ("nc", pl)
    if key not in _NC_CACHE:
        _NC_CACHE[key] = build_nc(pl)
    return _NC_CACHE[key]


def kernel(L0_attn_x, L0_attn_w_qkv, L0_attn_w_out, L0_attn_b_out,
           L0_ffn_weight, L0_ffn_bias):
    x = np.ascontiguousarray(np.asarray(L0_attn_x, dtype=np.float32))
    wqkvT = np.ascontiguousarray(np.asarray(L0_attn_w_qkv, dtype=np.float32).T)
    woutT = np.ascontiguousarray(np.asarray(L0_attn_w_out, dtype=np.float32).T)
    wffn = np.ascontiguousarray(np.asarray(L0_ffn_weight, dtype=np.float32))
    brow = np.ascontiguousarray(
        np.stack([np.asarray(L0_attn_b_out, dtype=np.float32),
                  np.asarray(L0_ffn_bias, dtype=np.float32)], axis=0))
    hc = _host_consts()
    base = {"wqkvT": wqkvT, "woutT": woutT, "wffn": wffn, "brow": brow}
    for k, v in hc.items():
        base["c_" + k] = np.ascontiguousarray(v)
    in_maps = []
    for c in range(8):
        m = dict(base)
        m["x"] = np.ascontiguousarray(x[c])
        in_maps.append(m)
    nc = _get_nc()
    res = run_bass_kernel_spmd(nc, in_maps, core_ids=list(range(8)))
    out = np.stack([np.asarray(r["out"]) for r in res.results], axis=0)
    return out.astype(np.float32)


# version 03
# speedup vs baseline: 1.0921x; 1.0921x over previous
import contextlib

import numpy as np
import ml_dtypes

import concourse.bass as bass
import concourse.mybir as mybir
from concourse.bass_utils import run_bass_kernel_spmd

F32 = mybir.dt.float32
BF16 = mybir.dt.bfloat16
FP8 = mybir.dt.float8e4
AX = mybir.AxisListType.X
ALU = mybir.AluOpType
ACT = mybir.ActivationFunctionType

T = 2048
D = 1024
NT = T // 128
NC = D // 128
NPAIR = 8
MAGIC8 = 1572864.0
NEG = -1.0e30

_EOFF, _OOFF = {}, {}
_o = 0
for _l in range(1, 5):
    n = 2048 >> _l
    _EOFF[_l] = _o
    _o += n // 2
    _OOFF[_l] = _o
    _o += n // 2
_OOFF[5], _OOFF[6], _EOFF[5], _EOFF[6] = 1920, 1952, 1984, 2016

_RL = {1: 32, 2: 16, 3: 16, 4: 8, 5: 4, 6: 2}


def _mask_s(causal, mask_pad):
    m = np.full((128, 128), NEG, np.float32)
    for i in range(128):
        b = i // 16
        if mask_pad and b == 7:
            continue
        for j in range(16 * b, 16 * b + 16):
            if causal and j % 16 > i % 16:
                continue
            m[i, j] = 0.0
    return m


def _mask_t01(causal, mask_pad):
    m = np.zeros((128, 128), np.float32)
    for j in range(128):
        b = j // 16
        if mask_pad and b == 7:
            continue
        for i in range(16 * b, 16 * b + 16):
            if causal and j % 16 > i % 16:
                continue
            m[j, i] = 1.0
    return m.astype(ml_dtypes.bfloat16)


_NVAR = {1: 4, 2: 8, 3: 8, 4: 8}


def _perm_names():
    names = []
    for l in range(1, 5):
        for v in range(_NVAR[l]):
            names.append(f"P_L{l}_{v}")
    for v in range(12):
        names.append(f"P_C56_{v}")
    return names


def _host_consts():
    c = {}
    c["mask_s0"] = _mask_s(True, False)
    c["mask_sb"] = _mask_s(False, False)
    c["mask_sc"] = _mask_s(False, True)
    c["mask_t0"] = _mask_t01(True, False)
    c["mask_tb"] = _mask_t01(False, False)
    c["mask_tc"] = _mask_t01(False, True)
    c["ident"] = np.eye(128, dtype=np.float32).astype(ml_dtypes.bfloat16)
    for l in range(1, 5):
        r = _RL[l]
        for v in range(_NVAR[l]):
            m = np.zeros((128, 128), np.float32)
            for j in range(r):
                row = v * r + j
                w = row % 16
                m[row, ((17 * (1 << l) - 1) + (1 << l) * w
                        + (1 << (l + 5)) * (row // 16)) % 128] = 1.0
            c[f"P_L{l}_{v}"] = m.astype(ml_dtypes.bfloat16)
    for v in range(12):
        tt = v + 4
        m = np.zeros((128, 128), np.float32)
        if (tt // 4) % 2 == 1:
            pp, qq = tt // 8, tt % 4
            for j in range(4):
                j5 = 16 * pp + 4 * qq + j
                m[64 + j5, (543 + 32 * j5) % 128] = 1.0
        if tt >= 8:
            for j in range(2):
                j6 = 2 * (tt - 8) + j
                m[96 + j6, (1087 + 64 * j6) % 128] = 1.0
        c[f"P_C56_{v}"] = m.astype(ml_dtypes.bfloat16)
    c["ones2"] = np.ones((2, 128), np.float32).astype(ml_dtypes.bfloat16)
    c["zeros2"] = np.zeros((2, 128), np.float32).astype(ml_dtypes.bfloat16)
    c["onescol"] = np.ones((128, 1), np.float32).astype(ml_dtypes.bfloat16)
    sc = np.zeros((128, 1), np.float32)
    sc[0:64] = 4.0 ** -4
    sc[64:96] = 4.0 ** -5
    sc[96:128] = 4.0 ** -6
    c["scalecol"] = sc
    return c


class Ctr:
    def __init__(self, sem):
        self.sem = sem
        self.n = 0

    def inc(self, inst, k=1):
        inst.then_inc(self.sem, k)
        self.n += k
        return self.n


def build_nc(phase_limit=99):
    nc = bass.Bass("TRN2", target_bir_lowering=False)
    x_h = nc.dram_tensor("x", [T, D], F32, kind="ExternalInput").ap()
    wqkvT_h = nc.dram_tensor("wqkvT", [D, 3 * D], F32, kind="ExternalInput").ap()
    woutT_h = nc.dram_tensor("woutT", [D, D], F32, kind="ExternalInput").ap()
    wffn_h = nc.dram_tensor("wffn", [D, D], F32, kind="ExternalInput").ap()
    brow_h = nc.dram_tensor("brow", [2, D], F32, kind="ExternalInput").ap()
    hc = _host_consts()
    cst = {}
    for name, arr in hc.items():
        dt = F32 if arr.dtype == np.float32 else BF16
        cst[name] = nc.dram_tensor("c_" + name, list(arr.shape), dt,
                                   kind="ExternalInput").ap()
    import os as _os0
    _dbg = int(_os0.environ.get("KDBG", "0"))
    _kind = "ExternalOutput" if _dbg else "Internal"
    out_h = nc.dram_tensor("out", [T, D], F32, kind="ExternalOutput").ap()
    xd_sp = nc.dram_tensor("xd_spill", [T, D], BF16, kind=_kind).ap()
    wd_sp = nc.dram_tensor("wd_spill", [D, D], BF16, kind=_kind).ap()
    attn_sp = nc.dram_tensor("attn_spill", [T, D], BF16, kind=_kind).ap()
    xn_dbg = nc.dram_tensor("xn_dbg", [T, D], BF16, kind=_kind).ap() if _dbg else None
    qp_dbg = nc.dram_tensor("qp_dbg", [128, 2048], BF16, kind=_kind).ap() if _dbg else None
    kp_dbg = nc.dram_tensor("kp_dbg", [128, 2048], BF16, kind=_kind).ap() if _dbg else None
    vp_dbg = nc.dram_tensor("vp_dbg", [128, 2048], BF16, kind=_kind).ap() if _dbg else None
    stg_dbg = nc.dram_tensor("stg_dbg", [128, 2 * 8 * 65], BF16, kind=_kind).ap() if _dbg else None
    y0_dbg = nc.dram_tensor("y0_dbg", [128, 2 * 16 * 65], BF16, kind=_kind).ap() if _dbg else None
    em_dbg = nc.dram_tensor("em_dbg", [128, 48], F32, kind=_kind).ap() if _dbg else None
    em2_dbg = nc.dram_tensor("em2_dbg", [128, 160], F32, kind=_kind).ap() if _dbg else None

    ctx = contextlib.ExitStack()
    with ctx:
        E = ctx.enter_context

        brow_b = E(nc.sbuf_tensor("brow_b", [2, D], BF16))
        msk_s = E(nc.sbuf_tensor("msk_s", [128, 3, 128], F32))
        msk_t = E(nc.sbuf_tensor("msk_t", [128, 3, 128], BF16))
        ident = E(nc.sbuf_tensor("ident", [128, 128], BF16))
        nperm = len(_perm_names())
        perm = E(nc.sbuf_tensor("perm", [128, nperm, 128], BF16))
        ones2 = E(nc.sbuf_tensor("ones2", [2, 128], BF16))
        zeros2 = E(nc.sbuf_tensor("zeros2", [2, 128], BF16))
        onescol = E(nc.sbuf_tensor("onescol", [128, 1], BF16))
        scalecol = E(nc.sbuf_tensor("scalecol", [128, 1], F32))
        _ACT_CONSTS = [1e-5, -1.0, -0.5, 0.125, 0.25, 0.0625,
                       4.0 ** -3, 4.0 ** -4, 4.0 ** -5, 4.0 ** -6]
        actc = E(nc.sbuf_tensor("actc", [128, len(_ACT_CONSTS)], F32))
        for i, val in enumerate(_ACT_CONSTS):
            nc.const_aps.register(F32, val, actc[:, i:i + 1])

        ps_g = E(nc.psum_tensor("ps_g", [128, 2, 512], F32))
        ps_a = E(nc.psum_tensor("ps_a", [128, 2, 512], F32))
        ps_b = E(nc.psum_tensor("ps_b", [128, 2, 512], F32))
        ps_y = E(nc.psum_tensor("ps_y", [128, 2, 512], F32))

        ctx2 = contextlib.ExitStack()
        ctx.enter_context(ctx2)
        E2 = ctx2.enter_context
        xnT = E2(nc.sbuf_tensor("xnT", [128, NC, T], BF16))
        wqkvT = E2(nc.sbuf_tensor("wqkvT_s", [128, NC, 3 * D], BF16))
        v_sb = E2(nc.sbuf_tensor("v_sb", [128, NT, D], BF16))
        vT = E2(nc.sbuf_tensor("vT", [128, NPAIR, T], BF16))

        ctx1 = contextlib.ExitStack()
        ctx.enter_context(ctx1)
        E1 = ctx1.enter_context
        x_t = E1(nc.sbuf_tensor("x_t", [128, 2, D], F32))
        wffn_t = E1(nc.sbuf_tensor("wffn_t", [128, 1, D], F32))
        wd_t = E1(nc.sbuf_tensor("wd_t", [128, 2, D], BF16))
        xn_f = E1(nc.sbuf_tensor("xn_f", [128, 2, D], F32))
        xnb_t = E1(nc.sbuf_tensor("xnb_t", [128, 2, D], BF16))
        
        q_g = E1(nc.sbuf_tensor("q_g", [128, 32], F32))
        q_inv = E1(nc.sbuf_tensor("q_inv", [128, 32], F32))
        q_s = E1(nc.sbuf_tensor("q_s", [128, 32], F32))
        q_u = E1(nc.sbuf_tensor("q_u", [128, D], F32))
        q_mg = E1(nc.sbuf_tensor("q_mg", [128, D], F32))
        q_h8 = E1(nc.sbuf_tensor("q_h8", [128, D], FP8))
        q_l8 = E1(nc.sbuf_tensor("q_l8", [128, D], FP8))
        q_qt = E1(nc.sbuf_tensor("q_qt", [128, D], F32))
        xd_t = E1(nc.sbuf_tensor("xd_t", [128, 2, D], BF16))
        dump = q_u
        ln_nm = E1(nc.sbuf_tensor("ln_nm", [128, 2], F32))
        ln_sx = E1(nc.sbuf_tensor("ln_sx", [128, 2], F32))
        ln_sq = E1(nc.sbuf_tensor("ln_sq", [128, 2], F32))
        ln_s2 = E1(nc.sbuf_tensor("ln_s2", [128, 2], F32))
        ln_nv = E1(nc.sbuf_tensor("ln_nv", [128, 2], F32))
        ln_rs = E1(nc.sbuf_tensor("ln_rs", [128, 2], F32))
        ln_nmr = E1(nc.sbuf_tensor("ln_nmr", [128, 2], F32))

        sx = Ctr(E(nc.semaphore("sx")))
        sw = Ctr(E(nc.semaphore("sw")))
        st = Ctr(E(nc.semaphore("st")))
        sv = Ctr(E(nc.semaphore("sv")))
        ss = Ctr(E(nc.semaphore("ss")))
        sp = Ctr(E(nc.semaphore("sp")))
        so = Ctr(E(nc.semaphore("so")))
        sa = Ctr(E(nc.semaphore("sa")))
        blk = E(nc.Block())

        @blk.vector
        def _(v):
            for i, val in enumerate(_ACT_CONSTS):
                v.memset(actc[:, i:i + 1], val)

        @blk.sync
        def _(s):
            for t in range(2):
                sx.inc(s.dma_start(out=x_t[:, t, :],
                                   in_=x_h[128 * t:128 * (t + 1), :]), 16)

        sw_marks = {}

        @blk.gpsimd
        def _(g):
            for c_ in range(NC):
                for h_ in range(2):
                    sw.inc(g.dma_start(
                        out=wqkvT[:, c_, 1536 * h_:1536 * (h_ + 1)],
                        in_=wqkvT_h[128 * c_:128 * (c_ + 1),
                                    1536 * h_:1536 * (h_ + 1)]), 16)
            sw_marks["wqkvT"] = sw.n
            sw.inc(g.dma_start(out=brow_b[:], in_=brow_h[:]), 16)
            for i, nm in enumerate(["mask_s0", "mask_sb", "mask_sc"]):
                sw.inc(g.dma_start(out=msk_s[:, i, :], in_=cst[nm]), 16)
            for i, nm in enumerate(["mask_t0", "mask_tb", "mask_tc"]):
                sw.inc(g.dma_start(out=msk_t[:, i, :], in_=cst[nm]), 16)
            sw.inc(g.dma_start(out=ident[:], in_=cst["ident"]), 16)
            for pi, nm in enumerate(_perm_names()):
                sw.inc(g.dma_start(out=perm[:, pi, :], in_=cst[nm]), 16)
            sw.inc(g.dma_start(out=ones2[:], in_=cst["ones2"]), 16)
            sw.inc(g.dma_start(out=zeros2[:], in_=cst["zeros2"]), 16)
            sw.inc(g.dma_start(out=onescol[:], in_=cst["onescol"]), 16)
            sw.inc(g.dma_start(out=scalecol[:], in_=cst["scalecol"]), 16)
            sw_marks["all"] = sw.n

        def _quant(v, src, dst):
            v.tensor_reduce(out=q_g[:], in_=src.rearrange("p (b k) -> p b k", k=32),
                            axis=AX, op=ALU.max, apply_absolute_value=True)
            v.drain()
            v.reciprocal(out=q_inv[:], in_=q_g[:])
            v.drain()
            v.tensor_scalar(out=q_inv[:], in0=q_inv[:], scalar1=7.5, scalar2=7.5e30,
                            op0=ALU.mult, op1=ALU.min)
            v.tensor_scalar(out=q_s[:], in0=q_g[:], scalar1=1.0 / 7.5, scalar2=1e-30,
                            op0=ALU.mult, op1=ALU.max)
            v.tensor_tensor(out=q_u[:].rearrange("p (b k) -> p b k", k=32),
                            in0=src.rearrange("p (b k) -> p b k", k=32),
                            in1=q_inv[:].unsqueeze(2).broadcast_to((128, 32, 32)),
                            op=ALU.mult)
            v.tensor_scalar(out=q_mg[:], in0=q_u[:], scalar1=1.0, scalar2=-1.0,
                            op0=ALU.min, op1=ALU.max)
            v.tensor_scalar(out=q_mg[:], in0=q_mg[:], scalar1=MAGIC8, scalar2=MAGIC8,
                            op0=ALU.add, op1=ALU.subtract)
            v.tensor_scalar(out=q_h8[:], in0=q_u[:], scalar1=1.0, scalar2=None,
                            op0=ALU.max)
            v.tensor_scalar(out=q_l8[:], in0=q_u[:], scalar1=-1.0, scalar2=None,
                            op0=ALU.min)
            v.scalar_tensor_tensor(out=q_qt[:], in0=q_mg[:], scalar=0.0, in1=q_h8[:],
                                   op0=ALU.add, op1=ALU.add)
            v.scalar_tensor_tensor(out=q_qt[:], in0=q_qt[:], scalar=0.0, in1=q_l8[:],
                                   op0=ALU.add, op1=ALU.add)
            return v.tensor_tensor(out=dst.rearrange("p (b k) -> p b k", k=32),
                            in0=q_qt[:].rearrange("p (b k) -> p b k", k=32),
                            in1=q_s[:].unsqueeze(2).broadcast_to((128, 32, 32)),
                            op=ALU.mult)

        st_wq = []
        xn_done = []
        sv_xnb_t = []
        sv_xd_tl = []
        ss_xnf_t = []
        for t in range(NT):
            r2 = t % 2

            @blk.vector
            def _(v, t=t, r2=r2):
                v.wait_ge(sx.sem, 16 * (t + 1))
                if t >= 2:
                    v.wait_ge(ss.sem, ss_xnf_t[t - 2])
                xt = x_t[:, r2, :]
                v.tensor_reduce(out=ln_sx[:, r2:r2 + 1], in_=xt, axis=AX, op=ALU.add)
                v.tensor_tensor(out=dump[:], in0=xt, in1=xt, op=ALU.mult)
                v.tensor_reduce(out=ln_sq[:, r2:r2 + 1], in_=dump[:], axis=AX,
                                op=ALU.add)
                v.tensor_scalar(out=ln_nm[:, r2:r2 + 1], in0=ln_sx[:, r2:r2 + 1],
                                scalar1=-1.0 / D, scalar2=None, op0=ALU.mult)
                v.tensor_scalar(out=ln_s2[:, r2:r2 + 1], in0=ln_sq[:, r2:r2 + 1],
                                scalar1=1.0 / D, scalar2=None, op0=ALU.mult)
                sv.inc(v.scalar_tensor_tensor(
                    out=ln_nv[:, r2:r2 + 1], in0=ln_nm[:, r2:r2 + 1],
                    scalar=ln_nm[:, r2:r2 + 1], in1=ln_s2[:, r2:r2 + 1],
                    op0=ALU.mult, op1=ALU.subtract))

            sv_stats = sv.n

            @blk.scalar
            def _(s, t=t, r2=r2, sv_stats=sv_stats):
                s.wait_ge(sv.sem, sv_stats)
                if t >= 2:
                    s.wait_ge(sv.sem, sv_xd_tl[t - 2])
                s.activation(out=ln_s2[:, r2:r2 + 1], in_=ln_nv[:, r2:r2 + 1],
                             func=ACT.Ln, bias=1e-5, scale=-1.0)
                s.activation(out=ln_rs[:, r2:r2 + 1], in_=ln_s2[:, r2:r2 + 1],
                             func=ACT.Exp, bias=0.0, scale=-0.5)
                s.mul(out=ln_nmr[:, r2:r2 + 1], in_=ln_rs[:, r2:r2 + 1],
                      mul=ln_nm[:, r2:r2 + 1])
                ss.inc(s.mul(out=ln_s2[:, r2:r2 + 1], in_=ln_rs[:, r2:r2 + 1],
                             mul=1.0))

            ss_xnf_t.append(ss.n)

            @blk.vector
            def _(v, t=t, r2=r2, mark=ss.n):
                v.wait_ge(ss.sem, mark)
                if t >= 2:
                    v.wait_ge(st.sem, xn_done[t - 2] + (32 if xn_dbg is not None else 16))
                sv.inc(v.tensor_scalar(out=xnb_t[:, r2, :], in0=x_t[:, r2, :],
                                scalar1=ln_rs[:, r2:r2 + 1],
                                scalar2=ln_nmr[:, r2:r2 + 1],
                                op0=ALU.mult, op1=ALU.add))
                v.tensor_scalar(out=xn_f[:, r2, :], in0=x_t[:, r2, :],
                                scalar1=ln_rs[:, r2:r2 + 1],
                                scalar2=ln_nmr[:, r2:r2 + 1],
                                op0=ALU.mult, op1=ALU.add)
                sv.inc(_quant(v, xn_f[:, r2, :], xd_t[:, r2, :]))

            sv_xnb_t.append(sv.n - 1)
            sv_xd_tl.append(sv.n)

            @blk.sync
            def _(s, t=t, r2=r2, a=sv.n - 1, b=sv.n, ssm=ss.n):
                s.wait_ge(sv.sem, a)
                st.inc(s.dma_start_transpose(out=xnT[:, :, 128 * t:128 * (t + 1)],
                                             in_=xnb_t[:, r2, :]), 16)
                s.wait_ge(sv.sem, b)
                st.inc(s.dma_start(out=xd_sp[128 * t:128 * (t + 1), :],
                                   in_=xd_t[:, r2, :]), 16)
                if xn_dbg is not None:
                    st.inc(s.dma_start(out=xn_dbg[128 * t:128 * (t + 1), :],
                                       in_=xnb_t[:, r2, :]), 16)
                if t + 2 < NT:
                    s.wait_ge(ss.sem, ssm)
                    sx.inc(s.dma_start(out=x_t[:, t % 2, :],
                                       in_=x_h[128 * (t + 2):128 * (t + 3), :]), 16)
            xn_done.append(st.n - (32 if xn_dbg is not None else 16))

        sv_wq = []
        for wt in range(NC):
            @blk.gpsimd
            def _(g, wt=wt):
                if wt >= 1:
                    g.wait_ge(sv.sem, sv_wq[wt - 1])
                sw.inc(g.dma_start(out=wffn_t[:, 0, :],
                                   in_=wffn_h[128 * wt:128 * (wt + 1), :]), 16)

            @blk.vector
            def _(v, wt=wt, mark=sw.n):
                v.wait_ge(sw.sem, mark)
                if wt >= 2:
                    v.wait_ge(st.sem, st_wq[wt - 2])
                sv.inc(_quant(v, wffn_t[:, 0, :], wd_t[:, wt % 2, :]))
            sv_wq.append(sv.n)

            @blk.sync
            def _(s, wt=wt, mark=sv.n):
                s.wait_ge(sv.sem, mark)
                st.inc(s.dma_start(out=wd_sp[128 * wt:128 * (wt + 1), :],
                                   in_=wd_t[:, wt % 2, :]), 16)
            st_wq.append(st.n)
        st_p1_done = st.n
        if phase_limit <= 1:
            @blk.gpsimd
            def _(g, a=st.n, b=sv.n, c=ss.n):
                g.wait_ge(st.sem, a)
                g.wait_ge(sv.sem, b)
                g.wait_ge(ss.sem, c)
                sa.inc(g.dma_start(out=out_h[0:128, :], in_=x_t[:, 0, :]), 16)
                g.wait_ge(sa.sem, 16)
            return nc

        import os as _os
        N2A = int(_os.environ.get("K2A", str(NT)))
        NO_VT = int(_os.environ.get("KNOVT", "0"))
        gbank_free = {0: 0, 1: 0}
        st_v_last = 0
        for t in range(N2A):
            for n2 in range(2):
                bank = (2 * t + n2) % 2

                @blk.tensor
                def _(p, t=t, n2=n2, bank=bank):
                    if t == 0 and n2 == 0:
                        p.wait_ge(sw.sem, sw_marks["wqkvT"])
                    p.wait_ge(st.sem, xn_done[t])
                    p.wait_ge(ss.sem, gbank_free[bank])
                    for c in range(NC):
                        mm = p.matmul(ps_g[:, bank, :], xnT[:, c, 128 * t:128 * (t + 1)],
                                 wqkvT[:, c, 2 * D + 512 * n2:2 * D + 512 * (n2 + 1)],
                                 start=(c == 0), stop=(c == NC - 1))
                    sp.inc(mm)

                @blk.scalar
                def _(s, t=t, n2=n2, bank=bank, mark=sp.n):
                    s.wait_ge(sp.sem, mark)
                    ss.inc(s.copy(out=v_sb[:, t, 512 * n2:512 * (n2 + 1)],
                                  in_=ps_g[:, bank, :]))
                gbank_free[bank] = ss.n

            if not NO_VT:
                @blk.sync
                def _(s, t=t, mark=ss.n):
                    s.wait_ge(ss.sem, mark)
                    st.inc(s.dma_start_transpose(out=vT[:, :, 128 * t:128 * (t + 1)],
                                                 in_=v_sb[:, t, :]), 16)
                st_v_last = st.n

        if not int(_os.environ.get("KNOCLOSE", "0")):
            ctx1.close()
        ctx3 = contextlib.ExitStack()
        ctx.enter_context(ctx3)
        E3 = ctx3.enter_context
        qT_p = E3(nc.sbuf_tensor("qT_p", [128, T], BF16))
        kT_p = E3(nc.sbuf_tensor("kT_p", [128, T], BF16))
        qP = E3(nc.sbuf_tensor("qP", [128, 2048], BF16))
        kP = E3(nc.sbuf_tensor("kP", [128, 2048], BF16))
        vP = E3(nc.sbuf_tensor("vP", [128, 2048], BF16))
        vPn = E3(nc.sbuf_tensor("vPn", [128, 16, 128], BF16))
        am = E3(nc.sbuf_tensor("am", [128, 2, 512], BF16))
        smS = E3(nc.sbuf_tensor("smS", [128, 512], F32))
        mbuf = E3(nc.sbuf_tensor("mbuf", [128, 2, 24], F32))
        embuf = E3(nc.sbuf_tensor("embuf", [128, 2, 24], F32))
        y0 = E3(nc.sbuf_tensor("y0", [128, 2, 16, 65], BF16))
        stg = E3(nc.sbuf_tensor("stg", [128, 2, 8, 65], BF16))
        avb = E3(nc.sbuf_tensor("avb", [128, 16], F32))
        rinv = E3(nc.sbuf_tensor("rinv", [128, 16], F32))
        attn_hd = E3(nc.sbuf_tensor("attn_hd", [128, 16, 64], BF16))
        em2 = E3(nc.sbuf_tensor("em2", [128, 160], F32)) if _dbg else None

        if phase_limit <= 2:
            @blk.gpsimd
            def _(g, a=st.n, b=sv.n, c=ss.n, d=sp.n):
                g.wait_ge(st.sem, a)
                g.wait_ge(sv.sem, b)
                g.wait_ge(ss.sem, c)
                g.wait_ge(sp.sem, d)
                sa.inc(g.dma_start(out=out_h[0:128, :], in_=x_t[:, 0, :]), 16)
                g.wait_ge(sa.sem, 16)
            return nc

        GRAM = (
            [("L0", 128 * t, 128 * t, 1.0) for t in range(16)]
            + [("L1", _OOFF[1] + 128 * a, _EOFF[1] + 128 * a, 0.25) for a in range(4)]
            + [("L2", _OOFF[2] + 128 * a, _EOFF[2] + 128 * a, 0.0625) for a in range(2)]
            + [("L3", _OOFF[3], _EOFF[3], 4.0 ** -3)]
            + [("CB", None, None, None)]
        )
        CB_SUBS = [(4, 64, 0), (5, 48, 64)]

        def _emit_grams(p, b, hh, orient):
            ps = ps_a if orient == "S" else ps_b
            bank = b % 2
            pr0 = 64 * hh
            first = [True]
            lastmm = [None]

            def mm(out, lt, rh, last):
                lastmm[0] = p.matmul(out, lt, rh, start=first[0], stop=last,
                                     skip_group_check=True)
                first[0] = False

            for bi in range(4):
                ai = 4 * b + bi
                kind, qo, ko, _sc = GRAM[ai]
                lastb = bi == 3
                if kind != "CB":
                    sq = qT_p if ai < 16 else qP
                    sk = kT_p if ai < 16 else kP
                    a_, b_ = (qo, ko) if orient == "S" else (ko, qo)
                    s1, s2 = (sq, sk) if orient == "S" else (sk, sq)
                    mm(ps[:, bank, 128 * bi:128 * (bi + 1)],
                       s1[pr0:pr0 + 64, a_:a_ + 128], s2[pr0:pr0 + 64, b_:b_ + 128],
                       lastb)
                else:
                    mm(ps[:, bank, 128 * bi:128 * (bi + 1)], zeros2[:], zeros2[:],
                       False)
                    for si, (l, w, ro) in enumerate(CB_SUBS):
                        qo2, ko2 = _OOFF[l], _EOFF[l]
                        a_, b_ = (qo2, ko2) if orient == "S" else (ko2, qo2)
                        s1, s2 = (qP, kP) if orient == "S" else (kP, qP)
                        lastmm[0] = p.matmul(
                            ps[ro:ro + w, bank, 128 * bi + ro:128 * bi + ro + w],
                            s1[pr0:pr0 + 64, a_:a_ + w],
                            s2[pr0:pr0 + 64, b_:b_ + w],
                            start=False, stop=(lastb and si == 1),
                            skip_group_check=True)
            return lastmm[0]

        abank_free = {0: 0, 1: 0}
        bbank_free = {0: 0, 1: 0}
        ybank_free = {0: 0, 1: 0}
        am_free = {0: 0, 1: 0}
        sv_attn_all = 0

        @blk.tensor
        def _(p):
            p.wait_ge(sw.sem, sw_marks["all"])
            for bk in range(2):
                p.matmul(ps_y[:, bk, 0:512], zeros2[:], brow_b[:, 0:512],
                         start=True, stop=True, skip_group_check=True)

        N2B = int(_os.environ.get("K2B", str(NPAIR)))
        STG2B = _os.environ.get("K2BS", "full")
        for pair in range(N2B):
            for feat0, dstbuf, qscale in [(128 * pair, qT_p, 0.125),
                                          (D + 128 * pair, kT_p, None)]:
                for n4 in range(4):
                    bank = n4 % 2

                    @blk.tensor
                    def _(p, pair=pair, feat0=feat0, n4=n4, bank=bank):
                        if pair == 0 and n4 == 0 and feat0 < D:
                            p.wait_ge(st.sem, st_p1_done)
                        p.wait_ge(ss.sem, gbank_free[bank])
                        for c in range(NC):
                            mm = p.matmul(ps_g[:, bank, :],
                                     wqkvT[:, c, feat0:feat0 + 128],
                                     xnT[:, c, 512 * n4:512 * (n4 + 1)],
                                     start=(c == 0), stop=(c == NC - 1))
                        sp.inc(mm)

                    @blk.scalar
                    def _(s, dstbuf=dstbuf, qscale=qscale, n4=n4, bank=bank,
                          mark=sp.n):
                        s.wait_ge(sp.sem, mark)
                        if qscale is None:
                            ss.inc(s.copy(out=dstbuf[:, 512 * n4:512 * (n4 + 1)],
                                   in_=ps_g[:, bank, :]))
                        else:
                            ss.inc(s.activation(out=dstbuf[:, 512 * n4:512 * (n4 + 1)],
                                         in_=ps_g[:, bank, :], func=ACT.Copy,
                                         bias=0.0, scale=qscale))
                    gbank_free[bank] = ss.n
            ss_qk = ss.n

            if STG2B == "qk":
                continue

            @blk.vector
            def _(v, pair=pair, ss_qk=ss_qk, stv=st_v_last):
                v.wait_ge(ss.sem, ss_qk)
                if pair == 0:
                    v.wait_ge(st.sem, stv)
                for srcbuf, dstbuf in [(qT_p, qP), (kT_p, kP), (None, vP)]:
                    src0 = vT[:, pair, :] if srcbuf is None else srcbuf[:]
                    rv = src0.rearrange("p (bp r) -> p bp r", r=64)
                    v.tensor_tensor(
                        out=dstbuf[:, _EOFF[1]:_EOFF[1] + 512].rearrange(
                            "p (a w) -> p a w", w=16),
                        in0=rv[:, :, 0:31:2], in1=rv[:, :, 1:32:2], op=ALU.add)
                    v.tensor_tensor(
                        out=dstbuf[:, _OOFF[1]:_OOFF[1] + 512].rearrange(
                            "p (a w) -> p a w", w=16),
                        in0=rv[:, :, 32:63:2], in1=rv[:, :, 33:64:2], op=ALU.add)
                    for l in range(2, 7):
                        P = (2048 >> l) // 32
                        for (dof, sof) in [(_EOFF[l], _EOFF[l - 1]),
                                           (_EOFF[l] + 8, _OOFF[l - 1]),
                                           (_OOFF[l], _EOFF[l - 1] + 16),
                                           (_OOFF[l] + 8, _OOFF[l - 1] + 16)]:
                            o = dstbuf[:, dof:dof + 16 * P].rearrange(
                                "p (g u) -> p g u", u=16)[:, :, 0:8]
                            i0 = dstbuf[:, sof:sof + 32 * P].rearrange(
                                "p (g u) -> p g u", u=32)[:, :, 0:15:2]
                            i1 = dstbuf[:, sof + 1:sof + 1 + 32 * P].rearrange(
                                "p (g u) -> p g u", u=32)[:, :, 0:15:2]
                            last = v.tensor_tensor(out=o, in0=i0, in1=i1, op=ALU.add)
                sv.inc(last)
            sv_pool = sv.n

            @blk.sync
            def _(s, mark=sv.n):
                s.wait_ge(sv.sem, mark)
                st.inc(s.dma_start_transpose(out=vPn[:], in_=vP[:]), 16)
            st_vpn = st.n
            if STG2B == "pool":
                continue

            for hh in range(2):
                for b in range(6):
                    bank = b % 2

                    @blk.tensor
                    def _(p, b=b, hh=hh, bank=bank, sv_pool=sv_pool, ss_qk=ss_qk):
                        if b == 0 and hh == 0:
                            p.wait_ge(ss.sem, ss_qk)
                        if b == 4 and hh == 0:
                            p.wait_ge(sv.sem, sv_pool)
                        p.wait_ge(sv.sem, abank_free[bank])
                        sp.inc(_emit_grams(p, b, hh, "S"))

                    @blk.vector
                    def _(v, b=b, hh=hh, bank=bank, mark=sp.n):
                        v.wait_ge(sp.sem, mark)
                        pv = ps_a[:, bank, :]
                        if b < 4:
                            v.scalar_tensor_tensor(
                                out=smS[:].rearrange("p (n f) -> p n f", f=128),
                                in0=pv.rearrange("p (n f) -> p n f", f=128),
                                scalar=1.0, op0=ALU.mult, op1=ALU.add,
                                in1=msk_s[:, 0:1, :].broadcast_to((128, 4, 128)))
                        elif b == 4:
                            v.scalar_tensor_tensor(
                                out=smS[:].rearrange("p (n f) -> p n f", f=128),
                                in0=pv.rearrange("p (n f) -> p n f", f=128),
                                scalar=0.25, op0=ALU.mult, op1=ALU.add,
                                in1=msk_s[:, 1:2, :].broadcast_to((128, 4, 128)))
                        else:
                            v.scalar_tensor_tensor(
                                out=smS[:, 0:256].rearrange("p (n f) -> p n f", f=128),
                                in0=pv[:, 0:256].rearrange("p (n f) -> p n f", f=128),
                                scalar=0.0625, op0=ALU.mult, op1=ALU.add,
                                in1=msk_s[:, 1:2, :].broadcast_to((128, 2, 128)))
                            v.scalar_tensor_tensor(
                                out=smS[:, 256:384], in0=pv[:, 256:384],
                                scalar=4.0 ** -3, op0=ALU.mult, op1=ALU.add,
                                in1=msk_s[:, 1, :])
                            v.scalar_tensor_tensor(
                                out=smS[:, 384:512], in0=pv[:, 384:512],
                                scalar=scalecol[:], op0=ALU.mult, op1=ALU.add,
                                in1=msk_s[:, 1, :])
                        sv.inc(v.tensor_reduce(
                            out=mbuf[:, hh, 4 * b:4 * (b + 1)],
                            in_=smS[:].rearrange("p (n f) -> p n f", f=128),
                            axis=AX, op=ALU.max))
                    abank_free[bank] = sv.n
                sv_max = sv.n
                if STG2B == "ph1":
                    continue

                @blk.scalar
                def _(s, hh=hh, sv_max=sv_max):
                    s.wait_ge(sv.sem, sv_max)
                    ss.inc(s.activation(out=embuf[:, hh, :], in_=mbuf[:, hh, :],
                                        func=ACT.Exp, bias=0.0, scale=-1.0))
                ss_em = ss.n
                if STG2B == "em":
                    continue

                sv_am_l = {}

                def emit_st(b, hh=hh):
                    bank = b % 2

                    @blk.tensor
                    def _(p, b=b, hh=hh, bank=bank):
                        p.wait_ge(ss.sem, bbank_free[bank])
                        sp.inc(_emit_grams(p, b, hh, "T"))

                    @blk.scalar
                    def _(s, b=b, bank=bank, mark=sp.n, amf=am_free[bank]):
                        s.wait_ge(sp.sem, mark)
                        s.wait_ge(sp.sem, amf)
                        pb = ps_b[:, bank, :]
                        ab = am[:, bank, :]
                        if b < 4:
                            ei = s.activation(out=ab, in_=pb, func=ACT.Exp, bias=0.0,
                                              scale=1.0)
                        elif b == 4:
                            ei = s.activation(out=ab, in_=pb, func=ACT.Exp, bias=0.0,
                                              scale=0.25)
                        else:
                            s.activation(out=ab[:, 0:256], in_=pb[:, 0:256],
                                         func=ACT.Exp, bias=0.0, scale=0.0625)
                            s.activation(out=ab[:, 256:384], in_=pb[:, 256:384],
                                         func=ACT.Exp, bias=0.0, scale=4.0 ** -3)
                            s.activation(out=ab[:, 384:448], in_=pb[:, 384:448],
                                         func=ACT.Exp, bias=0.0, scale=4.0 ** -4)
                            s.activation(out=ab[:, 448:480], in_=pb[:, 448:480],
                                         func=ACT.Exp, bias=0.0, scale=4.0 ** -5)
                            ei = s.activation(out=ab[:, 480:512], in_=pb[:, 480:512],
                                              func=ACT.Exp, bias=0.0, scale=4.0 ** -6)
                        ss.inc(ei)
                    bbank_free[bank] = ss.n

                    @blk.vector
                    def _(v, b=b, bank=bank, mark=ss.n):
                        v.wait_ge(ss.sem, mark)
                        ab3 = am[:, bank, :].rearrange("p (n f) -> p n f", f=128)
                        if b < 4:
                            mi = v.tensor_tensor(out=ab3, in0=ab3,
                                            in1=msk_t[:, 0:1, :].broadcast_to(
                                                (128, 4, 128)), op=ALU.mult)
                        elif b == 4:
                            mi = v.tensor_tensor(out=ab3, in0=ab3,
                                            in1=msk_t[:, 1:2, :].broadcast_to(
                                                (128, 4, 128)), op=ALU.mult)
                        else:
                            v.tensor_tensor(out=ab3[:, 0:3, :], in0=ab3[:, 0:3, :],
                                            in1=msk_t[:, 1:2, :].broadcast_to(
                                                (128, 3, 128)), op=ALU.mult)
                            mi = v.tensor_tensor(out=ab3[:, 3, :], in0=ab3[:, 3, :],
                                                 in1=msk_t[:, 2, :], op=ALU.mult)
                        sv.inc(mi)
                    sv_am_l[b] = sv.n

                def emit_pv(b, hh=hh, pair=pair):
                    bank = b % 2
                    sv_am = sv_am_l[b]

                    @blk.tensor
                    def _(p, pair=pair, hh=hh, b=b, bank=bank, sv_am=sv_am,
                          st_vpn=st_vpn):
                        p.wait_ge(sv.sem, sv_am)
                        p.wait_ge(sv.sem, ybank_free[bank])
                        if b == 4 and hh == 0:
                            p.wait_ge(st.sem, st_vpn)
                        h = 2 * pair + hh
                        K3PV = _os.environ.get("K3PV", "full")
                        first = True
                        lastpv = None
                        for bi in range(4):
                            ai = 4 * b + bi
                            kind = GRAM[ai][0]
                            amat = am[:, bank, 128 * bi:128 * (bi + 1)]
                            yc = 66 * bi
                            if kind == "L0":
                                rhs = v_sb[:, ai, 64 * h:64 * h + 64]
                                lastpv = p.matmul(ps_y[:, bank, yc:yc + 64], amat, rhs,
                                         start=True, stop=True,
                                         skip_group_check=True)
                                if K3PV != "noones":
                                    lastpv = p.matmul(ps_y[:, bank, yc + 64:yc + 65], amat,
                                             onescol[:], start=True,
                                             stop=True, skip_group_check=True)
                            elif kind != "CB":
                                mt = {16: 0, 17: 1, 18: 2, 19: 3,
                                      20: 8, 21: 9, 22: 12}[ai]
                                rhs = vPn[:, mt, 64 * hh:64 * hh + 64]
                                if K3PV == "nolvl":
                                    continue
                                lastpv = p.matmul(ps_y[:, bank, yc:yc + 64], amat, rhs,
                                         start=True, stop=True,
                                         skip_group_check=True)
                                if K3PV != "noones":
                                    lastpv = p.matmul(ps_y[:, bank, yc + 64:yc + 65], amat,
                                             onescol[:], start=True,
                                             stop=True, skip_group_check=True)
                            else:
                                pieces = [(0, 64, 14), (64, 48, 15)]
                                for (r0, w, mt) in pieces:
                                    asl = am[r0:r0 + w, bank,
                                             128 * bi + r0:128 * bi + r0 + w]
                                    p.matmul(ps_y[r0:r0 + w, bank, yc:yc + 64],
                                             asl,
                                             vPn[r0:r0 + w, mt,
                                                 64 * hh:64 * hh + 64],
                                             start=True, stop=True,
                                             skip_group_check=True)
                                    lastpv = p.matmul(
                                        ps_y[r0:r0 + w, bank, yc + 64:yc + 65],
                                        asl, onescol[r0:r0 + w, :], start=True,
                                        stop=True, skip_group_check=True)
                            first = False
                        sp.inc(lastpv)
                    am_free[bank] = sp.n

                def emit_scale(b, hh=hh, pair=pair, ss_em=ss_em):
                    bank = b % 2

                    @blk.vector
                    def _(v, pair=pair, hh=hh, b=b, bank=bank, mark=sp.n,
                          ss_em=ss_em):
                        v.wait_ge(sp.sem, mark)
                        v.wait_ge(ss.sem, ss_em)
                        src = ps_y[:, bank, 0:264].rearrange("p (n f) -> p n f", f=66)[:, :, 0:65]
                        if b < 4:
                            dst = y0[:, hh, 4 * b:4 * (b + 1), :]
                        elif b == 4:
                            dst = stg[:, hh, 0:4, :]
                        else:
                            dst = stg[:, hh, 4:8, :]
                        snap = _dbg and pair == 7 and hh == 0 and b >= 4
                        if snap:
                            v.tensor_copy(out=em2[:, (b - 4) * 4:(b - 4) * 4 + 4],
                                          in_=embuf[:, hh, 4 * b:4 * (b + 1)])
                        if snap and b == 4:
                            bca = embuf[:, hh, 4 * b:4 * (b + 1)].unsqueeze(2) \
                                .broadcast_to((128, 4, 65))
                            v.tensor_copy(
                                out=em2[:, 8:12].rearrange("p (a c) -> p a c", c=1),
                                in_=bca[:, :, 0:1])
                            v.scalar_tensor_tensor(
                                out=em2[:, 16:81].rearrange("p (a c) -> p a c", a=1),
                                in0=src[:, 0:1, :], scalar=1.0, op0=ALU.mult,
                                op1=ALU.mult,
                                in1=embuf[:, hh, 16:17].unsqueeze(2)
                                    .broadcast_to((128, 1, 65)))
                        sv.inc(v.scalar_tensor_tensor(
                            out=dst, in0=src, scalar=1.0, op0=ALU.mult,
                            op1=ALU.mult,
                            in1=embuf[:, hh, 4 * b:4 * (b + 1)].unsqueeze(2)
                                .broadcast_to((128, 4, 65))))
                        if snap and b == 4:
                            v.tensor_copy(out=em2[:, 81:146],
                                          in_=stg[:, 0, 0, :])
                    ybank_free[bank] = sv.n

                for k in range(7):
                    if k < 6:
                        emit_st(k)
                    if k >= 1:
                        emit_pv(k - 1)
                        emit_scale(k - 1)
                sv_stage = sv.n
                if STG2B == "ph3":
                    continue

                PIDX = {1: 0, 2: 4, 3: 12, 4: 20, "C56": 28}
                for wi, wave in enumerate([list(range(0, 7)), list(range(7, 14)),
                                           [14, 15]]):
                    bank = wi % 2

                    @blk.tensor
                    def _(p, hh=hh, wave=wave, bank=bank, sv_stage=sv_stage):
                        p.wait_ge(sv.sem, sv_stage)
                        p.wait_ge(sv.sem, ybank_free[bank])
                        pd = ps_y[:, bank, 0:462].rearrange("p (n f) -> p n f", f=66)
                        K3G = _os.environ.get("K3G", "")
                        mms = []
                        for ti, tt in enumerate(wave):
                            mms.append((pd[:, ti, 0:65], ident[:], y0[:, hh, tt, :]))
                        if K3G == "ionly":
                            for mi, (o_, lt, rh) in enumerate(mms):
                                gi = p.matmul(o_, lt, rh, start=(mi == 0),
                                              stop=(mi == len(mms) - 1),
                                              skip_group_check=True)
                            sp.inc(gi)
                            return
                        for l in range(1, 5):
                            for ti, tt in enumerate(wave):
                                if l == 1:
                                    var, acol = tt % 4, tt // 4
                                elif l == 2:
                                    var, acol = tt % 8, 4 + tt // 8
                                elif l == 3:
                                    if tt % 2 == 0:
                                        continue
                                    var, acol = (tt - 1) // 2, 6
                                else:
                                    if tt % 4 < 2:
                                        continue
                                    var = 2 * (tt // 4) + (tt % 4 - 2)
                                    acol = 7
                                mms.append((pd[:, ti, 0:65],
                                            perm[:, PIDX[l] + var, :],
                                            stg[:, hh, acol, :]))
                        for ti, tt in enumerate(wave):
                            if tt < 4:
                                continue
                            mms.append((pd[:, ti, 0:65],
                                        perm[:, PIDX["C56"] + (tt - 4), :],
                                        stg[:, hh, 7, :]))
                        if K3G == "tilemajor":
                            bytile = {}
                            for (o_, lt, rh) in mms:
                                bytile.setdefault(id(o_.tensor) if False else str(o_.offset), []).append((o_, lt, rh))
                            per = [[] for _ in wave]
                            k = len(wave)
                            per_idx = {}
                            mms2 = []
                            for ti in range(len(wave)):
                                mms2.append(mms[ti])
                            rest = mms[len(wave):]
                            for m in rest:
                                pass
                            groups = {}
                            order = []
                            for m in mms:
                                key = str(m[0].offset)
                                if key not in groups:
                                    groups[key] = []
                                    order.append(key)
                                groups[key].append(m)
                            for key in order:
                                g_ = groups[key]
                                for mi, (o_, lt, rh) in enumerate(g_):
                                    gi = p.matmul(o_, lt, rh, start=(mi == 0),
                                                  stop=(mi == len(g_) - 1),
                                                  skip_group_check=True)
                        else:
                            for mi, (o_, lt, rh) in enumerate(mms):
                                gi = p.matmul(o_, lt, rh, start=(mi == 0),
                                              stop=(mi == len(mms) - 1),
                                              skip_group_check=True)
                        sp.inc(gi)

                    if _os.environ.get("K3G", "") == "mmonly":
                        ybank_free[bank] = 0
                        continue

                    @blk.vector
                    def _(v, pair=pair, hh=hh, wave=wave, bank=bank, mark=sp.n):
                        v.wait_ge(sp.sem, mark)
                        nw = len(wave)
                        t0 = wave[0]
                        pd = ps_y[:, bank, 0:462].rearrange("p (n f) -> p n f", f=66)
                        v.tensor_scalar(out=avb[:, t0:t0 + nw],
                                        in0=pd[:, 0:nw, 64], scalar1=1e-8,
                                        scalar2=None, op0=ALU.add)
                        v.drain()
                        v.reciprocal(out=rinv[:, t0:t0 + nw], in_=avb[:, t0:t0 + nw])
                        v.drain()
                        if wave[0] == 0 and (pair + hh) > 0:
                            v.wait_ge(sa.sem, 16 * (2 * pair + hh))
                        sv.inc(v.scalar_tensor_tensor(
                            out=attn_hd[:, t0:t0 + nw, :],
                            in0=pd[:, 0:nw, 0:64], scalar=1.0,
                            op0=ALU.mult, op1=ALU.mult,
                            in1=rinv[:, t0:t0 + nw].unsqueeze(2)
                                .broadcast_to((128, nw, 64))))
                    ybank_free[bank] = sv.n

                @blk.gpsimd
                def _(g, pair=pair, hh=hh, mark=sv.n):
                    g.wait_ge(sv.sem, mark)
                    h = 2 * pair + hh
                    sa.inc(g.dma_start(
                        out=attn_sp.rearrange("(t p) d -> p t d", p=128)[:, :, 64 * h:64 * h + 64],
                        in_=attn_hd[:]), 16)
        if xn_dbg is not None:
            @blk.gpsimd
            def _(g, mark=sv.n):
                g.wait_ge(sv.sem, mark)
                sa.inc(g.dma_start(out=qp_dbg, in_=qP[:]), 16)
                sa.inc(g.dma_start(out=kp_dbg, in_=kP[:]), 16)
                sa.inc(g.dma_start(out=vp_dbg, in_=vP[:]), 16)
                sa.inc(g.dma_start(out=stg_dbg, in_=stg[:].rearrange("p a b c -> p (a b c)")), 16)
                sa.inc(g.dma_start(out=y0_dbg, in_=y0[:].rearrange("p a b c -> p (a b c)")), 16)
                sa.inc(g.dma_start(out=em_dbg, in_=embuf[:].rearrange("p a b -> p (a b)")), 16)
                sa.inc(g.dma_start(out=em2_dbg, in_=em2[:]), 16)

        sv_attn_all = sv.n
        if phase_limit <= 3:
            @blk.gpsimd
            def _(g, a=st.n, b=sv.n, c=ss.n, d=sp.n, e=sa.n):
                g.wait_ge(st.sem, a)
                g.wait_ge(sv.sem, b)
                g.wait_ge(ss.sem, c)
                g.wait_ge(sp.sem, d)
                g.wait_ge(sa.sem, e)
                sa.inc(g.dma_start(out=out_h[0:128, 0:24], in_=mbuf[:].rearrange("p a b -> p (a b)")[:, 0:24]), 16)
                g.wait_ge(sa.sem, e + 16)
            return nc

        ctx3.close()
        ctx2.close()
        ctx4 = contextlib.ExitStack()
        ctx.enter_context(ctx4)
        E4 = ctx4.enter_context
        xdT = E4(nc.sbuf_tensor("xdT", [128, NC, T], BF16))
        attnT = E4(nc.sbuf_tensor("attnT", [128, NC, T], BF16))
        wdT = E4(nc.sbuf_tensor("wdT", [128, NC, D], BF16))
        woutT = E4(nc.sbuf_tensor("woutT_s", [128, NC, D], BF16))
        x_r = E4(nc.sbuf_tensor("x_r", [128, 2, D], F32))
        out_sb = E4(nc.sbuf_tensor("out_sb", [128, 2, D], F32))

        obank_free = {0: 0, 1: 0}
        xr_free = {0: 0, 1: 0}
        ob_free = {0: 0, 1: 0}
        @blk.gpsimd
        def _(g, mark=sv_attn_all):
            g.wait_ge(sv.sem, mark)
            for c_ in range(NC):
                sw.inc(g.dma_start(out=woutT[:, c_, :],
                                   in_=woutT_h[128 * c_:128 * (c_ + 1), :]), 16)
        sw_marks["woutT"] = sw.n

        @blk.sync
        def _(s, san=sa.n):
            s.wait_ge(sa.sem, san)
            for wt in range(NC):
                st.inc(s.dma_start_transpose(out=wdT[:, :, 128 * wt:128 * (wt + 1)],
                                             in_=wd_sp[128 * wt:128 * (wt + 1), :]), 16)
        st_wd_done = st.n

        at_mark = []
        for t in range(NT):
            @blk.sync
            def _(s, t=t):
                st.inc(s.dma_start_transpose(out=attnT[:, :, 128 * t:128 * (t + 1)],
                                             in_=attn_sp[128 * t:128 * (t + 1), :]), 16)
                st.inc(s.dma_start_transpose(out=xdT[:, :, 128 * t:128 * (t + 1)],
                                             in_=xd_sp[128 * t:128 * (t + 1), :]), 16)
                if t >= 2:
                    s.wait_ge(sv.sem, xr_free[t % 2])
                sx.inc(s.dma_start(out=x_r[:, t % 2, :],
                                   in_=x_h[128 * t:128 * (t + 1), :]), 16)
            at_mark.append((st.n, sx.n))

            for n2 in range(2):
                bank = (2 * t + n2) % 2

                @blk.tensor
                def _(p, t=t, n2=n2, bank=bank):
                    p.wait_ge(st.sem, at_mark[t][0])
                    if t == 0 and n2 == 0:
                        p.wait_ge(st.sem, st_wd_done)
                        p.wait_ge(sw.sem, sw_marks["woutT"])
                        p.wait_ge(ss.sem, gbank_free[0])
                        p.wait_ge(ss.sem, gbank_free[1])
                    p.wait_ge(sv.sem, obank_free[bank])
                    p.matmul(ps_g[:, bank, :], ones2[:],
                             brow_b[:, 512 * n2:512 * (n2 + 1)],
                             start=True, stop=False)
                    for c in range(NC):
                        p.matmul(ps_g[:, bank, :], attnT[:, c, 128 * t:128 * (t + 1)],
                                 woutT[:, c, 512 * n2:512 * (n2 + 1)],
                                 start=False, stop=False)
                    for c in range(NC):
                        mm = p.matmul(ps_g[:, bank, :], xdT[:, c, 128 * t:128 * (t + 1)],
                                 wdT[:, c, 512 * n2:512 * (n2 + 1)],
                                 start=False, stop=(c == NC - 1))
                    sp.inc(mm)

                @blk.vector
                def _(v, t=t, n2=n2, bank=bank, mark=sp.n):
                    v.wait_ge(sp.sem, mark)
                    v.wait_ge(sx.sem, at_mark[t][1])
                    if t >= 2 and n2 == 0:
                        v.wait_ge(so.sem, ob_free[t % 2])
                    sv.inc(v.tensor_tensor(
                        out=out_sb[:, t % 2, 512 * n2:512 * (n2 + 1)],
                        in0=x_r[:, t % 2, 512 * n2:512 * (n2 + 1)],
                        in1=ps_g[:, bank, :], op=ALU.add))
                obank_free[bank] = sv.n
            xr_free[t % 2] = sv.n

            @blk.sync
            def _(s, t=t, mark=sv.n):
                s.wait_ge(sv.sem, mark)
                so.inc(s.dma_start(out=out_h[128 * t:128 * (t + 1), :],
                                   in_=out_sb[:, t % 2, :]), 16)
            ob_free[t % 2] = so.n

        @blk.gpsimd
        def _(g):
            g.wait_ge(so.sem, so.n)

    return nc


_NC_CACHE = {}


def _get_nc():
    import os
    pl = int(os.environ.get("KPHASE", "99"))
    key = ("nc", pl)
    if key not in _NC_CACHE:
        _NC_CACHE[key] = build_nc(pl)
    return _NC_CACHE[key]


def kernel(L0_attn_x, L0_attn_w_qkv, L0_attn_w_out, L0_attn_b_out,
           L0_ffn_weight, L0_ffn_bias):
    x = np.ascontiguousarray(np.asarray(L0_attn_x, dtype=np.float32))
    wqkvT = np.ascontiguousarray(np.asarray(L0_attn_w_qkv, dtype=np.float32).T)
    woutT = np.ascontiguousarray(np.asarray(L0_attn_w_out, dtype=np.float32).T)
    wffn = np.ascontiguousarray(np.asarray(L0_ffn_weight, dtype=np.float32))
    brow = np.ascontiguousarray(
        np.stack([np.asarray(L0_attn_b_out, dtype=np.float32),
                  np.asarray(L0_ffn_bias, dtype=np.float32)], axis=0))
    hc = _host_consts()
    base = {"wqkvT": wqkvT, "woutT": woutT, "wffn": wffn, "brow": brow}
    for k, v in hc.items():
        base["c_" + k] = np.ascontiguousarray(v)
    in_maps = []
    for c in range(8):
        m = dict(base)
        m["x"] = np.ascontiguousarray(x[c])
        in_maps.append(m)
    nc = _get_nc()
    res = run_bass_kernel_spmd(nc, in_maps, core_ids=list(range(8)))
    out = np.stack([np.asarray(r["out"]) for r in res.results], axis=0)
    return out.astype(np.float32)


# version 04
# speedup vs baseline: 1.1621x; 1.0641x over previous
import contextlib

import numpy as np
import ml_dtypes

import concourse.bass as bass
import concourse.mybir as mybir
from concourse.bass_utils import run_bass_kernel_spmd

F32 = mybir.dt.float32
BF16 = mybir.dt.bfloat16
FP8 = mybir.dt.float8e4
AX = mybir.AxisListType.X
ALU = mybir.AluOpType
ACT = mybir.ActivationFunctionType

T = 2048
D = 1024
NT = T // 128
NC = D // 128
NPAIR = 8
MAGIC8 = 1572864.0
NEG = -1.0e30

_EOFF, _OOFF = {}, {}
_o = 0
for _l in range(1, 5):
    n = 2048 >> _l
    _EOFF[_l] = _o
    _o += n // 2
    _OOFF[_l] = _o
    _o += n // 2
_OOFF[5], _OOFF[6], _EOFF[5], _EOFF[6] = 1920, 1952, 1984, 2016

_RL = {1: 32, 2: 16, 3: 16, 4: 8, 5: 4, 6: 2}


def _mask_s(causal, mask_pad):
    m = np.full((128, 128), NEG, np.float32)
    for i in range(128):
        b = i // 16
        if mask_pad and b == 7:
            continue
        for j in range(16 * b, 16 * b + 16):
            if causal and j % 16 > i % 16:
                continue
            m[i, j] = 0.0
    return m


def _mask_t01(causal, mask_pad):
    m = np.zeros((128, 128), np.float32)
    for j in range(128):
        b = j // 16
        if mask_pad and b == 7:
            continue
        for i in range(16 * b, 16 * b + 16):
            if causal and j % 16 > i % 16:
                continue
            m[j, i] = 1.0
    return m.astype(ml_dtypes.bfloat16)


_NVAR = {1: 4, 2: 8, 3: 8, 4: 8}


def _perm_names():
    names = []
    for l in range(1, 5):
        for v in range(_NVAR[l]):
            names.append(f"P_L{l}_{v}")
    for v in range(12):
        names.append(f"P_C56_{v}")
    return names


def _host_consts():
    c = {}
    c["mask_s0"] = _mask_s(True, False)
    c["mask_sb"] = _mask_s(False, False)
    c["mask_sc"] = _mask_s(False, True)
    c["mask_t0"] = _mask_t01(True, False)
    c["mask_tb"] = _mask_t01(False, False)
    c["mask_tc"] = _mask_t01(False, True)
    c["ident"] = np.eye(128, dtype=np.float32).astype(ml_dtypes.bfloat16)
    for l in range(1, 5):
        r = _RL[l]
        for v in range(_NVAR[l]):
            m = np.zeros((128, 128), np.float32)
            for j in range(r):
                row = v * r + j
                w = row % 16
                m[row, ((17 * (1 << l) - 1) + (1 << l) * w
                        + (1 << (l + 5)) * (row // 16)) % 128] = 1.0
            c[f"P_L{l}_{v}"] = m.astype(ml_dtypes.bfloat16)
    for v in range(12):
        tt = v + 4
        m = np.zeros((128, 128), np.float32)
        if (tt // 4) % 2 == 1:
            pp, qq = tt // 8, tt % 4
            for j in range(4):
                j5 = 16 * pp + 4 * qq + j
                m[64 + j5, (543 + 32 * j5) % 128] = 1.0
        if tt >= 8:
            for j in range(2):
                j6 = 2 * (tt - 8) + j
                m[96 + j6, (1087 + 64 * j6) % 128] = 1.0
        c[f"P_C56_{v}"] = m.astype(ml_dtypes.bfloat16)
    c["ones2"] = np.ones((2, 128), np.float32).astype(ml_dtypes.bfloat16)
    c["zeros2"] = np.zeros((2, 128), np.float32).astype(ml_dtypes.bfloat16)
    c["onescol"] = np.ones((128, 1), np.float32).astype(ml_dtypes.bfloat16)
    sc = np.zeros((128, 1), np.float32)
    sc[0:64] = 4.0 ** -4
    sc[64:96] = 4.0 ** -5
    sc[96:128] = 4.0 ** -6
    c["scalecol"] = sc
    return c


class Ctr:
    def __init__(self, sem):
        self.sem = sem
        self.n = 0

    def inc(self, inst, k=1):
        inst.then_inc(self.sem, k)
        self.n += k
        return self.n


def build_nc(phase_limit=99):
    nc = bass.Bass("TRN2", target_bir_lowering=False)
    x_h = nc.dram_tensor("x", [T, D], F32, kind="ExternalInput").ap()
    wqkvT_h = nc.dram_tensor("wqkvT", [D, 3 * D], F32, kind="ExternalInput").ap()
    woutT_h = nc.dram_tensor("woutT", [D, D], F32, kind="ExternalInput").ap()
    wffn_h = nc.dram_tensor("wffn", [D, D], F32, kind="ExternalInput").ap()
    brow_h = nc.dram_tensor("brow", [2, D], F32, kind="ExternalInput").ap()
    hc = _host_consts()
    cst = {}
    for name, arr in hc.items():
        dt = F32 if arr.dtype == np.float32 else BF16
        cst[name] = nc.dram_tensor("c_" + name, list(arr.shape), dt,
                                   kind="ExternalInput").ap()
    import os as _os0
    _dbg = int(_os0.environ.get("KDBG", "0"))
    _kind = "ExternalOutput" if _dbg else "Internal"
    out_h = nc.dram_tensor("out", [T, D], F32, kind="ExternalOutput").ap()
    xd_sp = nc.dram_tensor("xd_spill", [T, D], BF16, kind=_kind).ap()
    wd_sp = nc.dram_tensor("wd_spill", [D, D], BF16, kind=_kind).ap()
    attn_sp = nc.dram_tensor("attn_spill", [T, D], BF16, kind=_kind).ap()
    xn_dbg = nc.dram_tensor("xn_dbg", [T, D], BF16, kind=_kind).ap() if _dbg else None
    qp_dbg = nc.dram_tensor("qp_dbg", [128, 2048], BF16, kind=_kind).ap() if _dbg else None
    kp_dbg = nc.dram_tensor("kp_dbg", [128, 2048], BF16, kind=_kind).ap() if _dbg else None
    vp_dbg = nc.dram_tensor("vp_dbg", [128, 2048], BF16, kind=_kind).ap() if _dbg else None
    stg_dbg = nc.dram_tensor("stg_dbg", [128, 2 * 8 * 65], BF16, kind=_kind).ap() if _dbg else None
    y0_dbg = nc.dram_tensor("y0_dbg", [128, 2 * 16 * 65], BF16, kind=_kind).ap() if _dbg else None
    em_dbg = nc.dram_tensor("em_dbg", [128, 48], F32, kind=_kind).ap() if _dbg else None
    em2_dbg = nc.dram_tensor("em2_dbg", [128, 160], F32, kind=_kind).ap() if _dbg else None

    ctx = contextlib.ExitStack()
    with ctx:
        E = ctx.enter_context

        brow_b = E(nc.sbuf_tensor("brow_b", [2, D], BF16))
        msk_s = E(nc.sbuf_tensor("msk_s", [128, 3, 128], F32))
        msk_t = E(nc.sbuf_tensor("msk_t", [128, 3, 128], BF16))
        ident = E(nc.sbuf_tensor("ident", [128, 128], BF16))
        nperm = len(_perm_names())
        perm = E(nc.sbuf_tensor("perm", [128, nperm, 128], BF16))
        ones2 = E(nc.sbuf_tensor("ones2", [2, 128], BF16))
        zeros2 = E(nc.sbuf_tensor("zeros2", [2, 128], BF16))
        onescol = E(nc.sbuf_tensor("onescol", [128, 1], BF16))
        scalecol = E(nc.sbuf_tensor("scalecol", [128, 1], F32))
        _ACT_CONSTS = [1e-5, -1.0, -0.5, 0.125, 0.25, 0.0625,
                       4.0 ** -3, 4.0 ** -4, 4.0 ** -5, 4.0 ** -6]
        actc = E(nc.sbuf_tensor("actc", [128, len(_ACT_CONSTS)], F32))
        for i, val in enumerate(_ACT_CONSTS):
            nc.const_aps.register(F32, val, actc[:, i:i + 1])

        ps_g = E(nc.psum_tensor("ps_g", [128, 2, 512], F32))
        ps_a = E(nc.psum_tensor("ps_a", [128, 2, 512], F32))
        ps_b = E(nc.psum_tensor("ps_b", [128, 2, 512], F32))
        ps_y = E(nc.psum_tensor("ps_y", [128, 2, 512], F32))

        ctx2 = contextlib.ExitStack()
        ctx.enter_context(ctx2)
        E2 = ctx2.enter_context
        xnT = E2(nc.sbuf_tensor("xnT", [128, NC, T], BF16))
        wqkvT = E2(nc.sbuf_tensor("wqkvT_s", [128, NC, 3 * D], BF16))
        v_sb = E2(nc.sbuf_tensor("v_sb", [128, NT, D], BF16))
        vT = E2(nc.sbuf_tensor("vT", [128, NPAIR, T], BF16))

        ctx1 = contextlib.ExitStack()
        ctx.enter_context(ctx1)
        E1 = ctx1.enter_context
        x_t = E1(nc.sbuf_tensor("x_t", [128, 2, D], F32))
        wffn_t = E1(nc.sbuf_tensor("wffn_t", [128, 1, D], F32))
        wd_t = E1(nc.sbuf_tensor("wd_t", [128, 2, D], BF16))
        xn_f = E1(nc.sbuf_tensor("xn_f", [128, 2, D], F32))
        xnb_t = E1(nc.sbuf_tensor("xnb_t", [128, 2, D], BF16))
        
        q_g = E1(nc.sbuf_tensor("q_g", [128, 32], F32))
        q_inv = E1(nc.sbuf_tensor("q_inv", [128, 32], F32))
        q_s = E1(nc.sbuf_tensor("q_s", [128, 32], F32))
        q_u = E1(nc.sbuf_tensor("q_u", [128, D], F32))
        q_mg = E1(nc.sbuf_tensor("q_mg", [128, D], F32))
        q_h8 = E1(nc.sbuf_tensor("q_h8", [128, D], FP8))
        q_l8 = E1(nc.sbuf_tensor("q_l8", [128, D], FP8))
        q_qt = E1(nc.sbuf_tensor("q_qt", [128, D], F32))
        xd_t = E1(nc.sbuf_tensor("xd_t", [128, 2, D], BF16))
        dump = q_u
        ln_nm = E1(nc.sbuf_tensor("ln_nm", [128, 2], F32))
        ln_sx = E1(nc.sbuf_tensor("ln_sx", [128, 2], F32))
        ln_sq = E1(nc.sbuf_tensor("ln_sq", [128, 2], F32))
        ln_s2 = E1(nc.sbuf_tensor("ln_s2", [128, 2], F32))
        ln_nv = E1(nc.sbuf_tensor("ln_nv", [128, 2], F32))
        ln_rs = E1(nc.sbuf_tensor("ln_rs", [128, 2], F32))
        ln_nmr = E1(nc.sbuf_tensor("ln_nmr", [128, 2], F32))

        sx = Ctr(E(nc.semaphore("sx")))
        sw = Ctr(E(nc.semaphore("sw")))
        st = Ctr(E(nc.semaphore("st")))
        sv = Ctr(E(nc.semaphore("sv")))
        ss = Ctr(E(nc.semaphore("ss")))
        sp = Ctr(E(nc.semaphore("sp")))
        so = Ctr(E(nc.semaphore("so")))
        sa = Ctr(E(nc.semaphore("sa")))
        blk = E(nc.Block())

        @blk.vector
        def _(v):
            for i, val in enumerate(_ACT_CONSTS):
                v.memset(actc[:, i:i + 1], val)

        @blk.sync
        def _(s):
            for t in range(2):
                sx.inc(s.dma_start(out=x_t[:, t, :],
                                   in_=x_h[128 * t:128 * (t + 1), :]), 16)

        sw_marks = {}

        @blk.gpsimd
        def _(g):
            for c_ in range(NC):
                for h_ in range(2):
                    sw.inc(g.dma_start(
                        out=wqkvT[:, c_, 1536 * h_:1536 * (h_ + 1)],
                        in_=wqkvT_h[128 * c_:128 * (c_ + 1),
                                    1536 * h_:1536 * (h_ + 1)]), 16)
            sw_marks["wqkvT"] = sw.n
            sw.inc(g.dma_start(out=brow_b[:], in_=brow_h[:]), 16)
            for i, nm in enumerate(["mask_s0", "mask_sb", "mask_sc"]):
                sw.inc(g.dma_start(out=msk_s[:, i, :], in_=cst[nm]), 16)
            for i, nm in enumerate(["mask_t0", "mask_tb", "mask_tc"]):
                sw.inc(g.dma_start(out=msk_t[:, i, :], in_=cst[nm]), 16)
            sw.inc(g.dma_start(out=ident[:], in_=cst["ident"]), 16)
            for pi, nm in enumerate(_perm_names()):
                sw.inc(g.dma_start(out=perm[:, pi, :], in_=cst[nm]), 16)
            sw.inc(g.dma_start(out=ones2[:], in_=cst["ones2"]), 16)
            sw.inc(g.dma_start(out=zeros2[:], in_=cst["zeros2"]), 16)
            sw.inc(g.dma_start(out=onescol[:], in_=cst["onescol"]), 16)
            sw.inc(g.dma_start(out=scalecol[:], in_=cst["scalecol"]), 16)
            sw_marks["all"] = sw.n

        def _quant(v, src, dst):
            v.tensor_reduce(out=q_g[:], in_=src.rearrange("p (b k) -> p b k", k=32),
                            axis=AX, op=ALU.max, apply_absolute_value=True)
            v.drain()
            v.reciprocal(out=q_inv[:], in_=q_g[:])
            v.drain()
            v.tensor_scalar(out=q_inv[:], in0=q_inv[:], scalar1=7.5, scalar2=7.5e30,
                            op0=ALU.mult, op1=ALU.min)
            v.tensor_scalar(out=q_s[:], in0=q_g[:], scalar1=1.0 / 7.5, scalar2=1e-30,
                            op0=ALU.mult, op1=ALU.max)
            v.tensor_tensor(out=q_u[:].rearrange("p (b k) -> p b k", k=32),
                            in0=src.rearrange("p (b k) -> p b k", k=32),
                            in1=q_inv[:].unsqueeze(2).broadcast_to((128, 32, 32)),
                            op=ALU.mult)
            v.tensor_scalar(out=q_mg[:], in0=q_u[:], scalar1=1.0, scalar2=-1.0,
                            op0=ALU.min, op1=ALU.max)
            v.tensor_scalar(out=q_mg[:], in0=q_mg[:], scalar1=MAGIC8, scalar2=MAGIC8,
                            op0=ALU.add, op1=ALU.subtract)
            v.tensor_scalar(out=q_h8[:], in0=q_u[:], scalar1=1.0, scalar2=None,
                            op0=ALU.max)
            v.tensor_scalar(out=q_l8[:], in0=q_u[:], scalar1=-1.0, scalar2=None,
                            op0=ALU.min)
            v.scalar_tensor_tensor(out=q_qt[:], in0=q_mg[:], scalar=0.0, in1=q_h8[:],
                                   op0=ALU.add, op1=ALU.add)
            v.scalar_tensor_tensor(out=q_qt[:], in0=q_qt[:], scalar=0.0, in1=q_l8[:],
                                   op0=ALU.add, op1=ALU.add)
            return v.tensor_tensor(out=dst.rearrange("p (b k) -> p b k", k=32),
                            in0=q_qt[:].rearrange("p (b k) -> p b k", k=32),
                            in1=q_s[:].unsqueeze(2).broadcast_to((128, 32, 32)),
                            op=ALU.mult)

        st_wq = []
        xn_done = []
        sv_xnb_t = []
        sv_xd_tl = []
        ss_xnf_t = []
        for t in range(NT):
            r2 = t % 2

            @blk.vector
            def _(v, t=t, r2=r2):
                v.wait_ge(sx.sem, 16 * (t + 1))
                if t >= 2:
                    v.wait_ge(ss.sem, ss_xnf_t[t - 2])
                xt = x_t[:, r2, :]
                v.tensor_reduce(out=ln_sx[:, r2:r2 + 1], in_=xt, axis=AX, op=ALU.add)
                v.tensor_tensor(out=dump[:], in0=xt, in1=xt, op=ALU.mult)
                v.tensor_reduce(out=ln_sq[:, r2:r2 + 1], in_=dump[:], axis=AX,
                                op=ALU.add)
                v.tensor_scalar(out=ln_nm[:, r2:r2 + 1], in0=ln_sx[:, r2:r2 + 1],
                                scalar1=-1.0 / D, scalar2=None, op0=ALU.mult)
                v.tensor_scalar(out=ln_s2[:, r2:r2 + 1], in0=ln_sq[:, r2:r2 + 1],
                                scalar1=1.0 / D, scalar2=None, op0=ALU.mult)
                sv.inc(v.scalar_tensor_tensor(
                    out=ln_nv[:, r2:r2 + 1], in0=ln_nm[:, r2:r2 + 1],
                    scalar=ln_nm[:, r2:r2 + 1], in1=ln_s2[:, r2:r2 + 1],
                    op0=ALU.mult, op1=ALU.subtract))

            sv_stats = sv.n

            @blk.scalar
            def _(s, t=t, r2=r2, sv_stats=sv_stats):
                s.wait_ge(sv.sem, sv_stats)
                if t >= 2:
                    s.wait_ge(sv.sem, sv_xd_tl[t - 2])
                s.activation(out=ln_s2[:, r2:r2 + 1], in_=ln_nv[:, r2:r2 + 1],
                             func=ACT.Ln, bias=1e-5, scale=-1.0)
                s.activation(out=ln_rs[:, r2:r2 + 1], in_=ln_s2[:, r2:r2 + 1],
                             func=ACT.Exp, bias=0.0, scale=-0.5)
                s.mul(out=ln_nmr[:, r2:r2 + 1], in_=ln_rs[:, r2:r2 + 1],
                      mul=ln_nm[:, r2:r2 + 1])
                ss.inc(s.mul(out=ln_s2[:, r2:r2 + 1], in_=ln_rs[:, r2:r2 + 1],
                             mul=1.0))

            ss_xnf_t.append(ss.n)

            @blk.vector
            def _(v, t=t, r2=r2, mark=ss.n):
                v.wait_ge(ss.sem, mark)
                if t >= 2:
                    v.wait_ge(st.sem, xn_done[t - 2] + (32 if xn_dbg is not None else 16))
                sv.inc(v.tensor_scalar(out=xnb_t[:, r2, :], in0=x_t[:, r2, :],
                                scalar1=ln_rs[:, r2:r2 + 1],
                                scalar2=ln_nmr[:, r2:r2 + 1],
                                op0=ALU.mult, op1=ALU.add))
                v.tensor_scalar(out=xn_f[:, r2, :], in0=x_t[:, r2, :],
                                scalar1=ln_rs[:, r2:r2 + 1],
                                scalar2=ln_nmr[:, r2:r2 + 1],
                                op0=ALU.mult, op1=ALU.add)
                sv.inc(_quant(v, xn_f[:, r2, :], xd_t[:, r2, :]))

            sv_xnb_t.append(sv.n - 1)
            sv_xd_tl.append(sv.n)

            @blk.sync
            def _(s, t=t, r2=r2, a=sv.n - 1, b=sv.n, ssm=ss.n):
                s.wait_ge(sv.sem, a)
                st.inc(s.dma_start_transpose(out=xnT[:, :, 128 * t:128 * (t + 1)],
                                             in_=xnb_t[:, r2, :]), 16)
                s.wait_ge(sv.sem, b)
                st.inc(s.dma_start(out=xd_sp[128 * t:128 * (t + 1), :],
                                   in_=xd_t[:, r2, :]), 16)
                if xn_dbg is not None:
                    st.inc(s.dma_start(out=xn_dbg[128 * t:128 * (t + 1), :],
                                       in_=xnb_t[:, r2, :]), 16)
                if t + 2 < NT:
                    s.wait_ge(ss.sem, ssm)
                    sx.inc(s.dma_start(out=x_t[:, t % 2, :],
                                       in_=x_h[128 * (t + 2):128 * (t + 3), :]), 16)
            xn_done.append(st.n - (32 if xn_dbg is not None else 16))

        sv_wq = []
        for wt in range(NC):
            @blk.gpsimd
            def _(g, wt=wt):
                if wt >= 1:
                    g.wait_ge(sv.sem, sv_wq[wt - 1])
                sw.inc(g.dma_start(out=wffn_t[:, 0, :],
                                   in_=wffn_h[128 * wt:128 * (wt + 1), :]), 16)

            @blk.vector
            def _(v, wt=wt, mark=sw.n):
                v.wait_ge(sw.sem, mark)
                if wt >= 2:
                    v.wait_ge(st.sem, st_wq[wt - 2])
                sv.inc(_quant(v, wffn_t[:, 0, :], wd_t[:, wt % 2, :]))
            sv_wq.append(sv.n)

            @blk.sync
            def _(s, wt=wt, mark=sv.n):
                s.wait_ge(sv.sem, mark)
                st.inc(s.dma_start(out=wd_sp[128 * wt:128 * (wt + 1), :],
                                   in_=wd_t[:, wt % 2, :]), 16)
            st_wq.append(st.n)
        st_p1_done = st.n
        if phase_limit <= 1:
            @blk.gpsimd
            def _(g, a=st.n, b=sv.n, c=ss.n):
                g.wait_ge(st.sem, a)
                g.wait_ge(sv.sem, b)
                g.wait_ge(ss.sem, c)
                sa.inc(g.dma_start(out=out_h[0:128, :], in_=x_t[:, 0, :]), 16)
                g.wait_ge(sa.sem, 16)
            return nc

        import os as _os
        N2A = int(_os.environ.get("K2A", str(NT)))
        NO_VT = int(_os.environ.get("KNOVT", "0"))
        gbank_free = {0: 0, 1: 0}
        st_v_last = 0
        for t in range(N2A):
            for n2 in range(2):
                bank = (2 * t + n2) % 2

                @blk.tensor
                def _(p, t=t, n2=n2, bank=bank):
                    if t == 0 and n2 == 0:
                        p.wait_ge(sw.sem, sw_marks["wqkvT"])
                    p.wait_ge(st.sem, xn_done[t])
                    p.wait_ge(ss.sem, gbank_free[bank])
                    for c in range(NC):
                        mm = p.matmul(ps_g[:, bank, :], xnT[:, c, 128 * t:128 * (t + 1)],
                                 wqkvT[:, c, 2 * D + 512 * n2:2 * D + 512 * (n2 + 1)],
                                 start=(c == 0), stop=(c == NC - 1))
                    sp.inc(mm)

                @blk.scalar
                def _(s, t=t, n2=n2, bank=bank, mark=sp.n):
                    s.wait_ge(sp.sem, mark)
                    ss.inc(s.copy(out=v_sb[:, t, 512 * n2:512 * (n2 + 1)],
                                  in_=ps_g[:, bank, :]))
                gbank_free[bank] = ss.n

            if not NO_VT:
                @blk.sync
                def _(s, t=t, mark=ss.n):
                    s.wait_ge(ss.sem, mark)
                    st.inc(s.dma_start_transpose(out=vT[:, :, 128 * t:128 * (t + 1)],
                                                 in_=v_sb[:, t, :]), 16)
                st_v_last = st.n

        if not int(_os.environ.get("KNOCLOSE", "0")):
            ctx1.close()
        ctx3 = contextlib.ExitStack()
        ctx.enter_context(ctx3)
        E3 = ctx3.enter_context
        qT_p = E3(nc.sbuf_tensor("qT_p", [128, T], BF16))
        kT_p = E3(nc.sbuf_tensor("kT_p", [128, T], BF16))
        qP = E3(nc.sbuf_tensor("qP", [128, 2048], BF16))
        kP = E3(nc.sbuf_tensor("kP", [128, 2048], BF16))
        vP = E3(nc.sbuf_tensor("vP", [128, 2048], BF16))
        vPn = E3(nc.sbuf_tensor("vPn", [128, 16, 128], BF16))
        am = E3(nc.sbuf_tensor("am", [128, 2, 512], BF16))
        smS = E3(nc.sbuf_tensor("smS", [128, 512], F32))
        mbuf = E3(nc.sbuf_tensor("mbuf", [128, 2, 24], F32))
        embuf = E3(nc.sbuf_tensor("embuf", [128, 2, 24], F32))
        y0 = E3(nc.sbuf_tensor("y0", [128, 2, 16, 65], BF16))
        stg = E3(nc.sbuf_tensor("stg", [128, 2, 8, 65], BF16))
        avb = E3(nc.sbuf_tensor("avb", [128, 16], F32))
        rinv = E3(nc.sbuf_tensor("rinv", [128, 16], F32))
        attn_hd = E3(nc.sbuf_tensor("attn_hd", [128, 16, 64], BF16))
        em2 = E3(nc.sbuf_tensor("em2", [128, 160], F32)) if _dbg else None

        if phase_limit <= 2:
            @blk.gpsimd
            def _(g, a=st.n, b=sv.n, c=ss.n, d=sp.n):
                g.wait_ge(st.sem, a)
                g.wait_ge(sv.sem, b)
                g.wait_ge(ss.sem, c)
                g.wait_ge(sp.sem, d)
                sa.inc(g.dma_start(out=out_h[0:128, :], in_=x_t[:, 0, :]), 16)
                g.wait_ge(sa.sem, 16)
            return nc

        GRAM = (
            [("L0", 128 * t, 128 * t, 1.0) for t in range(16)]
            + [("L1", _OOFF[1] + 128 * a, _EOFF[1] + 128 * a, 0.25) for a in range(4)]
            + [("L2", _OOFF[2] + 128 * a, _EOFF[2] + 128 * a, 0.0625) for a in range(2)]
            + [("L3", _OOFF[3], _EOFF[3], 4.0 ** -3)]
            + [("CB", None, None, None)]
        )
        CB_SUBS = [(4, 64, 0), (5, 48, 64)]

        def _emit_grams(p, b, hh, orient):
            ps = ps_a if orient == "S" else ps_b
            bank = b % 2
            pr0 = 64 * hh
            first = [True]
            lastmm = [None]

            def mm(out, lt, rh, last):
                lastmm[0] = p.matmul(out, lt, rh, start=first[0], stop=last,
                                     skip_group_check=True)
                first[0] = False

            for bi in range(4):
                ai = 4 * b + bi
                kind, qo, ko, _sc = GRAM[ai]
                lastb = bi == 3
                if kind != "CB":
                    sq = qT_p if ai < 16 else qP
                    sk = kT_p if ai < 16 else kP
                    a_, b_ = (qo, ko) if orient == "S" else (ko, qo)
                    s1, s2 = (sq, sk) if orient == "S" else (sk, sq)
                    mm(ps[:, bank, 128 * bi:128 * (bi + 1)],
                       s1[pr0:pr0 + 64, a_:a_ + 128], s2[pr0:pr0 + 64, b_:b_ + 128],
                       lastb)
                else:
                    mm(ps[:, bank, 128 * bi:128 * (bi + 1)], zeros2[:], zeros2[:],
                       False)
                    for si, (l, w, ro) in enumerate(CB_SUBS):
                        qo2, ko2 = _OOFF[l], _EOFF[l]
                        a_, b_ = (qo2, ko2) if orient == "S" else (ko2, qo2)
                        s1, s2 = (qP, kP) if orient == "S" else (kP, qP)
                        lastmm[0] = p.matmul(
                            ps[ro:ro + w, bank, 128 * bi + ro:128 * bi + ro + w],
                            s1[pr0:pr0 + 64, a_:a_ + w],
                            s2[pr0:pr0 + 64, b_:b_ + w],
                            start=False, stop=(lastb and si == 1),
                            skip_group_check=True)
            return lastmm[0]

        abank_free = {0: 0, 1: 0}
        bbank_free = {0: 0, 1: 0}
        ybank_free = {0: 0, 1: 0}
        am_free = {0: 0, 1: 0}
        sv_attn_all = 0

        @blk.tensor
        def _(p):
            p.wait_ge(sw.sem, sw_marks["all"])
            for bk in range(2):
                p.matmul(ps_y[:, bk, 0:512], zeros2[:], brow_b[:, 0:512],
                         start=True, stop=True, skip_group_check=True)

        N2B = NPAIR
        ss_qk_d, st_vpn_d = {}, {}
        sv_poolq_d, sv_poolk_d = {}, {}

        def emit_qk_group(pair, gi):
            feat0 = (128 * pair) if gi < 4 else (D + 128 * pair)
            n4 = gi % 4
            bank = gi % 2
            dstbuf, qscale = (qT_p, 0.125) if gi < 4 else (kT_p, None)

            @blk.tensor
            def _(p, pair=pair, feat0=feat0, n4=n4, bank=bank, gi=gi):
                if pair == 0 and gi == 0:
                    p.wait_ge(st.sem, st_p1_done)
                p.wait_ge(ss.sem, gbank_free[bank])
                for c in range(NC):
                    mm = p.matmul(ps_g[:, bank, :],
                             wqkvT[:, c, feat0:feat0 + 128],
                             xnT[:, c, 512 * n4:512 * (n4 + 1)],
                             start=(c == 0), stop=(c == NC - 1))
                sp.inc(mm)

            @blk.scalar
            def _(s, dstbuf=dstbuf, qscale=qscale, n4=n4, bank=bank,
                  mark=sp.n):
                s.wait_ge(sp.sem, mark)
                if qscale is None:
                    ss.inc(s.copy(out=dstbuf[:, 512 * n4:512 * (n4 + 1)],
                           in_=ps_g[:, bank, :]))
                else:
                    ss.inc(s.activation(out=dstbuf[:, 512 * n4:512 * (n4 + 1)],
                                 in_=ps_g[:, bank, :], func=ACT.Copy,
                                 bias=0.0, scale=qscale))
            gbank_free[bank] = ss.n
            if gi == 7:
                ss_qk_d[pair] = ss.n

        def _pool_one(v, srcbuf, dstbuf, pair):
            src0 = vT[:, pair, :] if srcbuf is None else srcbuf[:]
            rv = src0.rearrange("p (bp r) -> p bp r", r=64)
            v.tensor_tensor(
                out=dstbuf[:, _EOFF[1]:_EOFF[1] + 512].rearrange(
                    "p (a w) -> p a w", w=16),
                in0=rv[:, :, 0:31:2], in1=rv[:, :, 1:32:2], op=ALU.add)
            v.tensor_tensor(
                out=dstbuf[:, _OOFF[1]:_OOFF[1] + 512].rearrange(
                    "p (a w) -> p a w", w=16),
                in0=rv[:, :, 32:63:2], in1=rv[:, :, 33:64:2], op=ALU.add)
            for l in range(2, 7):
                P = (2048 >> l) // 32
                for (dof, sof) in [(_EOFF[l], _EOFF[l - 1]),
                                   (_EOFF[l] + 8, _OOFF[l - 1]),
                                   (_OOFF[l], _EOFF[l - 1] + 16),
                                   (_OOFF[l] + 8, _OOFF[l - 1] + 16)]:
                    o = dstbuf[:, dof:dof + 16 * P].rearrange(
                        "p (g u) -> p g u", u=16)[:, :, 0:8]
                    i0 = dstbuf[:, sof:sof + 32 * P].rearrange(
                        "p (g u) -> p g u", u=32)[:, :, 0:15:2]
                    i1 = dstbuf[:, sof + 1:sof + 1 + 32 * P].rearrange(
                        "p (g u) -> p g u", u=32)[:, :, 0:15:2]
                    last = v.tensor_tensor(out=o, in0=i0, in1=i1, op=ALU.add)
            return last

        def emit_pool(pair):
            @blk.vector
            def _(v, pair=pair, m=ss_qk_d[pair], stv=st_v_last):
                v.wait_ge(ss.sem, m)
                if pair == 0:
                    v.wait_ge(st.sem, stv)
                sv.inc(_pool_one(v, None, vP, pair))

            @blk.sync
            def _(s, mark=sv.n):
                s.wait_ge(sv.sem, mark)
                st.inc(s.dma_start_transpose(out=vPn[:], in_=vP[:]), 16)
            st_vpn_d[pair] = st.n

            @blk.vector
            def _(v, pair=pair):
                sv.inc(_pool_one(v, qT_p, qP, pair))
            sv_poolq_d[pair] = sv.n

            @blk.vector
            def _(v, pair=pair):
                sv.inc(_pool_one(v, kT_p, kP, pair))
            sv_poolk_d[pair] = sv.n

        def emit_S(b, hh, pair):
            bank = b % 2

            @blk.tensor
            def _(p, b=b, hh=hh, bank=bank, pair=pair):
                if b == 0 and hh == 0:
                    p.wait_ge(ss.sem, ss_qk_d[pair])
                if b == 4 and hh == 0:
                    p.wait_ge(sv.sem, sv_poolq_d[pair])
                    p.wait_ge(sv.sem, sv_poolk_d[pair])
                p.wait_ge(sv.sem, abank_free[bank])
                sp.inc(_emit_grams(p, b, hh, "S"))

            @blk.vector
            def _(v, b=b, hh=hh, bank=bank, mark=sp.n):
                v.wait_ge(sp.sem, mark)
                pv = ps_a[:, bank, :]
                if b < 4:
                    v.scalar_tensor_tensor(
                        out=smS[:].rearrange("p (n f) -> p n f", f=128),
                        in0=pv.rearrange("p (n f) -> p n f", f=128),
                        scalar=1.0, op0=ALU.mult, op1=ALU.add,
                        in1=msk_s[:, 0:1, :].broadcast_to((128, 4, 128)))
                elif b == 4:
                    v.scalar_tensor_tensor(
                        out=smS[:].rearrange("p (n f) -> p n f", f=128),
                        in0=pv.rearrange("p (n f) -> p n f", f=128),
                        scalar=0.25, op0=ALU.mult, op1=ALU.add,
                        in1=msk_s[:, 1:2, :].broadcast_to((128, 4, 128)))
                else:
                    v.scalar_tensor_tensor(
                        out=smS[:, 0:256].rearrange("p (n f) -> p n f", f=128),
                        in0=pv[:, 0:256].rearrange("p (n f) -> p n f", f=128),
                        scalar=0.0625, op0=ALU.mult, op1=ALU.add,
                        in1=msk_s[:, 1:2, :].broadcast_to((128, 2, 128)))
                    v.scalar_tensor_tensor(
                        out=smS[:, 256:384], in0=pv[:, 256:384],
                        scalar=4.0 ** -3, op0=ALU.mult, op1=ALU.add,
                        in1=msk_s[:, 1, :])
                    v.scalar_tensor_tensor(
                        out=smS[:, 384:512], in0=pv[:, 384:512],
                        scalar=scalecol[:], op0=ALU.mult, op1=ALU.add,
                        in1=msk_s[:, 1, :])
                sv.inc(v.tensor_reduce(
                    out=mbuf[:, hh, 4 * b:4 * (b + 1)],
                    in_=smS[:].rearrange("p (n f) -> p n f", f=128),
                    axis=AX, op=ALU.max))
            abank_free[bank] = sv.n

        def emit_embuf(hh):
            @blk.scalar
            def _(s, hh=hh, sv_max=sv.n):
                s.wait_ge(sv.sem, sv_max)
                ss.inc(s.activation(out=embuf[:, hh, :], in_=mbuf[:, hh, :],
                                    func=ACT.Exp, bias=0.0, scale=-1.0))
            return ss.n

        sv_am_l = {}

        def emit_st(b, hh):
            bank = b % 2

            @blk.tensor
            def _(p, b=b, hh=hh, bank=bank):
                p.wait_ge(ss.sem, bbank_free[bank])
                sp.inc(_emit_grams(p, b, hh, "T"))

            @blk.scalar
            def _(s, b=b, bank=bank, mark=sp.n, amf=am_free[bank]):
                s.wait_ge(sp.sem, mark)
                s.wait_ge(sp.sem, amf)
                pb = ps_b[:, bank, :]
                ab = am[:, bank, :]
                if b < 4:
                    ei = s.activation(out=ab, in_=pb, func=ACT.Exp, bias=0.0,
                                      scale=1.0)
                elif b == 4:
                    ei = s.activation(out=ab, in_=pb, func=ACT.Exp, bias=0.0,
                                      scale=0.25)
                else:
                    s.activation(out=ab[:, 0:256], in_=pb[:, 0:256],
                                 func=ACT.Exp, bias=0.0, scale=0.0625)
                    s.activation(out=ab[:, 256:384], in_=pb[:, 256:384],
                                 func=ACT.Exp, bias=0.0, scale=4.0 ** -3)
                    s.activation(out=ab[:, 384:448], in_=pb[:, 384:448],
                                 func=ACT.Exp, bias=0.0, scale=4.0 ** -4)
                    s.activation(out=ab[:, 448:480], in_=pb[:, 448:480],
                                 func=ACT.Exp, bias=0.0, scale=4.0 ** -5)
                    ei = s.activation(out=ab[:, 480:512], in_=pb[:, 480:512],
                                      func=ACT.Exp, bias=0.0, scale=4.0 ** -6)
                ss.inc(ei)
            bbank_free[bank] = ss.n

            @blk.vector
            def _(v, b=b, bank=bank, mark=ss.n):
                v.wait_ge(ss.sem, mark)
                ab3 = am[:, bank, :].rearrange("p (n f) -> p n f", f=128)
                if b < 4:
                    mi = v.tensor_tensor(out=ab3, in0=ab3,
                                    in1=msk_t[:, 0:1, :].broadcast_to(
                                        (128, 4, 128)), op=ALU.mult)
                elif b == 4:
                    mi = v.tensor_tensor(out=ab3, in0=ab3,
                                    in1=msk_t[:, 1:2, :].broadcast_to(
                                        (128, 4, 128)), op=ALU.mult)
                else:
                    v.tensor_tensor(out=ab3[:, 0:3, :], in0=ab3[:, 0:3, :],
                                    in1=msk_t[:, 1:2, :].broadcast_to(
                                        (128, 3, 128)), op=ALU.mult)
                    mi = v.tensor_tensor(out=ab3[:, 3, :], in0=ab3[:, 3, :],
                                         in1=msk_t[:, 2, :], op=ALU.mult)
                sv.inc(mi)
            sv_am_l[b] = sv.n

        def emit_pv(b, hh, pair):
            bank = b % 2
            sv_am = sv_am_l[b]

            @blk.tensor
            def _(p, pair=pair, hh=hh, b=b, bank=bank, sv_am=sv_am):
                p.wait_ge(sv.sem, sv_am)
                p.wait_ge(sv.sem, ybank_free[bank])
                if b == 4 and hh == 0:
                    p.wait_ge(st.sem, st_vpn_d[pair])
                h = 2 * pair + hh
                lastpv = None
                for bi in range(4):
                    ai = 4 * b + bi
                    kind = GRAM[ai][0]
                    amat = am[:, bank, 128 * bi:128 * (bi + 1)]
                    yc = 66 * bi
                    if kind == "L0":
                        rhs = v_sb[:, ai, 64 * h:64 * h + 64]
                        lastpv = p.matmul(ps_y[:, bank, yc:yc + 64], amat, rhs,
                                 start=True, stop=True,
                                 skip_group_check=True)
                        lastpv = p.matmul(ps_y[:, bank, yc + 64:yc + 65], amat,
                                 onescol[:], start=True,
                                 stop=True, skip_group_check=True)
                    elif kind != "CB":
                        mt = {16: 0, 17: 1, 18: 2, 19: 3,
                              20: 8, 21: 9, 22: 12}[ai]
                        rhs = vPn[:, mt, 64 * hh:64 * hh + 64]
                        lastpv = p.matmul(ps_y[:, bank, yc:yc + 64], amat, rhs,
                                 start=True, stop=True,
                                 skip_group_check=True)
                        lastpv = p.matmul(ps_y[:, bank, yc + 64:yc + 65], amat,
                                 onescol[:], start=True,
                                 stop=True, skip_group_check=True)
                    else:
                        pieces = [(0, 64, 14), (64, 48, 15)]
                        for (r0, w, mt) in pieces:
                            asl = am[r0:r0 + w, bank,
                                     128 * bi + r0:128 * bi + r0 + w]
                            p.matmul(ps_y[r0:r0 + w, bank, yc:yc + 64],
                                     asl,
                                     vPn[r0:r0 + w, mt,
                                         64 * hh:64 * hh + 64],
                                     start=True, stop=True,
                                     skip_group_check=True)
                            lastpv = p.matmul(
                                ps_y[r0:r0 + w, bank, yc + 64:yc + 65],
                                asl, onescol[r0:r0 + w, :], start=True,
                                stop=True, skip_group_check=True)
                sp.inc(lastpv)
            am_free[bank] = sp.n

        def emit_scale(b, hh, pair, ss_em):
            bank = b % 2

            @blk.vector
            def _(v, pair=pair, hh=hh, b=b, bank=bank, mark=sp.n,
                  ss_em=ss_em):
                v.wait_ge(sp.sem, mark)
                v.wait_ge(ss.sem, ss_em)
                src = ps_y[:, bank, 0:264].rearrange("p (n f) -> p n f", f=66)[:, :, 0:65]
                if b < 4:
                    dst = y0[:, hh, 4 * b:4 * (b + 1), :]
                elif b == 4:
                    dst = stg[:, hh, 0:4, :]
                else:
                    dst = stg[:, hh, 4:8, :]
                sv.inc(v.scalar_tensor_tensor(
                    out=dst, in0=src, scalar=1.0, op0=ALU.mult,
                    op1=ALU.mult,
                    in1=embuf[:, hh, 4 * b:4 * (b + 1)].unsqueeze(2)
                        .broadcast_to((128, 4, 65))))
            ybank_free[bank] = sv.n

        PIDX = {1: 0, 2: 4, 3: 12, 4: 20, "C56": 28}
        GWAVES = [list(range(0, 7)), list(range(7, 14)), [14, 15]]

        def emit_gather(wi, hh, pair, sv_stage):
            wave = GWAVES[wi]
            bank = wi % 2

            @blk.tensor
            def _(p, hh=hh, wave=wave, bank=bank, sv_stage=sv_stage):
                p.wait_ge(sv.sem, sv_stage)
                p.wait_ge(sv.sem, ybank_free[bank])
                pd = ps_y[:, bank, 0:462].rearrange("p (n f) -> p n f", f=66)
                mms = []
                for ti, tt in enumerate(wave):
                    mms.append((pd[:, ti, 0:65], ident[:], y0[:, hh, tt, :]))
                for l in range(1, 5):
                    for ti, tt in enumerate(wave):
                        if l == 1:
                            var, acol = tt % 4, tt // 4
                        elif l == 2:
                            var, acol = tt % 8, 4 + tt // 8
                        elif l == 3:
                            if tt % 2 == 0:
                                continue
                            var, acol = (tt - 1) // 2, 6
                        else:
                            if tt % 4 < 2:
                                continue
                            var = 2 * (tt // 4) + (tt % 4 - 2)
                            acol = 7
                        mms.append((pd[:, ti, 0:65],
                                    perm[:, PIDX[l] + var, :],
                                    stg[:, hh, acol, :]))
                for ti, tt in enumerate(wave):
                    if tt < 4:
                        continue
                    mms.append((pd[:, ti, 0:65],
                                perm[:, PIDX["C56"] + (tt - 4), :],
                                stg[:, hh, 7, :]))
                for mi, (o_, lt, rh) in enumerate(mms):
                    gi = p.matmul(o_, lt, rh, start=(mi == 0),
                                  stop=(mi == len(mms) - 1),
                                  skip_group_check=True)
                sp.inc(gi)

            @blk.vector
            def _(v, pair=pair, hh=hh, wave=wave, bank=bank, mark=sp.n):
                v.wait_ge(sp.sem, mark)
                nw = len(wave)
                t0 = wave[0]
                pd = ps_y[:, bank, 0:462].rearrange("p (n f) -> p n f", f=66)
                v.tensor_scalar(out=avb[:, t0:t0 + nw],
                                in0=pd[:, 0:nw, 64], scalar1=1e-8,
                                scalar2=None, op0=ALU.add)
                v.drain()
                v.reciprocal(out=rinv[:, t0:t0 + nw], in_=avb[:, t0:t0 + nw])
                v.drain()
                if wave[0] == 0 and (pair + hh) > 0:
                    v.wait_ge(sa.sem, 16 * (2 * pair + hh))
                sv.inc(v.scalar_tensor_tensor(
                    out=attn_hd[:, t0:t0 + nw, :],
                    in0=pd[:, 0:nw, 0:64], scalar=1.0,
                    op0=ALU.mult, op1=ALU.mult,
                    in1=rinv[:, t0:t0 + nw].unsqueeze(2)
                        .broadcast_to((128, nw, 64))))
            ybank_free[bank] = sv.n

        def emit_attn_dma(pair, hh):
            @blk.gpsimd
            def _(g, pair=pair, hh=hh, mark=sv.n):
                g.wait_ge(sv.sem, mark)
                h = 2 * pair + hh
                sa.inc(g.dma_start(
                    out=attn_sp.rearrange("(t p) d -> p t d", p=128)[:, :, 64 * h:64 * h + 64],
                    in_=attn_hd[:]), 16)

        for pair in range(N2B):
            if pair == 0:
                for gi in range(8):
                    emit_qk_group(0, gi)
                emit_pool(0)
            for b in range(6):
                emit_S(b, 0, pair)
            ss_em0 = emit_embuf(0)
            for k in range(7):
                if k < 6:
                    emit_st(k, 0)
                if 1 <= k <= 4:
                    emit_S(k - 1, 1, pair)
                if k >= 1:
                    emit_pv(k - 1, 0, pair)
                    emit_scale(k - 1, 0, pair, ss_em0)
            sv_stage = sv.n
            emit_gather(0, 0, pair, sv_stage)
            emit_S(4, 1, pair)
            emit_gather(1, 0, pair, sv_stage)
            emit_S(5, 1, pair)
            emit_gather(2, 0, pair, sv_stage)
            emit_attn_dma(pair, 0)
            ss_em1 = emit_embuf(1)
            nxt = pair + 1 if pair + 1 < N2B else None
            for k in range(7):
                if k < 6:
                    emit_st(k, 1)
                if k >= 1:
                    emit_pv(k - 1, 1, pair)
                    emit_scale(k - 1, 1, pair, ss_em1)
                if nxt is not None and 4 <= k <= 6:
                    emit_qk_group(nxt, k - 4)
            sv_stage = sv.n
            emit_gather(0, 1, pair, sv_stage)
            if nxt is not None:
                emit_qk_group(nxt, 3)
                emit_qk_group(nxt, 4)
            emit_gather(1, 1, pair, sv_stage)
            if nxt is not None:
                emit_qk_group(nxt, 5)
                emit_qk_group(nxt, 6)
            emit_gather(2, 1, pair, sv_stage)
            emit_attn_dma(pair, 1)
            if nxt is not None:
                emit_qk_group(nxt, 7)
                emit_pool(nxt)

        if xn_dbg is not None:
            @blk.gpsimd
            def _(g, mark=sv.n):
                g.wait_ge(sv.sem, mark)
                sa.inc(g.dma_start(out=qp_dbg, in_=qP[:]), 16)
                sa.inc(g.dma_start(out=kp_dbg, in_=kP[:]), 16)
                sa.inc(g.dma_start(out=vp_dbg, in_=vP[:]), 16)
                sa.inc(g.dma_start(out=stg_dbg, in_=stg[:].rearrange("p a b c -> p (a b c)")), 16)
                sa.inc(g.dma_start(out=y0_dbg, in_=y0[:].rearrange("p a b c -> p (a b c)")), 16)
                sa.inc(g.dma_start(out=em_dbg, in_=embuf[:].rearrange("p a b -> p (a b)")), 16)
                sa.inc(g.dma_start(out=em2_dbg, in_=em2[:]), 16)

        sv_attn_all = sv.n
        if phase_limit <= 3:
            @blk.gpsimd
            def _(g, a=st.n, b=sv.n, c=ss.n, d=sp.n, e=sa.n):
                g.wait_ge(st.sem, a)
                g.wait_ge(sv.sem, b)
                g.wait_ge(ss.sem, c)
                g.wait_ge(sp.sem, d)
                g.wait_ge(sa.sem, e)
                sa.inc(g.dma_start(out=out_h[0:128, 0:24], in_=mbuf[:].rearrange("p a b -> p (a b)")[:, 0:24]), 16)
                g.wait_ge(sa.sem, e + 16)
            return nc

        ctx3.close()
        ctx2.close()
        ctx4 = contextlib.ExitStack()
        ctx.enter_context(ctx4)
        E4 = ctx4.enter_context
        xdT = E4(nc.sbuf_tensor("xdT", [128, NC, T], BF16))
        attnT = E4(nc.sbuf_tensor("attnT", [128, NC, T], BF16))
        wdT = E4(nc.sbuf_tensor("wdT", [128, NC, D], BF16))
        woutT = E4(nc.sbuf_tensor("woutT_s", [128, NC, D], BF16))
        x_r = E4(nc.sbuf_tensor("x_r", [128, 2, D], F32))
        out_sb = E4(nc.sbuf_tensor("out_sb", [128, 2, D], F32))

        obank_free = {0: 0, 1: 0}
        xr_free = {0: 0, 1: 0}
        ob_free = {0: 0, 1: 0}
        @blk.gpsimd
        def _(g, mark=sv_attn_all):
            g.wait_ge(sv.sem, mark)
            for c_ in range(NC):
                sw.inc(g.dma_start(out=woutT[:, c_, :],
                                   in_=woutT_h[128 * c_:128 * (c_ + 1), :]), 16)
        sw_marks["woutT"] = sw.n

        @blk.sync
        def _(s, san=sa.n):
            s.wait_ge(sa.sem, san)
            for wt in range(NC):
                st.inc(s.dma_start_transpose(out=wdT[:, :, 128 * wt:128 * (wt + 1)],
                                             in_=wd_sp[128 * wt:128 * (wt + 1), :]), 16)
        st_wd_done = st.n

        at_mark = []
        for t in range(NT):
            @blk.sync
            def _(s, t=t):
                st.inc(s.dma_start_transpose(out=attnT[:, :, 128 * t:128 * (t + 1)],
                                             in_=attn_sp[128 * t:128 * (t + 1), :]), 16)
                st.inc(s.dma_start_transpose(out=xdT[:, :, 128 * t:128 * (t + 1)],
                                             in_=xd_sp[128 * t:128 * (t + 1), :]), 16)
                if t >= 2:
                    s.wait_ge(sv.sem, xr_free[t % 2])
                sx.inc(s.dma_start(out=x_r[:, t % 2, :],
                                   in_=x_h[128 * t:128 * (t + 1), :]), 16)
            at_mark.append((st.n, sx.n))

            for n2 in range(2):
                bank = (2 * t + n2) % 2

                @blk.tensor
                def _(p, t=t, n2=n2, bank=bank):
                    p.wait_ge(st.sem, at_mark[t][0])
                    if t == 0 and n2 == 0:
                        p.wait_ge(st.sem, st_wd_done)
                        p.wait_ge(sw.sem, sw_marks["woutT"])
                        p.wait_ge(ss.sem, gbank_free[0])
                        p.wait_ge(ss.sem, gbank_free[1])
                    p.wait_ge(sv.sem, obank_free[bank])
                    p.matmul(ps_g[:, bank, :], ones2[:],
                             brow_b[:, 512 * n2:512 * (n2 + 1)],
                             start=True, stop=False)
                    for c in range(NC):
                        p.matmul(ps_g[:, bank, :], attnT[:, c, 128 * t:128 * (t + 1)],
                                 woutT[:, c, 512 * n2:512 * (n2 + 1)],
                                 start=False, stop=False)
                    for c in range(NC):
                        mm = p.matmul(ps_g[:, bank, :], xdT[:, c, 128 * t:128 * (t + 1)],
                                 wdT[:, c, 512 * n2:512 * (n2 + 1)],
                                 start=False, stop=(c == NC - 1))
                    sp.inc(mm)

                @blk.vector
                def _(v, t=t, n2=n2, bank=bank, mark=sp.n):
                    v.wait_ge(sp.sem, mark)
                    v.wait_ge(sx.sem, at_mark[t][1])
                    if t >= 2 and n2 == 0:
                        v.wait_ge(so.sem, ob_free[t % 2])
                    sv.inc(v.tensor_tensor(
                        out=out_sb[:, t % 2, 512 * n2:512 * (n2 + 1)],
                        in0=x_r[:, t % 2, 512 * n2:512 * (n2 + 1)],
                        in1=ps_g[:, bank, :], op=ALU.add))
                obank_free[bank] = sv.n
            xr_free[t % 2] = sv.n

            @blk.sync
            def _(s, t=t, mark=sv.n):
                s.wait_ge(sv.sem, mark)
                so.inc(s.dma_start(out=out_h[128 * t:128 * (t + 1), :],
                                   in_=out_sb[:, t % 2, :]), 16)
            ob_free[t % 2] = so.n

        @blk.gpsimd
        def _(g):
            g.wait_ge(so.sem, so.n)

    return nc


_NC_CACHE = {}


def _get_nc():
    import os
    pl = int(os.environ.get("KPHASE", "99"))
    key = ("nc", pl)
    if key not in _NC_CACHE:
        _NC_CACHE[key] = build_nc(pl)
    return _NC_CACHE[key]


def kernel(L0_attn_x, L0_attn_w_qkv, L0_attn_w_out, L0_attn_b_out,
           L0_ffn_weight, L0_ffn_bias):
    x = np.ascontiguousarray(np.asarray(L0_attn_x, dtype=np.float32))
    wqkvT = np.ascontiguousarray(np.asarray(L0_attn_w_qkv, dtype=np.float32).T)
    woutT = np.ascontiguousarray(np.asarray(L0_attn_w_out, dtype=np.float32).T)
    wffn = np.ascontiguousarray(np.asarray(L0_ffn_weight, dtype=np.float32))
    brow = np.ascontiguousarray(
        np.stack([np.asarray(L0_attn_b_out, dtype=np.float32),
                  np.asarray(L0_ffn_bias, dtype=np.float32)], axis=0))
    hc = _host_consts()
    base = {"wqkvT": wqkvT, "woutT": woutT, "wffn": wffn, "brow": brow}
    for k, v in hc.items():
        base["c_" + k] = np.ascontiguousarray(v)
    in_maps = []
    for c in range(8):
        m = dict(base)
        m["x"] = np.ascontiguousarray(x[c])
        in_maps.append(m)
    nc = _get_nc()
    res = run_bass_kernel_spmd(nc, in_maps, core_ids=list(range(8)))
    out = np.stack([np.asarray(r["out"]) for r in res.results], axis=0)
    return out.astype(np.float32)


# version 06
# speedup vs baseline: 1.2288x; 1.0574x over previous
import contextlib

import numpy as np
import ml_dtypes

import concourse.bass as bass
import concourse.mybir as mybir
from concourse.bass_utils import run_bass_kernel_spmd

F32 = mybir.dt.float32
BF16 = mybir.dt.bfloat16
FP8 = mybir.dt.float8e4
AX = mybir.AxisListType.X
ALU = mybir.AluOpType
ACT = mybir.ActivationFunctionType

T = 2048
D = 1024
NT = T // 128
NC = D // 128
NPAIR = 8
MAGIC8 = 1572864.0
NEG = -1.0e30

_EOFF, _OOFF = {}, {}
_o = 0
for _l in range(1, 5):
    n = 2048 >> _l
    _EOFF[_l] = _o
    _o += n // 2
    _OOFF[_l] = _o
    _o += n // 2
_OOFF[5], _OOFF[6], _EOFF[5], _EOFF[6] = 1920, 1952, 1984, 2016

_RL = {1: 32, 2: 16, 3: 16, 4: 8, 5: 4, 6: 2}


def _mask_s(causal, mask_pad):
    m = np.full((128, 128), NEG, np.float32)
    for i in range(128):
        b = i // 16
        if mask_pad and b == 7:
            continue
        for j in range(16 * b, 16 * b + 16):
            if causal and j % 16 > i % 16:
                continue
            m[i, j] = 0.0
    return m


def _mask_t01(causal, mask_pad):
    m = np.zeros((128, 128), np.float32)
    for j in range(128):
        b = j // 16
        if mask_pad and b == 7:
            continue
        for i in range(16 * b, 16 * b + 16):
            if causal and j % 16 > i % 16:
                continue
            m[j, i] = 1.0
    return m.astype(ml_dtypes.bfloat16)


_NVAR = {1: 4, 2: 8, 3: 8, 4: 8}


def _perm_names():
    names = []
    for l in range(1, 5):
        for v in range(_NVAR[l]):
            names.append(f"P_L{l}_{v}")
    for v in range(12):
        names.append(f"P_C56_{v}")
    return names


def _host_consts():
    c = {}
    c["mask_s0"] = _mask_s(True, False)
    c["mask_sb"] = _mask_s(False, False)
    c["mask_sc"] = _mask_s(False, True)
    c["mask_t0"] = _mask_t01(True, False)
    c["mask_tb"] = _mask_t01(False, False)
    c["mask_tc"] = _mask_t01(False, True)
    c["ident"] = np.eye(128, dtype=np.float32).astype(ml_dtypes.bfloat16)
    for l in range(1, 5):
        r = _RL[l]
        for v in range(_NVAR[l]):
            m = np.zeros((128, 128), np.float32)
            for j in range(r):
                row = v * r + j
                w = row % 16
                m[row, ((17 * (1 << l) - 1) + (1 << l) * w
                        + (1 << (l + 5)) * (row // 16)) % 128] = 1.0
            c[f"P_L{l}_{v}"] = m.astype(ml_dtypes.bfloat16)
    for v in range(12):
        tt = v + 4
        m = np.zeros((128, 128), np.float32)
        if (tt // 4) % 2 == 1:
            pp, qq = tt // 8, tt % 4
            for j in range(4):
                j5 = 16 * pp + 4 * qq + j
                m[64 + j5, (543 + 32 * j5) % 128] = 1.0
        if tt >= 8:
            for j in range(2):
                j6 = 2 * (tt - 8) + j
                m[96 + j6, (1087 + 64 * j6) % 128] = 1.0
        c[f"P_C56_{v}"] = m.astype(ml_dtypes.bfloat16)
    c["ones2"] = np.ones((2, 128), np.float32).astype(ml_dtypes.bfloat16)
    c["zeros2"] = np.zeros((2, 128), np.float32).astype(ml_dtypes.bfloat16)
    c["onescol"] = np.ones((128, 1), np.float32).astype(ml_dtypes.bfloat16)
    sc = np.zeros((128, 1), np.float32)
    sc[0:64] = 4.0 ** -4
    sc[64:96] = 4.0 ** -5
    sc[96:128] = 4.0 ** -6
    c["scalecol"] = sc
    return c


class Ctr:
    def __init__(self, sem):
        self.sem = sem
        self.n = 0

    def inc(self, inst, k=1):
        inst.then_inc(self.sem, k)
        self.n += k
        return self.n


def build_nc(phase_limit=99):
    nc = bass.Bass("TRN2", target_bir_lowering=False)
    x_h = nc.dram_tensor("x", [T, D], F32, kind="ExternalInput").ap()
    wqkvT_h = nc.dram_tensor("wqkvT", [D, 3 * D], F32, kind="ExternalInput").ap()
    woutT_h = nc.dram_tensor("woutT", [D, D], F32, kind="ExternalInput").ap()
    wffn_h = nc.dram_tensor("wffn", [D, D], F32, kind="ExternalInput").ap()
    brow_h = nc.dram_tensor("brow", [2, D], F32, kind="ExternalInput").ap()
    hc = _host_consts()
    cst = {}
    for name, arr in hc.items():
        dt = F32 if arr.dtype == np.float32 else BF16
        cst[name] = nc.dram_tensor("c_" + name, list(arr.shape), dt,
                                   kind="ExternalInput").ap()
    import os as _os0
    _dbg = int(_os0.environ.get("KDBG", "0"))
    _kind = "ExternalOutput" if _dbg else "Internal"
    out_h = nc.dram_tensor("out", [T, D], F32, kind="ExternalOutput").ap()
    xd_sp = nc.dram_tensor("xd_spill", [T, D], BF16, kind=_kind).ap()
    wd_sp = nc.dram_tensor("wd_spill", [D, D], BF16, kind=_kind).ap()
    attn_sp = nc.dram_tensor("attn_spill", [T, D], BF16, kind=_kind).ap()
    xn_dbg = nc.dram_tensor("xn_dbg", [T, D], BF16, kind=_kind).ap() if _dbg else None
    qp_dbg = nc.dram_tensor("qp_dbg", [128, 2048], BF16, kind=_kind).ap() if _dbg else None
    kp_dbg = nc.dram_tensor("kp_dbg", [128, 2048], BF16, kind=_kind).ap() if _dbg else None
    vp_dbg = nc.dram_tensor("vp_dbg", [128, 2048], BF16, kind=_kind).ap() if _dbg else None
    stg_dbg = nc.dram_tensor("stg_dbg", [128, 2 * 8 * 65], BF16, kind=_kind).ap() if _dbg else None
    y0_dbg = nc.dram_tensor("y0_dbg", [128, 2 * 16 * 65], BF16, kind=_kind).ap() if _dbg else None
    em_dbg = nc.dram_tensor("em_dbg", [128, 48], F32, kind=_kind).ap() if _dbg else None
    em2_dbg = nc.dram_tensor("em2_dbg", [128, 160], F32, kind=_kind).ap() if _dbg else None

    ctx = contextlib.ExitStack()
    with ctx:
        E = ctx.enter_context

        brow_b = E(nc.sbuf_tensor("brow_b", [2, D], BF16))
        msk_s = E(nc.sbuf_tensor("msk_s", [128, 3, 128], F32))
        msk_t = E(nc.sbuf_tensor("msk_t", [128, 3, 128], BF16))
        ident = E(nc.sbuf_tensor("ident", [128, 128], BF16))
        nperm = len(_perm_names())
        perm = E(nc.sbuf_tensor("perm", [128, nperm, 128], BF16))
        ones2 = E(nc.sbuf_tensor("ones2", [2, 128], BF16))
        zeros2 = E(nc.sbuf_tensor("zeros2", [2, 128], BF16))
        onescol = E(nc.sbuf_tensor("onescol", [128, 1], BF16))
        scalecol = E(nc.sbuf_tensor("scalecol", [128, 1], F32))
        _ACT_CONSTS = [1e-5, -1.0, -0.5, 0.125, 0.25, 0.0625,
                       4.0 ** -3, 4.0 ** -4, 4.0 ** -5, 4.0 ** -6]
        actc = E(nc.sbuf_tensor("actc", [128, len(_ACT_CONSTS)], F32))
        for i, val in enumerate(_ACT_CONSTS):
            nc.const_aps.register(F32, val, actc[:, i:i + 1])

        ps_g = E(nc.psum_tensor("ps_g", [128, 2, 512], F32))
        ps_a = E(nc.psum_tensor("ps_a", [128, 2, 512], F32))
        ps_b = E(nc.psum_tensor("ps_b", [128, 2, 512], F32))
        ps_y = E(nc.psum_tensor("ps_y", [128, 2, 512], F32))

        ctx2 = contextlib.ExitStack()
        ctx.enter_context(ctx2)
        E2 = ctx2.enter_context
        xnT = E2(nc.sbuf_tensor("xnT", [128, NC, T], BF16))
        wqkvT = E2(nc.sbuf_tensor("wqkvT_s", [128, NC, 3 * D], BF16))
        v_sb = E2(nc.sbuf_tensor("v_sb", [128, NT, D], BF16))
        vT = E2(nc.sbuf_tensor("vT", [128, NPAIR, T], BF16))

        ctx1 = contextlib.ExitStack()
        ctx.enter_context(ctx1)
        E1 = ctx1.enter_context
        x_t = E1(nc.sbuf_tensor("x_t", [128, 2, D], F32))
        wffn_t = E1(nc.sbuf_tensor("wffn_t", [128, 1, D], F32))
        wd_t = E1(nc.sbuf_tensor("wd_t", [128, 2, D], BF16))
        xn_f = E1(nc.sbuf_tensor("xn_f", [128, 2, D], F32))
        xnb_t = E1(nc.sbuf_tensor("xnb_t", [128, 2, D], BF16))
        
        q_g = E1(nc.sbuf_tensor("q_g", [128, 32], F32))
        q_inv = E1(nc.sbuf_tensor("q_inv", [128, 32], F32))
        q_s = E1(nc.sbuf_tensor("q_s", [128, 32], F32))
        q_u = E1(nc.sbuf_tensor("q_u", [128, D], F32))
        q_mg = E1(nc.sbuf_tensor("q_mg", [128, D], F32))
        q_h8 = E1(nc.sbuf_tensor("q_h8", [128, D], FP8))
        q_l8 = E1(nc.sbuf_tensor("q_l8", [128, D], FP8))
        q_qt = E1(nc.sbuf_tensor("q_qt", [128, D], F32))
        xd_t = E1(nc.sbuf_tensor("xd_t", [128, 2, D], BF16))
        dump = q_u
        ln_nm = E1(nc.sbuf_tensor("ln_nm", [128, 2], F32))
        ln_sx = E1(nc.sbuf_tensor("ln_sx", [128, 2], F32))
        ln_sq = E1(nc.sbuf_tensor("ln_sq", [128, 2], F32))
        ln_s2 = E1(nc.sbuf_tensor("ln_s2", [128, 2], F32))
        ln_nv = E1(nc.sbuf_tensor("ln_nv", [128, 2], F32))
        ln_rs = E1(nc.sbuf_tensor("ln_rs", [128, 2], F32))
        ln_nmr = E1(nc.sbuf_tensor("ln_nmr", [128, 2], F32))

        sx = Ctr(E(nc.semaphore("sx")))
        sw = Ctr(E(nc.semaphore("sw")))
        st = Ctr(E(nc.semaphore("st")))
        sv = Ctr(E(nc.semaphore("sv")))
        ss = Ctr(E(nc.semaphore("ss")))
        sp = Ctr(E(nc.semaphore("sp")))
        so = Ctr(E(nc.semaphore("so")))
        sa = Ctr(E(nc.semaphore("sa")))
        blk = E(nc.Block())

        @blk.vector
        def _(v):
            for i, val in enumerate(_ACT_CONSTS):
                v.memset(actc[:, i:i + 1], val)

        @blk.sync
        def _(s):
            for t in range(2):
                sx.inc(s.dma_start(out=x_t[:, t, :],
                                   in_=x_h[128 * t:128 * (t + 1), :]), 16)

        sw_marks = {}

        @blk.gpsimd
        def _(g):
            for c_ in range(NC):
                for h_ in range(2):
                    sw.inc(g.dma_start(
                        out=wqkvT[:, c_, 1536 * h_:1536 * (h_ + 1)],
                        in_=wqkvT_h[128 * c_:128 * (c_ + 1),
                                    1536 * h_:1536 * (h_ + 1)]), 16)
            sw_marks["wqkvT"] = sw.n
            sw.inc(g.dma_start(out=brow_b[:], in_=brow_h[:]), 16)
            for i, nm in enumerate(["mask_s0", "mask_sb", "mask_sc"]):
                sw.inc(g.dma_start(out=msk_s[:, i, :], in_=cst[nm]), 16)
            for i, nm in enumerate(["mask_t0", "mask_tb", "mask_tc"]):
                sw.inc(g.dma_start(out=msk_t[:, i, :], in_=cst[nm]), 16)
            sw.inc(g.dma_start(out=ident[:], in_=cst["ident"]), 16)
            for pi, nm in enumerate(_perm_names()):
                sw.inc(g.dma_start(out=perm[:, pi, :], in_=cst[nm]), 16)
            sw.inc(g.dma_start(out=ones2[:], in_=cst["ones2"]), 16)
            sw.inc(g.dma_start(out=zeros2[:], in_=cst["zeros2"]), 16)
            sw.inc(g.dma_start(out=onescol[:], in_=cst["onescol"]), 16)
            sw.inc(g.dma_start(out=scalecol[:], in_=cst["scalecol"]), 16)
            sw_marks["all"] = sw.n

        def _quant(v, src, dst):
            v.tensor_reduce(out=q_g[:], in_=src.rearrange("p (b k) -> p b k", k=32),
                            axis=AX, op=ALU.max, apply_absolute_value=True)
            v.drain()
            v.reciprocal(out=q_inv[:], in_=q_g[:])
            v.drain()
            v.tensor_scalar(out=q_inv[:], in0=q_inv[:], scalar1=7.5, scalar2=7.5e30,
                            op0=ALU.mult, op1=ALU.min)
            v.tensor_scalar(out=q_s[:], in0=q_g[:], scalar1=1.0 / 7.5, scalar2=1e-30,
                            op0=ALU.mult, op1=ALU.max)
            v.tensor_tensor(out=q_u[:].rearrange("p (b k) -> p b k", k=32),
                            in0=src.rearrange("p (b k) -> p b k", k=32),
                            in1=q_inv[:].unsqueeze(2).broadcast_to((128, 32, 32)),
                            op=ALU.mult)
            v.tensor_scalar(out=q_mg[:], in0=q_u[:], scalar1=1.0, scalar2=-1.0,
                            op0=ALU.min, op1=ALU.max)
            v.tensor_scalar(out=q_mg[:], in0=q_mg[:], scalar1=MAGIC8, scalar2=MAGIC8,
                            op0=ALU.add, op1=ALU.subtract)
            v.tensor_scalar(out=q_h8[:], in0=q_u[:], scalar1=1.0, scalar2=None,
                            op0=ALU.max)
            v.tensor_scalar(out=q_l8[:], in0=q_u[:], scalar1=-1.0, scalar2=None,
                            op0=ALU.min)
            v.scalar_tensor_tensor(out=q_qt[:], in0=q_mg[:], scalar=0.0, in1=q_h8[:],
                                   op0=ALU.add, op1=ALU.add)
            v.scalar_tensor_tensor(out=q_qt[:], in0=q_qt[:], scalar=0.0, in1=q_l8[:],
                                   op0=ALU.add, op1=ALU.add)
            return v.tensor_tensor(out=dst.rearrange("p (b k) -> p b k", k=32),
                            in0=q_qt[:].rearrange("p (b k) -> p b k", k=32),
                            in1=q_s[:].unsqueeze(2).broadcast_to((128, 32, 32)),
                            op=ALU.mult)

        st_wq = []
        xn_done = []
        sv_xnb_t = []
        sv_xd_tl = []
        ss_xnf_t = []
        sv_stats_l = []

        def emit_stats(t):
            r2 = t % 2

            @blk.vector
            def _(v, t=t, r2=r2):
                v.wait_ge(sx.sem, 16 * (t + 1))
                if t >= 2:
                    v.wait_ge(ss.sem, ss_xnf_t[t - 2])
                xt = x_t[:, r2, :]
                v.tensor_reduce(out=ln_sx[:, r2:r2 + 1], in_=xt, axis=AX, op=ALU.add)
                v.tensor_tensor(out=dump[:], in0=xt, in1=xt, op=ALU.mult)
                v.tensor_reduce(out=ln_sq[:, r2:r2 + 1], in_=dump[:], axis=AX,
                                op=ALU.add)
                v.tensor_scalar(out=ln_nm[:, r2:r2 + 1], in0=ln_sx[:, r2:r2 + 1],
                                scalar1=-1.0 / D, scalar2=None, op0=ALU.mult)
                v.tensor_scalar(out=ln_s2[:, r2:r2 + 1], in0=ln_sq[:, r2:r2 + 1],
                                scalar1=1.0 / D, scalar2=None, op0=ALU.mult)
                sv.inc(v.scalar_tensor_tensor(
                    out=ln_nv[:, r2:r2 + 1], in0=ln_nm[:, r2:r2 + 1],
                    scalar=ln_nm[:, r2:r2 + 1], in1=ln_s2[:, r2:r2 + 1],
                    op0=ALU.mult, op1=ALU.subtract))
            sv_stats_l.append(sv.n)

        emit_stats(0)
        for t in range(NT):
            r2 = t % 2

            @blk.scalar
            def _(s, t=t, r2=r2, sv_stats=sv_stats_l[t]):
                s.wait_ge(sv.sem, sv_stats)
                if t >= 2:
                    s.wait_ge(sv.sem, sv_xd_tl[t - 2])
                ss.inc(s.activation(out=ln_rs[:, r2:r2 + 1],
                                    in_=ln_nv[:, r2:r2 + 1],
                                    func=ACT.Sqrt, bias=1e-5, scale=-1.0))

            ss_xnf_t.append(ss.n)

            if t + 1 < NT:
                emit_stats(t + 1)

            @blk.vector
            def _(v, t=t, r2=r2, mark=ss_xnf_t[t]):
                v.wait_ge(ss.sem, mark)
                if t >= 2:
                    v.wait_ge(st.sem, xn_done[t - 2] + (32 if xn_dbg is not None else 16))
                v.drain()
                v.reciprocal(out=ln_nv[:, r2:r2 + 1], in_=ln_rs[:, r2:r2 + 1])
                v.drain()
                v.tensor_tensor(out=ln_nmr[:, r2:r2 + 1], in0=ln_nv[:, r2:r2 + 1],
                                in1=ln_nm[:, r2:r2 + 1], op=ALU.mult)
                sv.inc(v.tensor_scalar(out=xnb_t[:, r2, :], in0=x_t[:, r2, :],
                                scalar1=ln_nv[:, r2:r2 + 1],
                                scalar2=ln_nmr[:, r2:r2 + 1],
                                op0=ALU.mult, op1=ALU.add))
                v.tensor_scalar(out=xn_f[:, r2, :], in0=x_t[:, r2, :],
                                scalar1=ln_nv[:, r2:r2 + 1],
                                scalar2=ln_nmr[:, r2:r2 + 1],
                                op0=ALU.mult, op1=ALU.add)
                sv.inc(_quant(v, xn_f[:, r2, :], xd_t[:, r2, :]))

            sv_xnb_t.append(sv.n - 1)
            sv_xd_tl.append(sv.n)

            @blk.sync
            def _(s, t=t, r2=r2, a=sv.n - 1, b=sv.n, ssm=ss.n):
                s.wait_ge(sv.sem, a)
                st.inc(s.dma_start_transpose(out=xnT[:, :, 128 * t:128 * (t + 1)],
                                             in_=xnb_t[:, r2, :]), 16)
                s.wait_ge(sv.sem, b)
                st.inc(s.dma_start(out=xd_sp[128 * t:128 * (t + 1), :],
                                   in_=xd_t[:, r2, :]), 16)
                if xn_dbg is not None:
                    st.inc(s.dma_start(out=xn_dbg[128 * t:128 * (t + 1), :],
                                       in_=xnb_t[:, r2, :]), 16)
                if t + 2 < NT:
                    s.wait_ge(ss.sem, ssm)
                    sx.inc(s.dma_start(out=x_t[:, t % 2, :],
                                       in_=x_h[128 * (t + 2):128 * (t + 3), :]), 16)
            xn_done.append(st.n - (32 if xn_dbg is not None else 16))

        sv_wq = []
        for wt in range(NC):
            @blk.gpsimd
            def _(g, wt=wt):
                if wt >= 1:
                    g.wait_ge(sv.sem, sv_wq[wt - 1])
                sw.inc(g.dma_start(out=wffn_t[:, 0, :],
                                   in_=wffn_h[128 * wt:128 * (wt + 1), :]), 16)

            @blk.vector
            def _(v, wt=wt, mark=sw.n):
                v.wait_ge(sw.sem, mark)
                if wt >= 2:
                    v.wait_ge(st.sem, st_wq[wt - 2])
                sv.inc(_quant(v, wffn_t[:, 0, :], wd_t[:, wt % 2, :]))
            sv_wq.append(sv.n)

            @blk.sync
            def _(s, wt=wt, mark=sv.n):
                s.wait_ge(sv.sem, mark)
                st.inc(s.dma_start(out=wd_sp[128 * wt:128 * (wt + 1), :],
                                   in_=wd_t[:, wt % 2, :]), 16)
            st_wq.append(st.n)
        st_p1_done = st.n
        if phase_limit <= 1:
            @blk.gpsimd
            def _(g, a=st.n, b=sv.n, c=ss.n):
                g.wait_ge(st.sem, a)
                g.wait_ge(sv.sem, b)
                g.wait_ge(ss.sem, c)
                sa.inc(g.dma_start(out=out_h[0:128, :], in_=x_t[:, 0, :]), 16)
                g.wait_ge(sa.sem, 16)
            return nc

        import os as _os
        N2A = int(_os.environ.get("K2A", str(NT)))
        NO_VT = int(_os.environ.get("KNOVT", "0"))
        gbank_free = {0: 0, 1: 0}
        st_v_last = 0
        for t in range(N2A):
            for n2 in range(2):
                bank = (2 * t + n2) % 2

                @blk.tensor
                def _(p, t=t, n2=n2, bank=bank):
                    if t == 0 and n2 == 0:
                        p.wait_ge(sw.sem, sw_marks["wqkvT"])
                    p.wait_ge(st.sem, xn_done[t])
                    p.wait_ge(ss.sem, gbank_free[bank])
                    for c in range(NC):
                        mm = p.matmul(ps_g[:, bank, :], xnT[:, c, 128 * t:128 * (t + 1)],
                                 wqkvT[:, c, 2 * D + 512 * n2:2 * D + 512 * (n2 + 1)],
                                 start=(c == 0), stop=(c == NC - 1))
                    sp.inc(mm)

                @blk.scalar
                def _(s, t=t, n2=n2, bank=bank, mark=sp.n):
                    s.wait_ge(sp.sem, mark)
                    ss.inc(s.copy(out=v_sb[:, t, 512 * n2:512 * (n2 + 1)],
                                  in_=ps_g[:, bank, :]))
                gbank_free[bank] = ss.n

            if not NO_VT:
                @blk.sync
                def _(s, t=t, mark=ss.n):
                    s.wait_ge(ss.sem, mark)
                    st.inc(s.dma_start_transpose(out=vT[:, :, 128 * t:128 * (t + 1)],
                                                 in_=v_sb[:, t, :]), 16)
                st_v_last = st.n

        if not int(_os.environ.get("KNOCLOSE", "0")):
            ctx1.close()
        ctx3 = contextlib.ExitStack()
        ctx.enter_context(ctx3)
        E3 = ctx3.enter_context
        qT_p = E3(nc.sbuf_tensor("qT_p", [128, T], BF16))
        kT_p = E3(nc.sbuf_tensor("kT_p", [128, T], BF16))
        qP = E3(nc.sbuf_tensor("qP", [128, 2048], BF16))
        kP = E3(nc.sbuf_tensor("kP", [128, 2048], BF16))
        vP = E3(nc.sbuf_tensor("vP", [128, 2048], BF16))
        vPn = E3(nc.sbuf_tensor("vPn", [128, 16, 128], BF16))
        am = E3(nc.sbuf_tensor("am", [128, 2, 512], BF16))
        smS = E3(nc.sbuf_tensor("smS", [128, 512], F32))
        mbuf = E3(nc.sbuf_tensor("mbuf", [128, 2, 24], F32))
        embuf = E3(nc.sbuf_tensor("embuf", [128, 2, 24], F32))
        y0 = E3(nc.sbuf_tensor("y0", [128, 2, 16, 65], BF16))
        stg = E3(nc.sbuf_tensor("stg", [128, 2, 8, 65], BF16))
        avb = E3(nc.sbuf_tensor("avb", [128, 16], F32))
        rinv = E3(nc.sbuf_tensor("rinv", [128, 16], F32))
        attn_hd = E3(nc.sbuf_tensor("attn_hd", [128, 16, 64], BF16))
        em2 = E3(nc.sbuf_tensor("em2", [128, 160], F32)) if _dbg else None

        if phase_limit <= 2:
            @blk.gpsimd
            def _(g, a=st.n, b=sv.n, c=ss.n, d=sp.n):
                g.wait_ge(st.sem, a)
                g.wait_ge(sv.sem, b)
                g.wait_ge(ss.sem, c)
                g.wait_ge(sp.sem, d)
                sa.inc(g.dma_start(out=out_h[0:128, :], in_=x_t[:, 0, :]), 16)
                g.wait_ge(sa.sem, 16)
            return nc

        GRAM = (
            [("L0", 128 * t, 128 * t, 1.0) for t in range(16)]
            + [("L1", _OOFF[1] + 128 * a, _EOFF[1] + 128 * a, 0.25) for a in range(4)]
            + [("L2", _OOFF[2] + 128 * a, _EOFF[2] + 128 * a, 0.0625) for a in range(2)]
            + [("L3", _OOFF[3], _EOFF[3], 4.0 ** -3)]
            + [("CB", None, None, None)]
        )
        CB_SUBS = [(4, 64, 0), (5, 48, 64)]

        def _emit_grams(p, b, hh, orient):
            ps = ps_a if orient == "S" else ps_b
            bank = b % 2
            pr0 = 64 * hh
            first = [True]
            lastmm = [None]

            def mm(out, lt, rh, last):
                lastmm[0] = p.matmul(out, lt, rh, start=first[0], stop=last,
                                     skip_group_check=True)
                first[0] = False

            for bi in range(4):
                ai = 4 * b + bi
                kind, qo, ko, _sc = GRAM[ai]
                lastb = bi == 3
                if kind != "CB":
                    sq = qT_p if ai < 16 else qP
                    sk = kT_p if ai < 16 else kP
                    a_, b_ = (qo, ko) if orient == "S" else (ko, qo)
                    s1, s2 = (sq, sk) if orient == "S" else (sk, sq)
                    mm(ps[:, bank, 128 * bi:128 * (bi + 1)],
                       s1[pr0:pr0 + 64, a_:a_ + 128], s2[pr0:pr0 + 64, b_:b_ + 128],
                       lastb)
                else:
                    mm(ps[:, bank, 128 * bi:128 * (bi + 1)], zeros2[:], zeros2[:],
                       False)
                    for si, (l, w, ro) in enumerate(CB_SUBS):
                        qo2, ko2 = _OOFF[l], _EOFF[l]
                        a_, b_ = (qo2, ko2) if orient == "S" else (ko2, qo2)
                        s1, s2 = (qP, kP) if orient == "S" else (kP, qP)
                        lastmm[0] = p.matmul(
                            ps[ro:ro + w, bank, 128 * bi + ro:128 * bi + ro + w],
                            s1[pr0:pr0 + 64, a_:a_ + w],
                            s2[pr0:pr0 + 64, b_:b_ + w],
                            start=False, stop=(lastb and si == 1),
                            skip_group_check=True)
            return lastmm[0]

        abank_free = {0: 0, 1: 0}
        bbank_free = {0: 0, 1: 0}
        ybank_free = {0: 0, 1: 0}
        am_free = {0: 0, 1: 0}
        sv_attn_all = 0

        @blk.tensor
        def _(p):
            p.wait_ge(sw.sem, sw_marks["all"])
            for bk in range(2):
                p.matmul(ps_y[:, bk, 0:512], zeros2[:], brow_b[:, 0:512],
                         start=True, stop=True, skip_group_check=True)

        N2B = NPAIR
        ss_qk_d, st_vpn_d = {}, {}
        sv_poolq_d, sv_poolk_d = {}, {}

        def emit_qk_group(pair, gi):
            feat0 = (128 * pair) if gi < 4 else (D + 128 * pair)
            n4 = gi % 4
            bank = gi % 2
            dstbuf, qscale = (qT_p, 0.125) if gi < 4 else (kT_p, None)

            @blk.tensor
            def _(p, pair=pair, feat0=feat0, n4=n4, bank=bank, gi=gi):
                if pair == 0 and gi == 0:
                    p.wait_ge(st.sem, st_p1_done)
                p.wait_ge(ss.sem, gbank_free[bank])
                for c in range(NC):
                    mm = p.matmul(ps_g[:, bank, :],
                             wqkvT[:, c, feat0:feat0 + 128],
                             xnT[:, c, 512 * n4:512 * (n4 + 1)],
                             start=(c == 0), stop=(c == NC - 1))
                sp.inc(mm)

            @blk.vector
            def _(v, dstbuf=dstbuf, qscale=qscale, n4=n4, bank=bank,
                  mark=sp.n):
                v.wait_ge(sp.sem, mark)
                ss.inc(v.tensor_scalar(
                    out=dstbuf[:, 512 * n4:512 * (n4 + 1)],
                    in0=ps_g[:, bank, :],
                    scalar1=(1.0 if qscale is None else qscale),
                    scalar2=None, op0=ALU.mult))
            gbank_free[bank] = ss.n
            if gi == 7:
                ss_qk_d[pair] = ss.n

        def _pool_one(v, srcbuf, dstbuf, pair):
            src0 = vT[:, pair, :] if srcbuf is None else srcbuf[:]
            rv = src0.rearrange("p (bp r) -> p bp r", r=64)
            v.tensor_tensor(
                out=dstbuf[:, _EOFF[1]:_EOFF[1] + 512].rearrange(
                    "p (a w) -> p a w", w=16),
                in0=rv[:, :, 0:31:2], in1=rv[:, :, 1:32:2], op=ALU.add)
            v.tensor_tensor(
                out=dstbuf[:, _OOFF[1]:_OOFF[1] + 512].rearrange(
                    "p (a w) -> p a w", w=16),
                in0=rv[:, :, 32:63:2], in1=rv[:, :, 33:64:2], op=ALU.add)
            for l in range(2, 7):
                P = (2048 >> l) // 32
                for (dof, sof) in [(_EOFF[l], _EOFF[l - 1]),
                                   (_EOFF[l] + 8, _OOFF[l - 1]),
                                   (_OOFF[l], _EOFF[l - 1] + 16),
                                   (_OOFF[l] + 8, _OOFF[l - 1] + 16)]:
                    o = dstbuf[:, dof:dof + 16 * P].rearrange(
                        "p (g u) -> p g u", u=16)[:, :, 0:8]
                    i0 = dstbuf[:, sof:sof + 32 * P].rearrange(
                        "p (g u) -> p g u", u=32)[:, :, 0:15:2]
                    i1 = dstbuf[:, sof + 1:sof + 1 + 32 * P].rearrange(
                        "p (g u) -> p g u", u=32)[:, :, 0:15:2]
                    last = v.tensor_tensor(out=o, in0=i0, in1=i1, op=ALU.add)
            return last

        def emit_pool(pair):
            @blk.vector
            def _(v, pair=pair, m=ss_qk_d[pair], stv=st_v_last):
                v.wait_ge(ss.sem, m)
                if pair == 0:
                    v.wait_ge(st.sem, stv)
                sv.inc(_pool_one(v, None, vP, pair))

            @blk.sync
            def _(s, mark=sv.n):
                s.wait_ge(sv.sem, mark)
                st.inc(s.dma_start_transpose(out=vPn[:], in_=vP[:]), 16)
            st_vpn_d[pair] = st.n

            @blk.vector
            def _(v, pair=pair):
                sv.inc(_pool_one(v, qT_p, qP, pair))
            sv_poolq_d[pair] = sv.n

            @blk.vector
            def _(v, pair=pair):
                sv.inc(_pool_one(v, kT_p, kP, pair))
            sv_poolk_d[pair] = sv.n

        def emit_S(b, hh, pair):
            bank = b % 2

            @blk.tensor
            def _(p, b=b, hh=hh, bank=bank, pair=pair):
                if b == 0 and hh == 0:
                    p.wait_ge(ss.sem, ss_qk_d[pair])
                if b == 4 and hh == 0:
                    p.wait_ge(sv.sem, sv_poolq_d[pair])
                    p.wait_ge(sv.sem, sv_poolk_d[pair])
                p.wait_ge(sv.sem, abank_free[bank])
                sp.inc(_emit_grams(p, b, hh, "S"))

            @blk.vector
            def _(v, b=b, hh=hh, bank=bank, mark=sp.n):
                v.wait_ge(sp.sem, mark)
                pv = ps_a[:, bank, :]
                if b < 4:
                    v.scalar_tensor_tensor(
                        out=smS[:].rearrange("p (n f) -> p n f", f=128),
                        in0=pv.rearrange("p (n f) -> p n f", f=128),
                        scalar=1.0, op0=ALU.mult, op1=ALU.add,
                        in1=msk_s[:, 0:1, :].broadcast_to((128, 4, 128)))
                elif b == 4:
                    v.scalar_tensor_tensor(
                        out=smS[:].rearrange("p (n f) -> p n f", f=128),
                        in0=pv.rearrange("p (n f) -> p n f", f=128),
                        scalar=0.25, op0=ALU.mult, op1=ALU.add,
                        in1=msk_s[:, 1:2, :].broadcast_to((128, 4, 128)))
                else:
                    v.scalar_tensor_tensor(
                        out=smS[:, 0:256].rearrange("p (n f) -> p n f", f=128),
                        in0=pv[:, 0:256].rearrange("p (n f) -> p n f", f=128),
                        scalar=0.0625, op0=ALU.mult, op1=ALU.add,
                        in1=msk_s[:, 1:2, :].broadcast_to((128, 2, 128)))
                    v.scalar_tensor_tensor(
                        out=smS[:, 256:384], in0=pv[:, 256:384],
                        scalar=4.0 ** -3, op0=ALU.mult, op1=ALU.add,
                        in1=msk_s[:, 1, :])
                    v.scalar_tensor_tensor(
                        out=smS[:, 384:512], in0=pv[:, 384:512],
                        scalar=scalecol[:], op0=ALU.mult, op1=ALU.add,
                        in1=msk_s[:, 1, :])
                sv.inc(v.tensor_reduce(
                    out=mbuf[:, hh, 4 * b:4 * (b + 1)],
                    in_=smS[:].rearrange("p (n f) -> p n f", f=128),
                    axis=AX, op=ALU.max))
            abank_free[bank] = sv.n

        def emit_embuf(hh):
            @blk.scalar
            def _(s, hh=hh, sv_max=sv.n):
                s.wait_ge(sv.sem, sv_max)
                ss.inc(s.activation(out=embuf[:, hh, :], in_=mbuf[:, hh, :],
                                    func=ACT.Exp, bias=0.0, scale=-1.0))
            return ss.n

        sv_am_l = {}

        def emit_st(b, hh):
            bank = b % 2

            @blk.tensor
            def _(p, b=b, hh=hh, bank=bank):
                p.wait_ge(ss.sem, bbank_free[bank])
                sp.inc(_emit_grams(p, b, hh, "T"))

            @blk.scalar
            def _(s, b=b, bank=bank, mark=sp.n, amf=am_free[bank]):
                s.wait_ge(sp.sem, mark)
                s.wait_ge(sp.sem, amf)
                pb = ps_b[:, bank, :]
                ab = am[:, bank, :]
                if b < 4:
                    ei = s.activation(out=ab, in_=pb, func=ACT.Exp, bias=0.0,
                                      scale=1.0)
                elif b == 4:
                    ei = s.activation(out=ab, in_=pb, func=ACT.Exp, bias=0.0,
                                      scale=0.25)
                else:
                    s.activation(out=ab[:, 0:256], in_=pb[:, 0:256],
                                 func=ACT.Exp, bias=0.0, scale=0.0625)
                    s.activation(out=ab[:, 256:384], in_=pb[:, 256:384],
                                 func=ACT.Exp, bias=0.0, scale=4.0 ** -3)
                    s.activation(out=ab[:, 384:448], in_=pb[:, 384:448],
                                 func=ACT.Exp, bias=0.0, scale=4.0 ** -4)
                    s.activation(out=ab[:, 448:480], in_=pb[:, 448:480],
                                 func=ACT.Exp, bias=0.0, scale=4.0 ** -5)
                    ei = s.activation(out=ab[:, 480:512], in_=pb[:, 480:512],
                                      func=ACT.Exp, bias=0.0, scale=4.0 ** -6)
                ss.inc(ei)
            bbank_free[bank] = ss.n

            @blk.vector
            def _(v, b=b, bank=bank, mark=ss.n):
                v.wait_ge(ss.sem, mark)
                ab3 = am[:, bank, :].rearrange("p (n f) -> p n f", f=128)
                if b < 4:
                    mi = v.tensor_tensor(out=ab3, in0=ab3,
                                    in1=msk_t[:, 0:1, :].broadcast_to(
                                        (128, 4, 128)), op=ALU.mult)
                elif b == 4:
                    mi = v.tensor_tensor(out=ab3, in0=ab3,
                                    in1=msk_t[:, 1:2, :].broadcast_to(
                                        (128, 4, 128)), op=ALU.mult)
                else:
                    v.tensor_tensor(out=ab3[:, 0:3, :], in0=ab3[:, 0:3, :],
                                    in1=msk_t[:, 1:2, :].broadcast_to(
                                        (128, 3, 128)), op=ALU.mult)
                    mi = v.tensor_tensor(out=ab3[:, 3, :], in0=ab3[:, 3, :],
                                         in1=msk_t[:, 2, :], op=ALU.mult)
                sv.inc(mi)
            sv_am_l[b] = sv.n

        def emit_pv(b, hh, pair):
            bank = b % 2
            sv_am = sv_am_l[b]

            @blk.tensor
            def _(p, pair=pair, hh=hh, b=b, bank=bank, sv_am=sv_am):
                p.wait_ge(sv.sem, sv_am)
                p.wait_ge(sv.sem, ybank_free[bank])
                if b == 4 and hh == 0:
                    p.wait_ge(st.sem, st_vpn_d[pair])
                h = 2 * pair + hh
                lastpv = None
                for bi in range(4):
                    ai = 4 * b + bi
                    kind = GRAM[ai][0]
                    amat = am[:, bank, 128 * bi:128 * (bi + 1)]
                    yc = 66 * bi
                    if kind == "L0":
                        rhs = v_sb[:, ai, 64 * h:64 * h + 64]
                        lastpv = p.matmul(ps_y[:, bank, yc:yc + 64], amat, rhs,
                                 start=True, stop=True,
                                 skip_group_check=True)
                        lastpv = p.matmul(ps_y[:, bank, yc + 64:yc + 65], amat,
                                 onescol[:], start=True,
                                 stop=True, skip_group_check=True)
                    elif kind != "CB":
                        mt = {16: 0, 17: 1, 18: 2, 19: 3,
                              20: 8, 21: 9, 22: 12}[ai]
                        rhs = vPn[:, mt, 64 * hh:64 * hh + 64]
                        lastpv = p.matmul(ps_y[:, bank, yc:yc + 64], amat, rhs,
                                 start=True, stop=True,
                                 skip_group_check=True)
                        lastpv = p.matmul(ps_y[:, bank, yc + 64:yc + 65], amat,
                                 onescol[:], start=True,
                                 stop=True, skip_group_check=True)
                    else:
                        pieces = [(0, 64, 14), (64, 48, 15)]
                        for (r0, w, mt) in pieces:
                            asl = am[r0:r0 + w, bank,
                                     128 * bi + r0:128 * bi + r0 + w]
                            p.matmul(ps_y[r0:r0 + w, bank, yc:yc + 64],
                                     asl,
                                     vPn[r0:r0 + w, mt,
                                         64 * hh:64 * hh + 64],
                                     start=True, stop=True,
                                     skip_group_check=True)
                            lastpv = p.matmul(
                                ps_y[r0:r0 + w, bank, yc + 64:yc + 65],
                                asl, onescol[r0:r0 + w, :], start=True,
                                stop=True, skip_group_check=True)
                sp.inc(lastpv)
            am_free[bank] = sp.n

        def emit_scale(b, hh, pair, ss_em):
            bank = b % 2

            @blk.vector
            def _(v, pair=pair, hh=hh, b=b, bank=bank, mark=sp.n,
                  ss_em=ss_em):
                v.wait_ge(sp.sem, mark)
                v.wait_ge(ss.sem, ss_em)
                src = ps_y[:, bank, 0:264].rearrange("p (n f) -> p n f", f=66)[:, :, 0:65]
                if b < 4:
                    dst = y0[:, hh, 4 * b:4 * (b + 1), :]
                elif b == 4:
                    dst = stg[:, hh, 0:4, :]
                else:
                    dst = stg[:, hh, 4:8, :]
                sv.inc(v.scalar_tensor_tensor(
                    out=dst, in0=src, scalar=1.0, op0=ALU.mult,
                    op1=ALU.mult,
                    in1=embuf[:, hh, 4 * b:4 * (b + 1)].unsqueeze(2)
                        .broadcast_to((128, 4, 65))))
            ybank_free[bank] = sv.n

        PIDX = {1: 0, 2: 4, 3: 12, 4: 20, "C56": 28}
        GWAVES = [list(range(0, 7)), list(range(7, 14)), [14, 15]]

        def emit_gather(wi, hh, pair, sv_stage):
            wave = GWAVES[wi]
            bank = wi % 2

            @blk.tensor
            def _(p, hh=hh, wave=wave, bank=bank, sv_stage=sv_stage):
                p.wait_ge(sv.sem, sv_stage)
                p.wait_ge(sv.sem, ybank_free[bank])
                pd = ps_y[:, bank, 0:462].rearrange("p (n f) -> p n f", f=66)
                mms = []
                for ti, tt in enumerate(wave):
                    mms.append((pd[:, ti, 0:65], ident[:], y0[:, hh, tt, :]))
                for l in range(1, 5):
                    for ti, tt in enumerate(wave):
                        if l == 1:
                            var, acol = tt % 4, tt // 4
                        elif l == 2:
                            var, acol = tt % 8, 4 + tt // 8
                        elif l == 3:
                            if tt % 2 == 0:
                                continue
                            var, acol = (tt - 1) // 2, 6
                        else:
                            if tt % 4 < 2:
                                continue
                            var = 2 * (tt // 4) + (tt % 4 - 2)
                            acol = 7
                        mms.append((pd[:, ti, 0:65],
                                    perm[:, PIDX[l] + var, :],
                                    stg[:, hh, acol, :]))
                for ti, tt in enumerate(wave):
                    if tt < 4:
                        continue
                    mms.append((pd[:, ti, 0:65],
                                perm[:, PIDX["C56"] + (tt - 4), :],
                                stg[:, hh, 7, :]))
                for mi, (o_, lt, rh) in enumerate(mms):
                    gi = p.matmul(o_, lt, rh, start=(mi == 0),
                                  stop=(mi == len(mms) - 1),
                                  skip_group_check=True)
                sp.inc(gi)

            @blk.vector
            def _(v, pair=pair, hh=hh, wave=wave, bank=bank, mark=sp.n):
                v.wait_ge(sp.sem, mark)
                nw = len(wave)
                t0 = wave[0]
                pd = ps_y[:, bank, 0:462].rearrange("p (n f) -> p n f", f=66)
                v.tensor_scalar(out=avb[:, t0:t0 + nw],
                                in0=pd[:, 0:nw, 64], scalar1=1e-8,
                                scalar2=None, op0=ALU.add)
                v.drain()
                v.reciprocal(out=rinv[:, t0:t0 + nw], in_=avb[:, t0:t0 + nw])
                v.drain()
                if wave[0] == 0 and (pair + hh) > 0:
                    v.wait_ge(sa.sem, 16 * (2 * pair + hh))
                sv.inc(v.scalar_tensor_tensor(
                    out=attn_hd[:, t0:t0 + nw, :],
                    in0=pd[:, 0:nw, 0:64], scalar=1.0,
                    op0=ALU.mult, op1=ALU.mult,
                    in1=rinv[:, t0:t0 + nw].unsqueeze(2)
                        .broadcast_to((128, nw, 64))))
            ybank_free[bank] = sv.n

        def emit_attn_dma(pair, hh):
            @blk.gpsimd
            def _(g, pair=pair, hh=hh, mark=sv.n):
                g.wait_ge(sv.sem, mark)
                h = 2 * pair + hh
                sa.inc(g.dma_start(
                    out=attn_sp.rearrange("(t p) d -> p t d", p=128)[:, :, 64 * h:64 * h + 64],
                    in_=attn_hd[:]), 16)

        for pair in range(N2B):
            if pair == 0:
                for gi in range(8):
                    emit_qk_group(0, gi)
                emit_pool(0)
            for b in range(6):
                emit_S(b, 0, pair)
            ss_em0 = emit_embuf(0)
            for k in range(7):
                if k < 6:
                    emit_st(k, 0)
                if 1 <= k <= 4:
                    emit_S(k - 1, 1, pair)
                if k >= 1:
                    emit_pv(k - 1, 0, pair)
                    emit_scale(k - 1, 0, pair, ss_em0)
            sv_stage = sv.n
            emit_gather(0, 0, pair, sv_stage)
            emit_S(4, 1, pair)
            emit_gather(1, 0, pair, sv_stage)
            emit_S(5, 1, pair)
            emit_gather(2, 0, pair, sv_stage)
            emit_attn_dma(pair, 0)
            ss_em1 = emit_embuf(1)
            nxt = pair + 1 if pair + 1 < N2B else None
            for k in range(7):
                if k < 6:
                    emit_st(k, 1)
                if k >= 1:
                    emit_pv(k - 1, 1, pair)
                    emit_scale(k - 1, 1, pair, ss_em1)
                if nxt is not None and 4 <= k <= 6:
                    emit_qk_group(nxt, k - 4)
            sv_stage = sv.n
            emit_gather(0, 1, pair, sv_stage)
            if nxt is not None:
                emit_qk_group(nxt, 3)
                emit_qk_group(nxt, 4)
            emit_gather(1, 1, pair, sv_stage)
            if nxt is not None:
                emit_qk_group(nxt, 5)
                emit_qk_group(nxt, 6)
            emit_gather(2, 1, pair, sv_stage)
            emit_attn_dma(pair, 1)
            if nxt is not None:
                emit_qk_group(nxt, 7)
                emit_pool(nxt)

        if xn_dbg is not None:
            @blk.gpsimd
            def _(g, mark=sv.n):
                g.wait_ge(sv.sem, mark)
                sa.inc(g.dma_start(out=qp_dbg, in_=qP[:]), 16)
                sa.inc(g.dma_start(out=kp_dbg, in_=kP[:]), 16)
                sa.inc(g.dma_start(out=vp_dbg, in_=vP[:]), 16)
                sa.inc(g.dma_start(out=stg_dbg, in_=stg[:].rearrange("p a b c -> p (a b c)")), 16)
                sa.inc(g.dma_start(out=y0_dbg, in_=y0[:].rearrange("p a b c -> p (a b c)")), 16)
                sa.inc(g.dma_start(out=em_dbg, in_=embuf[:].rearrange("p a b -> p (a b)")), 16)
                sa.inc(g.dma_start(out=em2_dbg, in_=em2[:]), 16)

        sv_attn_all = sv.n
        if phase_limit <= 3:
            @blk.gpsimd
            def _(g, a=st.n, b=sv.n, c=ss.n, d=sp.n, e=sa.n):
                g.wait_ge(st.sem, a)
                g.wait_ge(sv.sem, b)
                g.wait_ge(ss.sem, c)
                g.wait_ge(sp.sem, d)
                g.wait_ge(sa.sem, e)
                sa.inc(g.dma_start(out=out_h[0:128, 0:24], in_=mbuf[:].rearrange("p a b -> p (a b)")[:, 0:24]), 16)
                g.wait_ge(sa.sem, e + 16)
            return nc

        ctx3.close()
        ctx2.close()
        ctx4 = contextlib.ExitStack()
        ctx.enter_context(ctx4)
        E4 = ctx4.enter_context
        xdT = E4(nc.sbuf_tensor("xdT", [128, NC, T], BF16))
        attnT = E4(nc.sbuf_tensor("attnT", [128, NC, T], BF16))
        wdT = E4(nc.sbuf_tensor("wdT", [128, NC, D], BF16))
        woutT = E4(nc.sbuf_tensor("woutT_s", [128, NC, D], BF16))
        x_r = E4(nc.sbuf_tensor("x_r", [128, 2, D], F32))
        out_sb = E4(nc.sbuf_tensor("out_sb", [128, 2, D], F32))

        obank_free = {0: 0, 1: 0}
        xr_free = {0: 0, 1: 0}
        ob_free = {0: 0, 1: 0}
        @blk.gpsimd
        def _(g, mark=sv_attn_all):
            g.wait_ge(sv.sem, mark)
            for c_ in range(NC):
                sw.inc(g.dma_start(out=woutT[:, c_, :],
                                   in_=woutT_h[128 * c_:128 * (c_ + 1), :]), 16)
        sw_marks["woutT"] = sw.n

        @blk.sync
        def _(s, san=sa.n):
            s.wait_ge(sa.sem, san)
            for wt in range(NC):
                st.inc(s.dma_start_transpose(out=wdT[:, :, 128 * wt:128 * (wt + 1)],
                                             in_=wd_sp[128 * wt:128 * (wt + 1), :]), 16)
        st_wd_done = st.n

        at_mark = []
        for t in range(NT):
            @blk.sync
            def _(s, t=t):
                st.inc(s.dma_start_transpose(out=attnT[:, :, 128 * t:128 * (t + 1)],
                                             in_=attn_sp[128 * t:128 * (t + 1), :]), 16)
                st.inc(s.dma_start_transpose(out=xdT[:, :, 128 * t:128 * (t + 1)],
                                             in_=xd_sp[128 * t:128 * (t + 1), :]), 16)
            at_mark.append((st.n, 0))

        for t in range(NT):
            @blk.sync
            def _(s, t=t):
                if t >= 2:
                    s.wait_ge(sv.sem, xr_free[t % 2])
                sx.inc(s.dma_start(out=x_r[:, t % 2, :],
                                   in_=x_h[128 * t:128 * (t + 1), :]), 16)
            at_mark[t] = (at_mark[t][0], sx.n)

            for n2 in range(2):
                bank = (2 * t + n2) % 2

                @blk.tensor
                def _(p, t=t, n2=n2, bank=bank):
                    p.wait_ge(st.sem, at_mark[t][0])
                    if t == 0 and n2 == 0:
                        p.wait_ge(st.sem, st_wd_done)
                        p.wait_ge(sw.sem, sw_marks["woutT"])
                        p.wait_ge(ss.sem, gbank_free[0])
                        p.wait_ge(ss.sem, gbank_free[1])
                    p.wait_ge(sv.sem, obank_free[bank])
                    p.matmul(ps_g[:, bank, :], ones2[:],
                             brow_b[:, 512 * n2:512 * (n2 + 1)],
                             start=True, stop=False)
                    for c in range(NC):
                        p.matmul(ps_g[:, bank, :], attnT[:, c, 128 * t:128 * (t + 1)],
                                 woutT[:, c, 512 * n2:512 * (n2 + 1)],
                                 start=False, stop=False)
                    for c in range(NC):
                        mm = p.matmul(ps_g[:, bank, :], xdT[:, c, 128 * t:128 * (t + 1)],
                                 wdT[:, c, 512 * n2:512 * (n2 + 1)],
                                 start=False, stop=(c == NC - 1))
                    sp.inc(mm)

                @blk.vector
                def _(v, t=t, n2=n2, bank=bank, mark=sp.n):
                    v.wait_ge(sp.sem, mark)
                    v.wait_ge(sx.sem, at_mark[t][1])
                    if t >= 2 and n2 == 0:
                        v.wait_ge(so.sem, ob_free[t % 2])
                    sv.inc(v.tensor_tensor(
                        out=out_sb[:, t % 2, 512 * n2:512 * (n2 + 1)],
                        in0=x_r[:, t % 2, 512 * n2:512 * (n2 + 1)],
                        in1=ps_g[:, bank, :], op=ALU.add))
                obank_free[bank] = sv.n
            xr_free[t % 2] = sv.n

            @blk.gpsimd
            def _(g, t=t, mark=sv.n):
                g.wait_ge(sv.sem, mark)
                so.inc(g.dma_start(out=out_h[128 * t:128 * (t + 1), :],
                                   in_=out_sb[:, t % 2, :]), 16)
            ob_free[t % 2] = so.n

        @blk.gpsimd
        def _(g):
            g.wait_ge(so.sem, so.n)

    return nc


_NC_CACHE = {}


def _get_nc():
    import os
    pl = int(os.environ.get("KPHASE", "99"))
    key = ("nc", pl)
    if key not in _NC_CACHE:
        _NC_CACHE[key] = build_nc(pl)
    return _NC_CACHE[key]


def kernel(L0_attn_x, L0_attn_w_qkv, L0_attn_w_out, L0_attn_b_out,
           L0_ffn_weight, L0_ffn_bias):
    x = np.ascontiguousarray(np.asarray(L0_attn_x, dtype=np.float32))
    wqkvT = np.ascontiguousarray(np.asarray(L0_attn_w_qkv, dtype=np.float32).T)
    woutT = np.ascontiguousarray(np.asarray(L0_attn_w_out, dtype=np.float32).T)
    wffn = np.ascontiguousarray(np.asarray(L0_ffn_weight, dtype=np.float32))
    brow = np.ascontiguousarray(
        np.stack([np.asarray(L0_attn_b_out, dtype=np.float32),
                  np.asarray(L0_ffn_bias, dtype=np.float32)], axis=0))
    hc = _host_consts()
    base = {"wqkvT": wqkvT, "woutT": woutT, "wffn": wffn, "brow": brow}
    for k, v in hc.items():
        base["c_" + k] = np.ascontiguousarray(v)
    in_maps = []
    for c in range(8):
        m = dict(base)
        m["x"] = np.ascontiguousarray(x[c])
        in_maps.append(m)
    nc = _get_nc()
    res = run_bass_kernel_spmd(nc, in_maps, core_ids=list(range(8)))
    out = np.stack([np.asarray(r["out"]) for r in res.results], axis=0)
    return out.astype(np.float32)


# version 08
# speedup vs baseline: 1.2364x; 1.0062x over previous
import contextlib

import numpy as np
import ml_dtypes

import concourse.bass as bass
import concourse.mybir as mybir
from concourse.bass_utils import run_bass_kernel_spmd

F32 = mybir.dt.float32
BF16 = mybir.dt.bfloat16
FP8 = mybir.dt.float8e4
AX = mybir.AxisListType.X
ALU = mybir.AluOpType
ACT = mybir.ActivationFunctionType

T = 2048
D = 1024
NT = T // 128
NC = D // 128
NPAIR = 8
MAGIC8 = 1572864.0
NEG = -1.0e30

_EOFF, _OOFF = {}, {}
_o = 0
for _l in range(1, 5):
    n = 2048 >> _l
    _EOFF[_l] = _o
    _o += n // 2
    _OOFF[_l] = _o
    _o += n // 2
_OOFF[5], _OOFF[6], _EOFF[5], _EOFF[6] = 1920, 1952, 1984, 2016

_RL = {1: 32, 2: 16, 3: 16, 4: 8, 5: 4, 6: 2}


def _mask_s(causal, mask_pad):
    m = np.full((128, 128), -NEG, np.float32)
    for i in range(128):
        b = i // 16
        if mask_pad and b == 7:
            continue
        for j in range(16 * b, 16 * b + 16):
            if causal and j % 16 > i % 16:
                continue
            m[i, j] = 0.0
    return m


def _mask_t01(causal, mask_pad):
    m = np.zeros((128, 128), np.float32)
    for j in range(128):
        b = j // 16
        if mask_pad and b == 7:
            continue
        for i in range(16 * b, 16 * b + 16):
            if causal and j % 16 > i % 16:
                continue
            m[j, i] = 1.0
    return m.astype(ml_dtypes.bfloat16)


_NVAR = {1: 4, 2: 8, 3: 8, 4: 8}


def _perm_names():
    names = []
    for l in range(1, 5):
        for v in range(_NVAR[l]):
            names.append(f"P_L{l}_{v}")
    for v in range(12):
        names.append(f"P_C56_{v}")
    return names


def _host_consts():
    c = {}
    c["mask_s0"] = _mask_s(True, False)
    c["mask_sb"] = _mask_s(False, False)
    c["mask_sc"] = _mask_s(False, True)
    c["mask_t0"] = _mask_t01(True, False)
    c["mask_tb"] = _mask_t01(False, False)
    c["mask_tc"] = _mask_t01(False, True)
    c["ident"] = np.eye(128, dtype=np.float32).astype(ml_dtypes.bfloat16)
    for l in range(1, 5):
        r = _RL[l]
        for v in range(_NVAR[l]):
            m = np.zeros((128, 128), np.float32)
            for j in range(r):
                row = v * r + j
                w = row % 16
                m[row, ((17 * (1 << l) - 1) + (1 << l) * w
                        + (1 << (l + 5)) * (row // 16)) % 128] = 1.0
            c[f"P_L{l}_{v}"] = m.astype(ml_dtypes.bfloat16)
    for v in range(12):
        tt = v + 4
        m = np.zeros((128, 128), np.float32)
        if (tt // 4) % 2 == 1:
            pp, qq = tt // 8, tt % 4
            for j in range(4):
                j5 = 16 * pp + 4 * qq + j
                m[64 + j5, (543 + 32 * j5) % 128] = 1.0
        if tt >= 8:
            for j in range(2):
                j6 = 2 * (tt - 8) + j
                m[96 + j6, (1087 + 64 * j6) % 128] = 1.0
        c[f"P_C56_{v}"] = m.astype(ml_dtypes.bfloat16)
    c["ones2"] = np.ones((2, 128), np.float32).astype(ml_dtypes.bfloat16)
    c["zeros2"] = np.zeros((2, 128), np.float32).astype(ml_dtypes.bfloat16)
    c["onescol"] = np.ones((128, 1), np.float32).astype(ml_dtypes.bfloat16)
    sc = np.zeros((128, 1), np.float32)
    sc[0:64] = 4.0 ** -4
    sc[64:96] = 4.0 ** -5
    sc[96:128] = 4.0 ** -6
    c["scalecol"] = sc
    qs = np.ones((128, 2048), np.float32)
    for l in range(1, 7):
        n = 2048 >> l
        for off in (_EOFF[l], _OOFF[l]):
            qs[:, off:off + n // 2] *= 4.0 ** -l
    c["qscl"] = qs.astype(ml_dtypes.bfloat16)
    return c


class Ctr:
    def __init__(self, sem):
        self.sem = sem
        self.n = 0

    def inc(self, inst, k=1):
        inst.then_inc(self.sem, k)
        self.n += k
        return self.n


def build_nc(phase_limit=99):
    nc = bass.Bass("TRN2", target_bir_lowering=False)
    x_h = nc.dram_tensor("x", [T, D], F32, kind="ExternalInput").ap()
    wqkvT_h = nc.dram_tensor("wqkvT", [D, 3 * D], F32, kind="ExternalInput").ap()
    woutT_h = nc.dram_tensor("woutT", [D, D], F32, kind="ExternalInput").ap()
    wffn_h = nc.dram_tensor("wffn", [D, D], F32, kind="ExternalInput").ap()
    brow_h = nc.dram_tensor("brow", [2, D], F32, kind="ExternalInput").ap()
    hc = _host_consts()
    cst = {}
    for name, arr in hc.items():
        dt = F32 if arr.dtype == np.float32 else BF16
        cst[name] = nc.dram_tensor("c_" + name, list(arr.shape), dt,
                                   kind="ExternalInput").ap()
    import os as _os0
    _dbg = int(_os0.environ.get("KDBG", "0"))
    _kind = "ExternalOutput" if _dbg else "Internal"
    out_h = nc.dram_tensor("out", [T, D], F32, kind="ExternalOutput").ap()
    xd_sp = nc.dram_tensor("xd_spill", [T, D], BF16, kind=_kind).ap()
    wd_sp = nc.dram_tensor("wd_spill", [D, D], BF16, kind=_kind).ap()
    attn_sp = nc.dram_tensor("attn_spill", [T, D], BF16, kind=_kind).ap()
    xn_dbg = nc.dram_tensor("xn_dbg", [T, D], BF16, kind=_kind).ap() if _dbg else None
    qp_dbg = nc.dram_tensor("qp_dbg", [128, 2048], BF16, kind=_kind).ap() if _dbg else None
    kp_dbg = nc.dram_tensor("kp_dbg", [128, 2048], BF16, kind=_kind).ap() if _dbg else None
    vp_dbg = nc.dram_tensor("vp_dbg", [128, 2048], BF16, kind=_kind).ap() if _dbg else None
    stg_dbg = nc.dram_tensor("stg_dbg", [128, 2 * 8 * 65], BF16, kind=_kind).ap() if _dbg else None
    y0_dbg = nc.dram_tensor("y0_dbg", [128, 2 * 16 * 65], BF16, kind=_kind).ap() if _dbg else None
    em_dbg = nc.dram_tensor("em_dbg", [128, 48], F32, kind=_kind).ap() if _dbg else None
    em2_dbg = nc.dram_tensor("em2_dbg", [128, 160], F32, kind=_kind).ap() if _dbg else None

    ctx = contextlib.ExitStack()
    with ctx:
        E = ctx.enter_context

        brow_b = E(nc.sbuf_tensor("brow_b", [2, D], BF16))
        msk_s = E(nc.sbuf_tensor("msk_s", [128, 3, 128], F32))
        msk_t = E(nc.sbuf_tensor("msk_t", [128, 3, 128], BF16))
        ident = E(nc.sbuf_tensor("ident", [128, 128], BF16))
        nperm = len(_perm_names())
        perm = E(nc.sbuf_tensor("perm", [128, nperm, 128], BF16))
        ones2 = E(nc.sbuf_tensor("ones2", [2, 128], BF16))
        zeros2 = E(nc.sbuf_tensor("zeros2", [2, 128], BF16))
        onescol = E(nc.sbuf_tensor("onescol", [128, 1], BF16))
        scalecol = E(nc.sbuf_tensor("scalecol", [128, 1], F32))
        _ACT_CONSTS = [1e-5, -1.0, -0.5, 0.125, 0.25, 0.0625,
                       4.0 ** -3, 4.0 ** -4, 4.0 ** -5, 4.0 ** -6]
        actc = E(nc.sbuf_tensor("actc", [128, len(_ACT_CONSTS)], F32))
        for i, val in enumerate(_ACT_CONSTS):
            nc.const_aps.register(F32, val, actc[:, i:i + 1])

        ps_g = E(nc.psum_tensor("ps_g", [128, 2, 512], F32))
        ps_a = E(nc.psum_tensor("ps_a", [128, 2, 512], F32))
        ps_b = E(nc.psum_tensor("ps_b", [128, 2, 512], F32))
        ps_y = E(nc.psum_tensor("ps_y", [128, 2, 512], F32))

        ctx2 = contextlib.ExitStack()
        ctx.enter_context(ctx2)
        E2 = ctx2.enter_context
        xnT = E2(nc.sbuf_tensor("xnT", [128, NC, T], BF16))
        wqkvT = E2(nc.sbuf_tensor("wqkvT_s", [128, NC, 3 * D], BF16))
        v_sb = E2(nc.sbuf_tensor("v_sb", [128, NT, D], BF16))
        vT = E2(nc.sbuf_tensor("vT", [128, NPAIR, T], BF16))

        ctx1 = contextlib.ExitStack()
        ctx.enter_context(ctx1)
        E1 = ctx1.enter_context
        x_t = E1(nc.sbuf_tensor("x_t", [128, 2, D], F32))
        wffn_t = E1(nc.sbuf_tensor("wffn_t", [128, 1, D], F32))
        wd_t = E1(nc.sbuf_tensor("wd_t", [128, 2, D], BF16))
        xn_f = E1(nc.sbuf_tensor("xn_f", [128, 2, D], F32))
        xnb_t = E1(nc.sbuf_tensor("xnb_t", [128, 2, D], BF16))
        
        q_g = E1(nc.sbuf_tensor("q_g", [128, 32], F32))
        q_inv = E1(nc.sbuf_tensor("q_inv", [128, 32], F32))
        q_s = E1(nc.sbuf_tensor("q_s", [128, 32], F32))
        q_u = E1(nc.sbuf_tensor("q_u", [128, D], F32))
        q_mg = E1(nc.sbuf_tensor("q_mg", [128, D], F32))
        q_h8 = E1(nc.sbuf_tensor("q_h8", [128, D], FP8))
        q_l8 = E1(nc.sbuf_tensor("q_l8", [128, D], FP8))
        q_qt = E1(nc.sbuf_tensor("q_qt", [128, D], F32))
        xd_t = E1(nc.sbuf_tensor("xd_t", [128, 2, D], BF16))
        dump = q_u
        ln_nm = E1(nc.sbuf_tensor("ln_nm", [128, 2], F32))
        ln_sx = E1(nc.sbuf_tensor("ln_sx", [128, 2], F32))
        ln_sq = E1(nc.sbuf_tensor("ln_sq", [128, 2], F32))
        ln_s2 = E1(nc.sbuf_tensor("ln_s2", [128, 2], F32))
        ln_nv = E1(nc.sbuf_tensor("ln_nv", [128, 2], F32))
        ln_rs = E1(nc.sbuf_tensor("ln_rs", [128, 2], F32))
        ln_nmr = E1(nc.sbuf_tensor("ln_nmr", [128, 2], F32))

        sx = Ctr(E(nc.semaphore("sx")))
        sw = Ctr(E(nc.semaphore("sw")))
        st = Ctr(E(nc.semaphore("st")))
        sv = Ctr(E(nc.semaphore("sv")))
        ss = Ctr(E(nc.semaphore("ss")))
        sp = Ctr(E(nc.semaphore("sp")))
        so = Ctr(E(nc.semaphore("so")))
        sa = Ctr(E(nc.semaphore("sa")))
        blk = E(nc.Block())

        @blk.vector
        def _(v):
            for i, val in enumerate(_ACT_CONSTS):
                v.memset(actc[:, i:i + 1], val)

        @blk.sync
        def _(s):
            for t in range(2):
                sx.inc(s.dma_start(out=x_t[:, t, :],
                                   in_=x_h[128 * t:128 * (t + 1), :]), 16)

        sw_marks = {}

        @blk.gpsimd
        def _(g):
            for c_ in range(NC):
                for h_ in range(2):
                    sw.inc(g.dma_start(
                        out=wqkvT[:, c_, 1536 * h_:1536 * (h_ + 1)],
                        in_=wqkvT_h[128 * c_:128 * (c_ + 1),
                                    1536 * h_:1536 * (h_ + 1)]), 16)
            sw_marks["wqkvT"] = sw.n
            sw.inc(g.dma_start(out=brow_b[:], in_=brow_h[:]), 16)
            for i, nm in enumerate(["mask_s0", "mask_sb", "mask_sc"]):
                sw.inc(g.dma_start(out=msk_s[:, i, :], in_=cst[nm]), 16)
            for i, nm in enumerate(["mask_t0", "mask_tb", "mask_tc"]):
                sw.inc(g.dma_start(out=msk_t[:, i, :], in_=cst[nm]), 16)
            sw.inc(g.dma_start(out=ident[:], in_=cst["ident"]), 16)
            for pi, nm in enumerate(_perm_names()):
                sw.inc(g.dma_start(out=perm[:, pi, :], in_=cst[nm]), 16)
            sw.inc(g.dma_start(out=ones2[:], in_=cst["ones2"]), 16)
            sw.inc(g.dma_start(out=zeros2[:], in_=cst["zeros2"]), 16)
            sw.inc(g.dma_start(out=onescol[:], in_=cst["onescol"]), 16)
            sw.inc(g.dma_start(out=scalecol[:], in_=cst["scalecol"]), 16)
            sw_marks["all"] = sw.n

        def _quant(v, src, dst):
            v.tensor_reduce(out=q_g[:], in_=src.rearrange("p (b k) -> p b k", k=32),
                            axis=AX, op=ALU.max, apply_absolute_value=True)
            v.drain()
            v.reciprocal(out=q_inv[:], in_=q_g[:])
            v.drain()
            v.tensor_scalar(out=q_inv[:], in0=q_inv[:], scalar1=7.5, scalar2=7.5e30,
                            op0=ALU.mult, op1=ALU.min)
            v.tensor_scalar(out=q_s[:], in0=q_g[:], scalar1=1.0 / 7.5, scalar2=1e-30,
                            op0=ALU.mult, op1=ALU.max)
            v.tensor_tensor(out=q_u[:].rearrange("p (b k) -> p b k", k=32),
                            in0=src.rearrange("p (b k) -> p b k", k=32),
                            in1=q_inv[:].unsqueeze(2).broadcast_to((128, 32, 32)),
                            op=ALU.mult)
            v.tensor_scalar(out=q_mg[:], in0=q_u[:], scalar1=1.0, scalar2=-1.0,
                            op0=ALU.min, op1=ALU.max)
            v.tensor_scalar(out=q_mg[:], in0=q_mg[:], scalar1=MAGIC8, scalar2=MAGIC8,
                            op0=ALU.add, op1=ALU.subtract)
            v.tensor_scalar(out=q_h8[:], in0=q_u[:], scalar1=1.0, scalar2=None,
                            op0=ALU.max)
            v.tensor_scalar(out=q_l8[:], in0=q_u[:], scalar1=-1.0, scalar2=None,
                            op0=ALU.min)
            v.scalar_tensor_tensor(out=q_qt[:], in0=q_mg[:], scalar=0.0, in1=q_h8[:],
                                   op0=ALU.add, op1=ALU.add)
            v.scalar_tensor_tensor(out=q_qt[:], in0=q_qt[:], scalar=0.0, in1=q_l8[:],
                                   op0=ALU.add, op1=ALU.add)
            return v.tensor_tensor(out=dst.rearrange("p (b k) -> p b k", k=32),
                            in0=q_qt[:].rearrange("p (b k) -> p b k", k=32),
                            in1=q_s[:].unsqueeze(2).broadcast_to((128, 32, 32)),
                            op=ALU.mult)

        st_wq = []
        xn_done = []
        sv_xnb_t = []
        sv_xd_tl = []
        ss_xnf_t = []
        sv_stats_l = []

        def emit_stats(t):
            r2 = t % 2

            @blk.vector
            def _(v, t=t, r2=r2):
                v.wait_ge(sx.sem, 16 * (t + 1))
                if t >= 2:
                    v.wait_ge(ss.sem, ss_xnf_t[t - 2])
                xt = x_t[:, r2, :]
                v.tensor_reduce(out=ln_sx[:, r2:r2 + 1], in_=xt, axis=AX, op=ALU.add)
                v.tensor_tensor(out=dump[:], in0=xt, in1=xt, op=ALU.mult)
                v.tensor_reduce(out=ln_sq[:, r2:r2 + 1], in_=dump[:], axis=AX,
                                op=ALU.add)
                v.tensor_scalar(out=ln_nm[:, r2:r2 + 1], in0=ln_sx[:, r2:r2 + 1],
                                scalar1=-1.0 / D, scalar2=None, op0=ALU.mult)
                v.tensor_scalar(out=ln_s2[:, r2:r2 + 1], in0=ln_sq[:, r2:r2 + 1],
                                scalar1=1.0 / D, scalar2=None, op0=ALU.mult)
                sv.inc(v.scalar_tensor_tensor(
                    out=ln_nv[:, r2:r2 + 1], in0=ln_nm[:, r2:r2 + 1],
                    scalar=ln_nm[:, r2:r2 + 1], in1=ln_s2[:, r2:r2 + 1],
                    op0=ALU.mult, op1=ALU.subtract))
            sv_stats_l.append(sv.n)

        gbank_free = {0: 0, 1: 0}
        st_v_l = [0]
        vg_marks = {}

        def emit_vgemm(t):
            marks = []
            for n2 in range(2):
                @blk.tensor
                def _(p, t=t, n2=n2):
                    if t == 0 and n2 == 0:
                        p.wait_ge(sw.sem, sw_marks["wqkvT"])
                    p.wait_ge(st.sem, xn_done[t])
                    p.wait_ge(ss.sem, gbank_free[n2])
                    for c in range(NC):
                        mm = p.matmul(
                            ps_g[:, n2, :], xnT[:, c, 128 * t:128 * (t + 1)],
                            wqkvT[:, c, 2 * D + 512 * n2:2 * D + 512 * (n2 + 1)],
                            start=(c == 0), stop=(c == NC - 1))
                    sp.inc(mm)
                marks.append(sp.n)
            vg_marks[t] = marks

        def emit_vcopy(t):
            for n2 in range(2):
                @blk.scalar
                def _(s, t=t, n2=n2, mark=vg_marks[t][n2]):
                    s.wait_ge(sp.sem, mark)
                    ss.inc(s.copy(out=v_sb[:, t, 512 * n2:512 * (n2 + 1)],
                                  in_=ps_g[:, n2, :]))
                gbank_free[n2] = ss.n

            @blk.sync
            def _(s, t=t, mark=ss.n):
                s.wait_ge(ss.sem, mark)
                st.inc(s.dma_start_transpose(out=vT[:, :, 128 * t:128 * (t + 1)],
                                             in_=v_sb[:, t, :]), 16)
            st_v_l[0] = st.n

        emit_stats(0)
        for t in range(NT):
            r2 = t % 2

            @blk.scalar
            def _(s, t=t, r2=r2, sv_stats=sv_stats_l[t]):
                s.wait_ge(sv.sem, sv_stats)
                if t >= 2:
                    s.wait_ge(sv.sem, sv_xd_tl[t - 2])
                ss.inc(s.activation(out=ln_rs[:, r2:r2 + 1],
                                    in_=ln_nv[:, r2:r2 + 1],
                                    func=ACT.Sqrt, bias=1e-5, scale=-1.0))

            ss_xnf_t.append(ss.n)

            if t >= 1:
                emit_vcopy(t - 1)

            if t + 1 < NT:
                emit_stats(t + 1)

            @blk.vector
            def _(v, t=t, r2=r2, mark=ss_xnf_t[t]):
                v.wait_ge(ss.sem, mark)
                if t >= 2:
                    v.wait_ge(st.sem, xn_done[t - 2] + (32 if xn_dbg is not None else 16))
                v.drain()
                v.reciprocal(out=ln_nv[:, r2:r2 + 1], in_=ln_rs[:, r2:r2 + 1])
                v.drain()
                v.tensor_tensor(out=ln_nmr[:, r2:r2 + 1], in0=ln_nv[:, r2:r2 + 1],
                                in1=ln_nm[:, r2:r2 + 1], op=ALU.mult)
                sv.inc(v.tensor_scalar(out=xnb_t[:, r2, :], in0=x_t[:, r2, :],
                                scalar1=ln_nv[:, r2:r2 + 1],
                                scalar2=ln_nmr[:, r2:r2 + 1],
                                op0=ALU.mult, op1=ALU.add))
                v.tensor_scalar(out=xn_f[:, r2, :], in0=x_t[:, r2, :],
                                scalar1=ln_nv[:, r2:r2 + 1],
                                scalar2=ln_nmr[:, r2:r2 + 1],
                                op0=ALU.mult, op1=ALU.add)
                sv.inc(_quant(v, xn_f[:, r2, :], xd_t[:, r2, :]))

            sv_xnb_t.append(sv.n - 1)
            sv_xd_tl.append(sv.n)

            @blk.sync
            def _(s, t=t, r2=r2, a=sv.n - 1, b=sv.n, ssm=ss.n):
                s.wait_ge(sv.sem, a)
                st.inc(s.dma_start_transpose(out=xnT[:, :, 128 * t:128 * (t + 1)],
                                             in_=xnb_t[:, r2, :]), 16)
                s.wait_ge(sv.sem, b)
                st.inc(s.dma_start(out=xd_sp[128 * t:128 * (t + 1), :],
                                   in_=xd_t[:, r2, :]), 16)
                if xn_dbg is not None:
                    st.inc(s.dma_start(out=xn_dbg[128 * t:128 * (t + 1), :],
                                       in_=xnb_t[:, r2, :]), 16)
                if t + 2 < NT:
                    s.wait_ge(ss.sem, ssm)
                    sx.inc(s.dma_start(out=x_t[:, t % 2, :],
                                       in_=x_h[128 * (t + 2):128 * (t + 3), :]), 16)
            xn_done.append(st.n - (32 if xn_dbg is not None else 16))
            emit_vgemm(t)

        emit_vcopy(NT - 1)
        sv_wq = []
        for wt in range(NC):
            @blk.gpsimd
            def _(g, wt=wt):
                if wt >= 1:
                    g.wait_ge(sv.sem, sv_wq[wt - 1])
                sw.inc(g.dma_start(out=wffn_t[:, 0, :],
                                   in_=wffn_h[128 * wt:128 * (wt + 1), :]), 16)

            @blk.vector
            def _(v, wt=wt, mark=sw.n):
                v.wait_ge(sw.sem, mark)
                if wt >= 2:
                    v.wait_ge(st.sem, st_wq[wt - 2])
                sv.inc(_quant(v, wffn_t[:, 0, :], wd_t[:, wt % 2, :]))
            sv_wq.append(sv.n)

            @blk.sync
            def _(s, wt=wt, mark=sv.n):
                s.wait_ge(sv.sem, mark)
                st.inc(s.dma_start(out=wd_sp[128 * wt:128 * (wt + 1), :],
                                   in_=wd_t[:, wt % 2, :]), 16)
            st_wq.append(st.n)
        st_p1_done = st.n
        if phase_limit <= 1:
            @blk.gpsimd
            def _(g, a=st.n, b=sv.n, c=ss.n):
                g.wait_ge(st.sem, a)
                g.wait_ge(sv.sem, b)
                g.wait_ge(ss.sem, c)
                sa.inc(g.dma_start(out=out_h[0:128, :], in_=x_t[:, 0, :]), 16)
                g.wait_ge(sa.sem, 16)
            return nc

        import os as _os
        st_v_last = st_v_l[0]

        if not int(_os.environ.get("KNOCLOSE", "0")):
            ctx1.close()
        ctx3 = contextlib.ExitStack()
        ctx.enter_context(ctx3)
        E3 = ctx3.enter_context
        qscl = E3(nc.sbuf_tensor("qscl", [128, 2048], BF16))

        @blk.gpsimd
        def _(g, mark=st_p1_done):
            g.wait_ge(st.sem, mark)
            sw.inc(g.dma_start(out=qscl[:], in_=cst["qscl"]), 16)
        sw_marks["qscl"] = sw.n

        qT_p = E3(nc.sbuf_tensor("qT_p", [128, T], BF16))
        kT_p = E3(nc.sbuf_tensor("kT_p", [128, T], BF16))
        qP = E3(nc.sbuf_tensor("qP", [128, 2048], BF16))
        kP = E3(nc.sbuf_tensor("kP", [128, 2048], BF16))
        vP = E3(nc.sbuf_tensor("vP", [128, 2048], BF16))
        vPn = E3(nc.sbuf_tensor("vPn", [128, 16, 128], BF16))
        am = E3(nc.sbuf_tensor("am", [128, 2, 512], BF16))
        smS = E3(nc.sbuf_tensor("smS", [128, 512], F32))
        mbuf = E3(nc.sbuf_tensor("mbuf", [128, 2, 24], F32))
        embuf = E3(nc.sbuf_tensor("embuf", [128, 2, 24], F32))
        y0 = E3(nc.sbuf_tensor("y0", [128, 2, 16, 65], BF16))
        stg = E3(nc.sbuf_tensor("stg", [128, 2, 8, 65], BF16))
        avb = E3(nc.sbuf_tensor("avb", [128, 16], F32))
        rinv = E3(nc.sbuf_tensor("rinv", [128, 16], F32))
        attn_hd = E3(nc.sbuf_tensor("attn_hd", [128, 16, 64], BF16))
        em2 = E3(nc.sbuf_tensor("em2", [128, 160], F32)) if _dbg else None

        if phase_limit <= 2:
            @blk.gpsimd
            def _(g, a=st.n, b=sv.n, c=ss.n, d=sp.n):
                g.wait_ge(st.sem, a)
                g.wait_ge(sv.sem, b)
                g.wait_ge(ss.sem, c)
                g.wait_ge(sp.sem, d)
                sa.inc(g.dma_start(out=out_h[0:128, :], in_=x_t[:, 0, :]), 16)
                g.wait_ge(sa.sem, 16)
            return nc

        GRAM = (
            [("L0", 128 * t, 128 * t, 1.0) for t in range(16)]
            + [("L1", _OOFF[1] + 128 * a, _EOFF[1] + 128 * a, 0.25) for a in range(4)]
            + [("L2", _OOFF[2] + 128 * a, _EOFF[2] + 128 * a, 0.0625) for a in range(2)]
            + [("L3", _OOFF[3], _EOFF[3], 4.0 ** -3)]
            + [("CB", None, None, None)]
        )
        CB_SUBS = [(4, 64, 0), (5, 48, 64)]

        def _emit_grams(p, b, hh, orient):
            ps = ps_a if orient == "S" else ps_b
            bank = b % 2
            pr0 = 64 * hh
            first = [True]
            lastmm = [None]

            def mm(out, lt, rh, last):
                lastmm[0] = p.matmul(out, lt, rh, start=first[0], stop=last,
                                     skip_group_check=True)
                first[0] = False

            for bi in range(4):
                ai = 4 * b + bi
                kind, qo, ko, _sc = GRAM[ai]
                lastb = bi == 3
                if kind != "CB":
                    sq = qT_p if ai < 16 else qP
                    sk = kT_p if ai < 16 else kP
                    a_, b_ = (qo, ko) if orient == "S" else (ko, qo)
                    s1, s2 = (sq, sk) if orient == "S" else (sk, sq)
                    mm(ps[:, bank, 128 * bi:128 * (bi + 1)],
                       s1[pr0:pr0 + 64, a_:a_ + 128], s2[pr0:pr0 + 64, b_:b_ + 128],
                       lastb)
                else:
                    mm(ps[:, bank, 128 * bi:128 * (bi + 1)], zeros2[:], zeros2[:],
                       False)
                    for si, (l, w, ro) in enumerate(CB_SUBS):
                        qo2, ko2 = _OOFF[l], _EOFF[l]
                        a_, b_ = (qo2, ko2) if orient == "S" else (ko2, qo2)
                        s1, s2 = (qP, kP) if orient == "S" else (kP, qP)
                        lastmm[0] = p.matmul(
                            ps[ro:ro + w, bank, 128 * bi + ro:128 * bi + ro + w],
                            s1[pr0:pr0 + 64, a_:a_ + w],
                            s2[pr0:pr0 + 64, b_:b_ + w],
                            start=False, stop=(lastb and si == 1),
                            skip_group_check=True)
            return lastmm[0]

        abank_free = {0: 0, 1: 0}
        bbank_free = {0: 0, 1: 0}
        ybank_free = {0: 0, 1: 0}
        am_free = {0: 0, 1: 0}
        sv_attn_all = 0

        @blk.tensor
        def _(p):
            p.wait_ge(sw.sem, sw_marks["all"])
            for bk in range(2):
                p.matmul(ps_y[:, bk, 0:512], zeros2[:], brow_b[:, 0:512],
                         start=True, stop=True, skip_group_check=True)

        N2B = NPAIR
        ss_qk_d, st_vpn_d = {}, {}
        sv_poolq_d, sv_poolk_d = {}, {}

        def emit_qk_group(pair, gi):
            feat0 = (128 * pair) if gi < 4 else (D + 128 * pair)
            n4 = gi % 4
            bank = gi % 2
            dstbuf, qscale = (qT_p, 0.125) if gi < 4 else (kT_p, None)

            @blk.tensor
            def _(p, pair=pair, feat0=feat0, n4=n4, bank=bank, gi=gi):
                if pair == 0 and gi == 0:
                    p.wait_ge(st.sem, st_p1_done)
                p.wait_ge(ss.sem, gbank_free[bank])
                for c in range(NC):
                    mm = p.matmul(ps_g[:, bank, :],
                             wqkvT[:, c, feat0:feat0 + 128],
                             xnT[:, c, 512 * n4:512 * (n4 + 1)],
                             start=(c == 0), stop=(c == NC - 1))
                sp.inc(mm)

            @blk.vector
            def _(v, dstbuf=dstbuf, qscale=qscale, n4=n4, bank=bank,
                  mark=sp.n):
                v.wait_ge(sp.sem, mark)
                ss.inc(v.tensor_scalar(
                    out=dstbuf[:, 512 * n4:512 * (n4 + 1)],
                    in0=ps_g[:, bank, :],
                    scalar1=(1.0 if qscale is None else qscale),
                    scalar2=None, op0=ALU.mult))
            gbank_free[bank] = ss.n
            if gi == 7:
                ss_qk_d[pair] = ss.n

        def _pool_one(v, srcbuf, dstbuf, pair):
            src0 = vT[:, pair, :] if srcbuf is None else srcbuf[:]
            rv = src0.rearrange("p (bp r) -> p bp r", r=64)
            v.tensor_tensor(
                out=dstbuf[:, _EOFF[1]:_EOFF[1] + 512].rearrange(
                    "p (a w) -> p a w", w=16),
                in0=rv[:, :, 0:31:2], in1=rv[:, :, 1:32:2], op=ALU.add)
            v.tensor_tensor(
                out=dstbuf[:, _OOFF[1]:_OOFF[1] + 512].rearrange(
                    "p (a w) -> p a w", w=16),
                in0=rv[:, :, 32:63:2], in1=rv[:, :, 33:64:2], op=ALU.add)
            for l in range(2, 7):
                P = (2048 >> l) // 32
                for (dof, sof) in [(_EOFF[l], _EOFF[l - 1]),
                                   (_EOFF[l] + 8, _OOFF[l - 1]),
                                   (_OOFF[l], _EOFF[l - 1] + 16),
                                   (_OOFF[l] + 8, _OOFF[l - 1] + 16)]:
                    o = dstbuf[:, dof:dof + 16 * P].rearrange(
                        "p (g u) -> p g u", u=16)[:, :, 0:8]
                    i0 = dstbuf[:, sof:sof + 32 * P].rearrange(
                        "p (g u) -> p g u", u=32)[:, :, 0:15:2]
                    i1 = dstbuf[:, sof + 1:sof + 1 + 32 * P].rearrange(
                        "p (g u) -> p g u", u=32)[:, :, 0:15:2]
                    last = v.tensor_tensor(out=o, in0=i0, in1=i1, op=ALU.add)
            return last

        def emit_pool(pair):
            @blk.vector
            def _(v, pair=pair, m=ss_qk_d[pair], stv=st_v_last):
                v.wait_ge(ss.sem, m)
                if pair == 0:
                    v.wait_ge(st.sem, stv)
                sv.inc(_pool_one(v, None, vP, pair))

            @blk.sync
            def _(s, mark=sv.n):
                s.wait_ge(sv.sem, mark)
                st.inc(s.dma_start_transpose(out=vPn[:], in_=vP[:]), 16)
            st_vpn_d[pair] = st.n

            @blk.vector
            def _(v, pair=pair):
                if pair == 0:
                    v.wait_ge(sw.sem, sw_marks["qscl"])
                _pool_one(v, qT_p, qP, pair)
                sv.inc(v.tensor_tensor(out=qP[:], in0=qP[:], in1=qscl[:],
                                       op=ALU.mult))
            sv_poolq_d[pair] = sv.n

            @blk.vector
            def _(v, pair=pair):
                sv.inc(_pool_one(v, kT_p, kP, pair))
            sv_poolk_d[pair] = sv.n

        def emit_S(b, hh, pair):
            bank = b % 2

            @blk.tensor
            def _(p, b=b, hh=hh, bank=bank, pair=pair):
                if b == 0 and hh == 0:
                    p.wait_ge(ss.sem, ss_qk_d[pair])
                if b == 4 and hh == 0:
                    p.wait_ge(sv.sem, sv_poolq_d[pair])
                    p.wait_ge(sv.sem, sv_poolk_d[pair])
                p.wait_ge(sv.sem, abank_free[bank])
                sp.inc(_emit_grams(p, b, hh, "S"))

            @blk.vector
            def _(v, b=b, hh=hh, bank=bank, mark=sp.n):
                v.wait_ge(sp.sem, mark)
                pv = ps_a[:, bank, :]
                mi = 0 if b < 4 else 1
                v.scalar_tensor_tensor(
                    out=smS[:].rearrange("p (n f) -> p n f", f=128),
                    in0=pv.rearrange("p (n f) -> p n f", f=128),
                    scalar=-1.0, op0=ALU.mult, op1=ALU.add,
                    in1=msk_s[:, mi:mi + 1, :].broadcast_to((128, 4, 128)))
                sv.inc(v.tensor_reduce(
                    out=mbuf[:, hh, 4 * b:4 * (b + 1)],
                    in_=smS[:].rearrange("p (n f) -> p n f", f=128),
                    axis=AX, op=ALU.min))
            abank_free[bank] = sv.n

        def emit_embuf(hh):
            @blk.scalar
            def _(s, hh=hh, sv_max=sv.n):
                s.wait_ge(sv.sem, sv_max)
                ss.inc(s.activation(out=embuf[:, hh, :], in_=mbuf[:, hh, :],
                                    func=ACT.Exp, bias=0.0, scale=1.0))
            return ss.n

        sv_am_l = {}

        def emit_st(b, hh):
            bank = b % 2

            @blk.tensor
            def _(p, b=b, hh=hh, bank=bank):
                p.wait_ge(ss.sem, bbank_free[bank])
                sp.inc(_emit_grams(p, b, hh, "T"))

            @blk.scalar
            def _(s, b=b, bank=bank, mark=sp.n, amf=am_free[bank]):
                s.wait_ge(sp.sem, mark)
                s.wait_ge(sp.sem, amf)
                pb = ps_b[:, bank, :]
                ab = am[:, bank, :]
                ss.inc(s.activation(out=ab, in_=pb, func=ACT.Exp, bias=0.0,
                                    scale=1.0))
            bbank_free[bank] = ss.n

            @blk.vector
            def _(v, b=b, bank=bank, mark=ss.n):
                v.wait_ge(ss.sem, mark)
                ab3 = am[:, bank, :].rearrange("p (n f) -> p n f", f=128)
                if b < 4:
                    mi = v.tensor_tensor(out=ab3, in0=ab3,
                                    in1=msk_t[:, 0:1, :].broadcast_to(
                                        (128, 4, 128)), op=ALU.mult)
                elif b == 4:
                    mi = v.tensor_tensor(out=ab3, in0=ab3,
                                    in1=msk_t[:, 1:2, :].broadcast_to(
                                        (128, 4, 128)), op=ALU.mult)
                else:
                    v.tensor_tensor(out=ab3[:, 0:3, :], in0=ab3[:, 0:3, :],
                                    in1=msk_t[:, 1:2, :].broadcast_to(
                                        (128, 3, 128)), op=ALU.mult)
                    mi = v.tensor_tensor(out=ab3[:, 3, :], in0=ab3[:, 3, :],
                                         in1=msk_t[:, 2, :], op=ALU.mult)
                sv.inc(mi)
            sv_am_l[b] = sv.n

        def emit_pv(b, hh, pair):
            bank = b % 2
            sv_am = sv_am_l[b]

            @blk.tensor
            def _(p, pair=pair, hh=hh, b=b, bank=bank, sv_am=sv_am):
                p.wait_ge(sv.sem, sv_am)
                p.wait_ge(sv.sem, ybank_free[bank])
                if b == 4 and hh == 0:
                    p.wait_ge(st.sem, st_vpn_d[pair])
                h = 2 * pair + hh
                lastpv = None
                for bi in range(4):
                    ai = 4 * b + bi
                    kind = GRAM[ai][0]
                    amat = am[:, bank, 128 * bi:128 * (bi + 1)]
                    yc = 66 * bi
                    if kind == "L0":
                        rhs = v_sb[:, ai, 64 * h:64 * h + 64]
                        lastpv = p.matmul(ps_y[:, bank, yc:yc + 64], amat, rhs,
                                 start=True, stop=True,
                                 skip_group_check=True)
                        lastpv = p.matmul(ps_y[:, bank, yc + 64:yc + 65], amat,
                                 onescol[:], start=True,
                                 stop=True, skip_group_check=True)
                    elif kind != "CB":
                        mt = {16: 0, 17: 1, 18: 2, 19: 3,
                              20: 8, 21: 9, 22: 12}[ai]
                        rhs = vPn[:, mt, 64 * hh:64 * hh + 64]
                        lastpv = p.matmul(ps_y[:, bank, yc:yc + 64], amat, rhs,
                                 start=True, stop=True,
                                 skip_group_check=True)
                        lastpv = p.matmul(ps_y[:, bank, yc + 64:yc + 65], amat,
                                 onescol[:], start=True,
                                 stop=True, skip_group_check=True)
                    else:
                        pieces = [(0, 64, 14), (64, 48, 15)]
                        for (r0, w, mt) in pieces:
                            asl = am[r0:r0 + w, bank,
                                     128 * bi + r0:128 * bi + r0 + w]
                            p.matmul(ps_y[r0:r0 + w, bank, yc:yc + 64],
                                     asl,
                                     vPn[r0:r0 + w, mt,
                                         64 * hh:64 * hh + 64],
                                     start=True, stop=True,
                                     skip_group_check=True)
                            lastpv = p.matmul(
                                ps_y[r0:r0 + w, bank, yc + 64:yc + 65],
                                asl, onescol[r0:r0 + w, :], start=True,
                                stop=True, skip_group_check=True)
                sp.inc(lastpv)
            am_free[bank] = sp.n

        def emit_scale(b, hh, pair, ss_em):
            bank = b % 2

            @blk.vector
            def _(v, pair=pair, hh=hh, b=b, bank=bank, mark=sp.n,
                  ss_em=ss_em):
                v.wait_ge(sp.sem, mark)
                v.wait_ge(ss.sem, ss_em)
                src = ps_y[:, bank, 0:264].rearrange("p (n f) -> p n f", f=66)[:, :, 0:65]
                if b < 4:
                    dst = y0[:, hh, 4 * b:4 * (b + 1), :]
                elif b == 4:
                    dst = stg[:, hh, 0:4, :]
                else:
                    dst = stg[:, hh, 4:8, :]
                sv.inc(v.scalar_tensor_tensor(
                    out=dst, in0=src, scalar=1.0, op0=ALU.mult,
                    op1=ALU.mult,
                    in1=embuf[:, hh, 4 * b:4 * (b + 1)].unsqueeze(2)
                        .broadcast_to((128, 4, 65))))
            ybank_free[bank] = sv.n

        PIDX = {1: 0, 2: 4, 3: 12, 4: 20, "C56": 28}
        GWAVES = [list(range(0, 7)), list(range(7, 14)), [14, 15]]

        def emit_gather(wi, hh, pair, sv_stage):
            wave = GWAVES[wi]
            bank = wi % 2

            @blk.tensor
            def _(p, hh=hh, wave=wave, bank=bank, sv_stage=sv_stage):
                p.wait_ge(sv.sem, sv_stage)
                p.wait_ge(sv.sem, ybank_free[bank])
                pd = ps_y[:, bank, 0:462].rearrange("p (n f) -> p n f", f=66)
                mms = []
                for ti, tt in enumerate(wave):
                    mms.append((pd[:, ti, 0:65], ident[:], y0[:, hh, tt, :]))
                for l in range(1, 5):
                    for ti, tt in enumerate(wave):
                        if l == 1:
                            var, acol = tt % 4, tt // 4
                        elif l == 2:
                            var, acol = tt % 8, 4 + tt // 8
                        elif l == 3:
                            if tt % 2 == 0:
                                continue
                            var, acol = (tt - 1) // 2, 6
                        else:
                            if tt % 4 < 2:
                                continue
                            var = 2 * (tt // 4) + (tt % 4 - 2)
                            acol = 7
                        mms.append((pd[:, ti, 0:65],
                                    perm[:, PIDX[l] + var, :],
                                    stg[:, hh, acol, :]))
                for ti, tt in enumerate(wave):
                    if tt < 4:
                        continue
                    mms.append((pd[:, ti, 0:65],
                                perm[:, PIDX["C56"] + (tt - 4), :],
                                stg[:, hh, 7, :]))
                for mi, (o_, lt, rh) in enumerate(mms):
                    gi = p.matmul(o_, lt, rh, start=(mi == 0),
                                  stop=(mi == len(mms) - 1),
                                  skip_group_check=True)
                sp.inc(gi)

            @blk.vector
            def _(v, pair=pair, hh=hh, wave=wave, bank=bank, mark=sp.n):
                v.wait_ge(sp.sem, mark)
                nw = len(wave)
                t0 = wave[0]
                pd = ps_y[:, bank, 0:462].rearrange("p (n f) -> p n f", f=66)
                v.tensor_scalar(out=avb[:, t0:t0 + nw],
                                in0=pd[:, 0:nw, 64], scalar1=1e-8,
                                scalar2=None, op0=ALU.add)
                v.drain()
                v.reciprocal(out=rinv[:, t0:t0 + nw], in_=avb[:, t0:t0 + nw])
                v.drain()
                if wave[0] == 0 and (pair + hh) > 0:
                    v.wait_ge(sa.sem, 16 * (2 * pair + hh))
                sv.inc(v.scalar_tensor_tensor(
                    out=attn_hd[:, t0:t0 + nw, :],
                    in0=pd[:, 0:nw, 0:64], scalar=1.0,
                    op0=ALU.mult, op1=ALU.mult,
                    in1=rinv[:, t0:t0 + nw].unsqueeze(2)
                        .broadcast_to((128, nw, 64))))
            ybank_free[bank] = sv.n

        def emit_attn_dma(pair, hh):
            @blk.gpsimd
            def _(g, pair=pair, hh=hh, mark=sv.n):
                g.wait_ge(sv.sem, mark)
                h = 2 * pair + hh
                sa.inc(g.dma_start(
                    out=attn_sp.rearrange("(t p) d -> p t d", p=128)[:, :, 64 * h:64 * h + 64],
                    in_=attn_hd[:]), 16)

        for pair in range(N2B):
            if pair == 0:
                for gi in range(8):
                    emit_qk_group(0, gi)
                emit_pool(0)
            for b in range(6):
                emit_S(b, 0, pair)
            ss_em0 = emit_embuf(0)
            for k in range(7):
                if k < 6:
                    emit_st(k, 0)
                if 1 <= k <= 4:
                    emit_S(k - 1, 1, pair)
                if k >= 1:
                    emit_pv(k - 1, 0, pair)
                    emit_scale(k - 1, 0, pair, ss_em0)
            sv_stage = sv.n
            emit_gather(0, 0, pair, sv_stage)
            emit_S(4, 1, pair)
            emit_gather(1, 0, pair, sv_stage)
            emit_S(5, 1, pair)
            emit_gather(2, 0, pair, sv_stage)
            emit_attn_dma(pair, 0)
            ss_em1 = emit_embuf(1)
            nxt = pair + 1 if pair + 1 < N2B else None
            for k in range(7):
                if k < 6:
                    emit_st(k, 1)
                if k >= 1:
                    emit_pv(k - 1, 1, pair)
                    emit_scale(k - 1, 1, pair, ss_em1)
                if nxt is not None and 4 <= k <= 6:
                    emit_qk_group(nxt, k - 4)
            sv_stage = sv.n
            emit_gather(0, 1, pair, sv_stage)
            if nxt is not None:
                emit_qk_group(nxt, 3)
                emit_qk_group(nxt, 4)
            emit_gather(1, 1, pair, sv_stage)
            if nxt is not None:
                emit_qk_group(nxt, 5)
                emit_qk_group(nxt, 6)
            emit_gather(2, 1, pair, sv_stage)
            emit_attn_dma(pair, 1)
            if nxt is not None:
                emit_qk_group(nxt, 7)
                emit_pool(nxt)

        if xn_dbg is not None:
            @blk.gpsimd
            def _(g, mark=sv.n):
                g.wait_ge(sv.sem, mark)
                sa.inc(g.dma_start(out=qp_dbg, in_=qP[:]), 16)
                sa.inc(g.dma_start(out=kp_dbg, in_=kP[:]), 16)
                sa.inc(g.dma_start(out=vp_dbg, in_=vP[:]), 16)
                sa.inc(g.dma_start(out=stg_dbg, in_=stg[:].rearrange("p a b c -> p (a b c)")), 16)
                sa.inc(g.dma_start(out=y0_dbg, in_=y0[:].rearrange("p a b c -> p (a b c)")), 16)
                sa.inc(g.dma_start(out=em_dbg, in_=embuf[:].rearrange("p a b -> p (a b)")), 16)
                sa.inc(g.dma_start(out=em2_dbg, in_=em2[:]), 16)

        sv_attn_all = sv.n
        if phase_limit <= 3:
            @blk.gpsimd
            def _(g, a=st.n, b=sv.n, c=ss.n, d=sp.n, e=sa.n):
                g.wait_ge(st.sem, a)
                g.wait_ge(sv.sem, b)
                g.wait_ge(ss.sem, c)
                g.wait_ge(sp.sem, d)
                g.wait_ge(sa.sem, e)
                sa.inc(g.dma_start(out=out_h[0:128, 0:24], in_=mbuf[:].rearrange("p a b -> p (a b)")[:, 0:24]), 16)
                g.wait_ge(sa.sem, e + 16)
            return nc

        ctx3.close()
        ctx2.close()
        ctx4 = contextlib.ExitStack()
        ctx.enter_context(ctx4)
        E4 = ctx4.enter_context
        xdT = E4(nc.sbuf_tensor("xdT", [128, NC, T], BF16))
        attnT = E4(nc.sbuf_tensor("attnT", [128, NC, T], BF16))
        wdT = E4(nc.sbuf_tensor("wdT", [128, NC, D], BF16))
        woutT = E4(nc.sbuf_tensor("woutT_s", [128, NC, D], BF16))
        x_r = E4(nc.sbuf_tensor("x_r", [128, 2, D], F32))
        out_sb = E4(nc.sbuf_tensor("out_sb", [128, 2, D], F32))

        obank_free = {0: 0, 1: 0}
        xr_free = {0: 0, 1: 0}
        ob_free = {0: 0, 1: 0}
        @blk.gpsimd
        def _(g, mark=sv_attn_all):
            g.wait_ge(sv.sem, mark)
            for c_ in range(NC):
                sw.inc(g.dma_start(out=woutT[:, c_, :],
                                   in_=woutT_h[128 * c_:128 * (c_ + 1), :]), 16)
        sw_marks["woutT"] = sw.n

        @blk.sync
        def _(s, san=sa.n):
            s.wait_ge(sa.sem, san)
            for wt in range(NC):
                st.inc(s.dma_start_transpose(out=wdT[:, :, 128 * wt:128 * (wt + 1)],
                                             in_=wd_sp[128 * wt:128 * (wt + 1), :]), 16)
        st_wd_done = st.n

        at_mark = []
        for t in range(NT):
            @blk.sync
            def _(s, t=t):
                st.inc(s.dma_start_transpose(out=attnT[:, :, 128 * t:128 * (t + 1)],
                                             in_=attn_sp[128 * t:128 * (t + 1), :]), 16)
                st.inc(s.dma_start_transpose(out=xdT[:, :, 128 * t:128 * (t + 1)],
                                             in_=xd_sp[128 * t:128 * (t + 1), :]), 16)
            at_mark.append((st.n, 0))

        for t in range(NT):
            @blk.sync
            def _(s, t=t):
                if t >= 2:
                    s.wait_ge(sv.sem, xr_free[t % 2])
                sx.inc(s.dma_start(out=x_r[:, t % 2, :],
                                   in_=x_h[128 * t:128 * (t + 1), :]), 16)
            at_mark[t] = (at_mark[t][0], sx.n)

            for n2 in range(2):
                bank = (2 * t + n2) % 2

                @blk.tensor
                def _(p, t=t, n2=n2, bank=bank):
                    p.wait_ge(st.sem, at_mark[t][0])
                    if t == 0 and n2 == 0:
                        p.wait_ge(st.sem, st_wd_done)
                        p.wait_ge(sw.sem, sw_marks["woutT"])
                        p.wait_ge(ss.sem, gbank_free[0])
                        p.wait_ge(ss.sem, gbank_free[1])
                    p.wait_ge(sv.sem, obank_free[bank])
                    p.matmul(ps_g[:, bank, :], ones2[:],
                             brow_b[:, 512 * n2:512 * (n2 + 1)],
                             start=True, stop=False)
                    for c in range(NC):
                        p.matmul(ps_g[:, bank, :], attnT[:, c, 128 * t:128 * (t + 1)],
                                 woutT[:, c, 512 * n2:512 * (n2 + 1)],
                                 start=False, stop=False)
                    for c in range(NC):
                        mm = p.matmul(ps_g[:, bank, :], xdT[:, c, 128 * t:128 * (t + 1)],
                                 wdT[:, c, 512 * n2:512 * (n2 + 1)],
                                 start=False, stop=(c == NC - 1))
                    sp.inc(mm)

                @blk.vector
                def _(v, t=t, n2=n2, bank=bank, mark=sp.n):
                    v.wait_ge(sp.sem, mark)
                    v.wait_ge(sx.sem, at_mark[t][1])
                    if t >= 2 and n2 == 0:
                        v.wait_ge(so.sem, ob_free[t % 2])
                    sv.inc(v.tensor_tensor(
                        out=out_sb[:, t % 2, 512 * n2:512 * (n2 + 1)],
                        in0=x_r[:, t % 2, 512 * n2:512 * (n2 + 1)],
                        in1=ps_g[:, bank, :], op=ALU.add))
                obank_free[bank] = sv.n
            xr_free[t % 2] = sv.n

            @blk.gpsimd
            def _(g, t=t, mark=sv.n):
                g.wait_ge(sv.sem, mark)
                so.inc(g.dma_start(out=out_h[128 * t:128 * (t + 1), :],
                                   in_=out_sb[:, t % 2, :]), 16)
            ob_free[t % 2] = so.n

        @blk.gpsimd
        def _(g):
            g.wait_ge(so.sem, so.n)

    return nc


_NC_CACHE = {}


def _get_nc():
    import os
    pl = int(os.environ.get("KPHASE", "99"))
    key = ("nc", pl)
    if key not in _NC_CACHE:
        _NC_CACHE[key] = build_nc(pl)
    return _NC_CACHE[key]


def kernel(L0_attn_x, L0_attn_w_qkv, L0_attn_w_out, L0_attn_b_out,
           L0_ffn_weight, L0_ffn_bias):
    x = np.ascontiguousarray(np.asarray(L0_attn_x, dtype=np.float32))
    wqkvT = np.ascontiguousarray(np.asarray(L0_attn_w_qkv, dtype=np.float32).T)
    woutT = np.ascontiguousarray(np.asarray(L0_attn_w_out, dtype=np.float32).T)
    wffn = np.ascontiguousarray(np.asarray(L0_ffn_weight, dtype=np.float32))
    brow = np.ascontiguousarray(
        np.stack([np.asarray(L0_attn_b_out, dtype=np.float32),
                  np.asarray(L0_ffn_bias, dtype=np.float32)], axis=0))
    hc = _host_consts()
    base = {"wqkvT": wqkvT, "woutT": woutT, "wffn": wffn, "brow": brow}
    for k, v in hc.items():
        base["c_" + k] = np.ascontiguousarray(v)
    in_maps = []
    for c in range(8):
        m = dict(base)
        m["x"] = np.ascontiguousarray(x[c])
        in_maps.append(m)
    nc = _get_nc()
    res = run_bass_kernel_spmd(nc, in_maps, core_ids=list(range(8)))
    out = np.stack([np.asarray(r["out"]) for r in res.results], axis=0)
    return out.astype(np.float32)
